# version 16
# baseline (speedup 1.0000x reference)
import contextlib
import math
import os
os.environ.setdefault("TRN_ALLOW_ICI_ALL_TO_ALL", "1")
import numpy as np
import ml_dtypes
import concourse.bass as bass
import concourse.mybir as mybir
from concourse.bass_utils import run_bass_kernel_spmd

F32 = mybir.dt.float32
BF16 = mybir.dt.bfloat16
I32 = mybir.dt.int32
AF = mybir.ActivationFunctionType
ALU = mybir.AluOpType
f32 = np.float32

NCORES = 8
D = 4096
TOK = 8192
TPC = TOK // NCORES
SEQ = 4096

NF4 = [-1.0, -0.6961928009986877, -0.5250730514526367, -0.39491748809814453,
       -0.28444138169288635, -0.18477343022823334, -0.09105003625154495, 0.0,
       0.07958029955625534, 0.16093020141124725, 0.24611230194568634,
       0.33791524171829224, 0.44070982933044434, 0.5626170039176941,
       0.7229568362236023, 1.0]

M0 = 75.0
M1 = 12.0
LN_EPS = 1e-5


class Ctx:

    def __init__(self, nc, es):
        self.nc = nc
        self.es = es
        self.hw = []
        self.hwcnt = []
        self.free = []
        self.bind = {}
        self.active = set()
        self.persistent = set(["init", "bar"])
        self.nbar = 0

    def _hw_new(self):
        h = self.es.enter_context(self.nc.semaphore(f"s{len(self.hw)}"))
        self.hw.append(h)
        self.hwcnt.append(0)
        return len(self.hw) - 1

    def _bind(self, name):
        if name not in self.bind:
            idx = self.free.pop() if self.free else self._hw_new()
            self.bind[name] = (idx, self.hwcnt[idx])
            self.active.add(name)
        return self.bind[name]

    def sem(self, name):
        return self.hw[self._bind(name)[0]]

    def inc(self, name, inst, by=1):
        idx, base = self._bind(name)
        inst.then_inc(self.hw[idx], by)
        self.hwcnt[idx] += by
        return inst

    def wait(self, eng, name, value):
        if value > 0:
            idx, base = self._bind(name)
            eng.wait_ge(self.hw[idx], base + value)

    def retire_phase(self):
        for name in list(self.active):
            if name in self.persistent or name.startswith("cc"):
                continue
            self.active.discard(name)
            idx, _ = self.bind.pop(name)
            self.free.append(idx)


def emit_barrier(ctx, block, exclude=()):
    ctx.nbar += 1
    bidx, bbase = ctx._bind("bar")
    k = ctx.hwcnt[bidx] + 1
    excl_idx = set()
    for name in exclude:
        if name in ctx.bind:
            excl_idx.add(ctx.bind[name][0])
    excl_idx.add(bidx)

    def _s(s):
        for idx in range(len(ctx.hw)):
            if idx in excl_idx:
                continue
            if ctx.hwcnt[idx] > 0:
                s.wait_ge(ctx.hw[idx], ctx.hwcnt[idx])
        s.sem_inc(ctx.hw[bidx], 1)
    block.sync(_s)
    ctx.hwcnt[bidx] = k
    for emit in (block.scalar, block.tensor, block.vector, block.gpsimd):
        def _w(e):
            e.wait_ge(ctx.hw[bidx], k)
        emit(_w)
    ctx.retire_phase()


def alloc_init_bufs(nc, es, with_masks):
    bufs = {}
    bufs["ones_col_f32"] = es.enter_context(nc.sbuf_tensor("ones_col_f32", [128, 1], F32))
    bufs["ones_row_f32"] = es.enter_context(nc.sbuf_tensor("ones_row_f32", [1, 128], F32))
    bufs["ones_row_bf"] = es.enter_context(nc.sbuf_tensor("ones_row_bf", [1, 512], BF16))
    bufs["ones_k_bf"] = es.enter_context(nc.sbuf_tensor("ones_k_bf", [128, 128], BF16))
    bufs["ones_k_f32"] = es.enter_context(nc.sbuf_tensor("ones_k_f32", [128, 128], F32))
    bufs["ident"] = es.enter_context(nc.sbuf_tensor("ident", [128, 128], BF16))
    bufs["negident"] = es.enter_context(nc.sbuf_tensor("negident", [128, 128], BF16))
    if with_masks:
        bufs["masks"] = [es.enter_context(nc.sbuf_tensor(f"mask{p}", [128, 512], BF16))
                         for p in range(4)]
    return bufs


def emit_init(ctx, block, bufs, with_masks, dyn_src=None):
    if dyn_src is not None:
        def _s(s):
            ctx.pid = s.partition_id()
            ctx.pid_off = ctx.pid * TPC
            ctx.inc("init0", s.dma_start(out=bufs["ident"][0:1, 0:16],
                                         in_=dyn_src[0:1, bass.ts(ctx.pid, 16)]), 16)
        block.sync(_s)

    def _(g):
        if dyn_src is not None:
            ctx.wait(g, "init0", 16)
        g.memset(bufs["ones_col_f32"][:, :], 1.0)
        g.memset(bufs["ones_row_f32"][:, :], 1.0)
        g.memset(bufs["ones_row_bf"][:, :], 1.0)
        g.memset(bufs["ones_k_bf"][:, :], 1.0)
        g.memset(bufs["ones_k_f32"][:, :], 1.0)
        g.memset(bufs["ident"][:, :], 0.0)
        g.memset(bufs["negident"][:, :], 0.0)
        g.drain()
        g.affine_select(out=bufs["ident"][:, :], in_=bufs["ident"][:, :],
                        compare_op=ALU.not_equal, fill=1.0, base=0,
                        pattern=[[-1, 128]], channel_multiplier=1)
        g.affine_select(out=bufs["negident"][:, :], in_=bufs["negident"][:, :],
                        compare_op=ALU.not_equal, fill=-1.0, base=0,
                        pattern=[[-1, 128]], channel_multiplier=1)
        if with_masks:
            for p in range(4):
                g.memset(bufs["masks"][p][:, :], 1.0)
            g.drain()
            for p in range(4):
                g.affine_select(out=bufs["masks"][p][:, :], in_=bufs["masks"][p][:, :],
                                compare_op=ALU.is_ge, fill=0.0, base=-128 * p,
                                pattern=[[1, 512]], channel_multiplier=-1)
        g.drain()
        ctx.inc("init", g.nop(), 1)
    block.gpsimd(_)

    def _sc(sc):
        ctx.wait(sc, "init", 1)
        sc.activation(bufs["ones_col_f32"][0:1, 0:1], bufs["ones_col_f32"][0:1, 0:1],
                      AF.Square)
        sc.drain()
    block.scalar(_sc)


class PrepV:
    def __init__(self, ctx, nc, es, dr, rg, stsem):
        self.ctx, self.nc, self.dr, self.rg, self.stsem = ctx, nc, dr, rg, stsem
        self.qwb = [es.enter_context(nc.sbuf_tensor(f"pv_qw{i}", [128, 4096], I32))
                    for i in range(2)]
        self.qf = es.enter_context(nc.sbuf_tensor("pv_qf", [128, 4096], F32))
        self.acc = es.enter_context(nc.sbuf_tensor("pv_acc", [128, 4096], F32))
        self.tmp = es.enter_context(nc.sbuf_tensor("pv_tmp", [128, 4096], F32))
        self.amb = es.enter_context(nc.sbuf_tensor("pv_amb", [128, 4096], F32))
        self.wout = es.enter_context(nc.sbuf_tensor("pv_w", [128, 4096], BF16))
        ctx.persistent.update({"ppl", "ppst", "ppv", "prep_done", stsem})
        self.nl = 0
        self.qtgt, self.atgt, self.ptgt = {}, {}, {}
        self.ztgt = self.stgt = None

    def _ldq(self, eng, t):
        self.ctx.inc("ppl", eng.dma_start(out=self.qwb[t % 2][:, :],
                     in_=self.dr["qw0"][128 * t:128 * (t + 1), :]), 16)
        self.nl += 16
        self.qtgt[t] = self.nl

    def _lda(self, eng, t):
        for h in range(2):
            self.ctx.inc("ppl", eng.dma_start(
                out=self.amb[64 * h:64 * (h + 1), :],
                in_=self.dr["am0"][2 * t + h:2 * t + h + 1, :].broadcast_to([64, 4096])), 16)
        self.nl += 32
        self.atgt[t] = self.nl

    def emit_loads(self, block):
        def _s(s):
            self._ldq(s, 0)
            self._ldq(s, 1)
            self._lda(s, 0)
        block.sync(_s)

    def emit_rest(self, block):
        ctx, dr = self.ctx, self.dr
        specs = [(0, False), (0, True), (1, False), (2, False)]

        def _sc(sc):
            def ldp(jj):
                ctx.inc("ppl", sc.dma_start(out=self.qwb[jj % 2][:, :],
                         in_=dr["qw1"][specs[jj][0], :, :]), 16)
                self.nl += 16
                self.ptgt[jj] = self.nl
            for k in range(1, 9):
                ctx.wait(sc, "ppv", k)
                if k <= 2:
                    self._ldq(sc, k + 1)
                if k <= 3:
                    self._lda(sc, k)
                if k == 3:
                    ldp(0)
                if k == 4:
                    ldp(1)
                    ctx.inc("ppl", sc.dma_start(out=self.amb[:, :],
                             in_=dr["zp1"].broadcast_to([128, 4096])), 16)
                    self.nl += 16
                    self.ztgt = self.nl
                    ctx.inc("ppl", sc.dma_start(out=self.tmp[:, :],
                             in_=dr["sc1"].broadcast_to([128, 4096])), 16)
                    self.nl += 16
                    self.stgt = self.nl
                if k == 5:
                    ldp(2)
                if k == 6:
                    ldp(3)
                if k <= 4:
                    t = k - 1
                    ctx.inc("ppst", sc.dma_start(
                        out=dr["w0t_loc"][128 * t:128 * (t + 1), :],
                        in_=self.wout[:, :]), 16)
                else:
                    jj = k - 5
                    dst = dr["w1t_loc"].rearrange("(g j) n -> j g n", j=4)[jj]
                    ctx.inc("ppst", sc.dma_start(out=dst, in_=self.wout[:, :]), 16)
        block.scalar(_sc)

        nf4 = [(float(j), float(c)) for j, c in enumerate(NF4) if c != 0.0]

        def _v(v):
            for t in range(4):
                ctx.wait(v, "ppl", self.qtgt[t])
                b = t % 2
                v.tensor_copy(self.qf[:, :], self.qwb[b][:, :])
                v.drain()
                first = True
                for j, c in nf4:
                    dstp = self.acc if first else self.tmp
                    v.tensor_scalar(dstp[:, :], self.qf[:, :], j, c,
                                    ALU.is_equal, ALU.mult)
                    v.drain()
                    if not first:
                        v.tensor_tensor(self.acc[:, :], self.acc[:, :],
                                        self.tmp[:, :], ALU.add)
                        v.drain()
                    first = False
                ctx.wait(v, "ppl", self.atgt[t])
                if t >= 1:
                    ctx.wait(v, "ppst", 16 * t)
                i = v.tensor_tensor(self.wout[:, :], self.acc[:, :],
                                    self.amb[:, :], ALU.mult)
                v.drain()
                ctx.inc("ppv", i, 1)
            for jj, (d, shift) in enumerate(specs):
                ctx.wait(v, "ppl", self.ptgt[jj])
                if jj == 0:
                    ctx.wait(v, "ppl", self.stgt)
                b = jj % 2
                if shift:
                    v.tensor_scalar(self.qwb[b][:, :], self.qwb[b][:, :], 6, 63,
                                    ALU.logical_shift_right, ALU.bitwise_and)
                else:
                    v.tensor_single_scalar(self.qwb[b][:, :], self.qwb[b][:, :],
                                           63, ALU.bitwise_and)
                v.drain()
                v.tensor_copy(self.qf[:, :], self.qwb[b][:, :])
                v.drain()
                v.tensor_tensor(self.qf[:, :], self.qf[:, :], self.amb[:, :],
                                ALU.subtract)
                v.drain()
                ctx.wait(v, "ppst", 16 * (4 + jj))
                i = v.tensor_tensor(self.wout[:, :], self.qf[:, :],
                                    self.tmp[:, :], ALU.mult)
                v.drain()
                ctx.inc("ppv", i, 1)
        block.vector(_v)

        def _g(g):
            ctx.wait(g, self.stsem, 64)
            ctx.inc("ccxa", g.collective_compute(
                "AllGather", ALU.bypass, replica_groups=self.rg,
                ins=[dr["xn_loc0"][:, :]], outs=[dr["xn_g0"][:, :]]), 1)
            ctx.wait(g, self.stsem, 128)
            ctx.inc("ccxa", g.collective_compute(
                "AllGather", ALU.bypass, replica_groups=self.rg,
                ins=[dr["xn_loc1"][:, :]], outs=[dr["xn_g1"][:, :]]), 1)
            ctx.wait(g, "ppst", 16 * 8)
            ctx.inc("prep_done", g.nop(), 1)
        block.gpsimd(_g)

    def emit_ccw(self, block):
        ctx, dr = self.ctx, self.dr

        def _g(g):
            ctx.wait(g, "ppst", 16 * 8)
            ctx.inc("ccw", g.collective_compute(
                "AllGather", ALU.bypass, replica_groups=self.rg,
                ins=[dr["w0t_loc"][:, :]], outs=[dr["w0t"][:, :]]), 1)
            ctx.inc("ccw", g.collective_compute(
                "AllGather", ALU.bypass, replica_groups=self.rg,
                ins=[dr["w1t_loc"][:, :]], outs=[dr["w1t"][:, :]]), 1)
        block.gpsimd(_g)


def emit_prep(ctx, block, nc, es, dr, tag=""):
    qw = es.enter_context(nc.sbuf_tensor(f"pp_qw{tag}", [128, 4096], I32))
    qf = es.enter_context(nc.sbuf_tensor(f"pp_qf{tag}", [128, 4096], F32))
    acc = es.enter_context(nc.sbuf_tensor(f"pp_acc{tag}", [128, 4096], F32))
    tmp = es.enter_context(nc.sbuf_tensor(f"pp_tmp{tag}", [128, 4096], F32))
    amb = es.enter_context(nc.sbuf_tensor(f"pp_amb{tag}", [128, 4096], F32))
    wout = es.enter_context(nc.sbuf_tensor(f"pp_w{tag}", [128, 4096], BF16))
    q3 = [es.enter_context(nc.sbuf_tensor(f"pp_q3_{d}{tag}", [128, 4096], I32)) for d in range(3)]
    zpb = es.enter_context(nc.sbuf_tensor(f"pp_zpb{tag}", [128, 4096], F32))
    scb = es.enter_context(nc.sbuf_tensor(f"pp_scb{tag}", [128, 4096], F32))
    u = es.enter_context(nc.sbuf_tensor(f"pp_u{tag}", [128, 4096], I32))
    uf = es.enter_context(nc.sbuf_tensor(f"pp_uf{tag}", [128, 4096], F32))
    ppl, ppv, ppst = f"ppl{tag}", f"ppv{tag}", f"ppst{tag}"

    def _sync(s):
        for t in range(4):
            if t > 0:
                ctx.wait(s, ppv, t)
            ctx.inc(ppl, s.dma_start(out=qw[:, :],
                     in_=dr["qw0"][128 * t:128 * t + 128, :]), 16)
            for half in range(2):
                ctx.inc(ppl, s.dma_start(
                    out=amb[64 * half:64 * half + 64, :],
                    in_=dr["am0"][2 * t + half:2 * t + half + 1, :].broadcast_to([64, 4096])), 16)
            ctx.wait(s, ppv, t + 1)
            ctx.inc(ppst, s.dma_start(out=dr["w0t_loc"][128 * t:128 * t + 128, :],
                                      in_=wout[:, :]), 16)
        for d in range(3):
            ctx.inc(ppl, s.dma_start(out=q3[d][:, :], in_=dr["qw1"][d, :, :]), 16)
        ctx.inc(ppl, s.dma_start(out=zpb[:, :], in_=dr["zp1"].broadcast_to([128, 4096])), 16)
        ctx.inc(ppl, s.dma_start(out=scb[:, :], in_=dr["sc1"].broadcast_to([128, 4096])), 16)
        for j in range(4):
            ctx.wait(s, ppv, 5 + j)
            dst = dr["w1t_loc"].rearrange("(g j) n -> j g n", j=4)[j]
            ctx.inc(ppst, s.dma_start(out=dst, in_=wout[:, :]), 16)
        ctx.wait(s, ppst, 16 * 8)
        ctx.inc(f"prep_done{tag}", s.nop(), 1)
    block.sync(_sync)

    def _vec(v):
        for t in range(4):
            ctx.wait(v, ppl, 48 * (t + 1))
            if t > 0:
                ctx.wait(v, ppst, 16 * t)
            v.tensor_copy(qf[:, :], qw[:, :])
            v.drain()
            first = True
            for j, c in enumerate(NF4):
                if c == 0.0:
                    continue
                dstp = acc if first else tmp
                v.tensor_scalar(dstp[:, :], qf[:, :], float(j), float(c),
                                ALU.is_equal, ALU.mult)
                v.drain()
                if not first:
                    v.tensor_tensor(acc[:, :], acc[:, :], tmp[:, :], ALU.add)
                    v.drain()
                first = False
            i = v.tensor_tensor(wout[:, :], acc[:, :], amb[:, :], ALU.mult)
            ctx.inc(ppv, i, 1)
            v.drain()
        ctx.wait(v, ppl, 192 + 80)
        specs = [(0, False), (0, True), (1, False), (2, False)]
        for j, (d, shift) in enumerate(specs):
            ctx.wait(v, ppst, 16 * (4 + j))
            if shift:
                v.tensor_scalar(u[:, :], q3[d][:, :], 6, 63,
                                ALU.logical_shift_right, ALU.bitwise_and)
            else:
                v.tensor_single_scalar(u[:, :], q3[d][:, :], 63, ALU.bitwise_and)
            v.drain()
            v.tensor_copy(uf[:, :], u[:, :])
            v.drain()
            v.tensor_tensor(uf[:, :], uf[:, :], zpb[:, :], ALU.subtract)
            v.drain()
            i = v.tensor_tensor(wout[:, :], uf[:, :], scb[:, :], ALU.mult)
            ctx.inc(ppv, i, 1)
            v.drain()
    block.vector(_vec)


def emit_p1(ctx, block, nc, es, bufs, xsrc, xn_locs, tag=""):
    NCH, CT = 8, 128
    xc = [es.enter_context(nc.sbuf_tensor(f"p1_x{i}{tag}", [128, 32 * CT], F32))
          for i in range(2)]
    sq = es.enter_context(nc.sbuf_tensor(f"p1_sq{tag}", [128, 32 * CT], F32))
    xn = es.enter_context(nc.sbuf_tensor(f"p1_xn{tag}", [128, 32 * CT], BF16))
    srow = es.enter_context(nc.sbuf_tensor(f"p1_srow{tag}", [1, CT], F32))
    vrow = es.enter_context(nc.sbuf_tensor(f"p1_vrow{tag}", [1, CT], F32))
    lnrow = es.enter_context(nc.sbuf_tensor(f"p1_lnrow{tag}", [1, CT], F32))
    rrow = es.enter_context(nc.sbuf_tensor(f"p1_rrow{tag}", [1, CT], F32))
    mrow = es.enter_context(nc.sbuf_tensor(f"p1_mrow{tag}", [1, CT], F32))
    rbs = es.enter_context(nc.sbuf_tensor(f"p1_rbs{tag}", [128, CT], F32))
    mbs = es.enter_context(nc.sbuf_tensor(f"p1_mbs{tag}", [128, CT], F32))
    ps_s = es.enter_context(nc.psum_tensor(f"p1_pss{tag}", [1, CT], F32))
    ps_q = es.enter_context(nc.psum_tensor(f"p1_psq{tag}", [1, CT], F32))
    ps_rb = es.enter_context(nc.psum_tensor(f"p1_psrb{tag}", [128, CT], F32))
    ps_mb = es.enter_context(nc.psum_tensor(f"p1_psmb{tag}", [128, CT], F32))
    ld, sqs, mms, var, rst, mr, bc, rbc, ap, st = [f"p1{n}{tag}" for n in
        ["ld", "sq", "mm", "var", "rst", "mr", "bc", "rbc", "ap", "st"]]
    stname = "p1st" + tag
    xcv = [b.rearrange("p (t c) -> p t c", t=32) for b in xc]
    sqv = sq.rearrange("p (t c) -> p t c", t=32)
    xnv = xn.rearrange("p (t c) -> p t c", t=32)

    def bcast(b):
        return b.rearrange("p (o c) -> p o c", o=1).broadcast_to([128, 32, CT])

    def _sync(s):
        for c in range(NCH):
            if c >= 2:
                ctx.wait(s, ap, c - 1)
            src = xsrc[:, CT * c:CT * (c + 1)].rearrange("(t p) c -> p t c", p=128)
            ctx.inc(ld, s.dma_start(out=xcv[c % 2], in_=src), 16)
            if c >= 1:
                ctx.wait(s, ap, c)
                h, sub = divmod(c - 1, NCH // 2)
                dst = xn_locs[h][:, CT * sub:CT * (sub + 1)] \
                    .rearrange("(t p) c -> p t c", p=128)
                ctx.inc(stname, s.dma_start(out=dst, in_=xnv), 16)
        ctx.wait(s, ap, NCH)
        h, sub = divmod(NCH - 1, NCH // 2)
        dst = xn_locs[h][:, CT * sub:CT * (sub + 1)].rearrange("(t p) c -> p t c", p=128)
        ctx.inc(stname, s.dma_start(out=dst, in_=xnv), 16)
        ctx.wait(s, stname, 16 * NCH)
        ctx.inc(f"p1_done{tag}", s.nop(), 1)
    block.sync(_sync)

    def _scalar(sc):
        for c in range(NCH):
            ctx.wait(sc, ld, 16 * (c + 1))
            if c >= 1:
                ctx.wait(sc, mms, c)
            i = sc.activation(sq[:, :], xc[c % 2][:, :], AF.Square)
            ctx.inc(sqs, i, 1)
            ctx.wait(sc, var, c + 1)
            if c >= 1:
                ctx.wait(sc, bc, c)
            sc.activation(lnrow[:, :], vrow[:, :], AF.Ln, bias=f32(LN_EPS))
            sc.drain()
            i = sc.activation(rrow[:, :], lnrow[:, :], AF.Exp, scale=f32(-0.5))
            sc.drain()
            ctx.inc(rst, i, 1)
    block.scalar(_scalar)

    def _tensor(t):
        ctx.wait(t, "init", 1)
        for c in range(NCH):
            ctx.wait(t, ld, 16 * (c + 1))
            ctx.wait(t, sqs, c + 1)
            if c >= 1:
                ctx.wait(t, var, c)
            for f in range(32):
                t.matmul(ps_s[:, :], bufs["ones_col_f32"][:, :], xcv[c % 2][:, f, :],
                         start=(f == 0), stop=(f == 31))
            for f in range(32):
                i = t.matmul(ps_q[:, :], bufs["ones_col_f32"][:, :], sqv[:, f, :],
                             start=(f == 0), stop=(f == 31))
            ctx.inc(mms, i, 1)
            ctx.wait(t, mr, c + 1)
            if c >= 1:
                ctx.wait(t, rbc, c)
            t.matmul(ps_rb[:, :], bufs["ones_row_f32"][:, :], rrow[:, :],
                     start=True, stop=True)
            i = t.matmul(ps_mb[:, :], bufs["ones_row_f32"][:, :], mrow[:, :],
                         start=True, stop=True)
            ctx.inc(bc, i, 1)
    block.tensor(_tensor)

    def _vec(v):
        for c in range(NCH):
            ctx.wait(v, mms, c + 1)
            v.tensor_single_scalar(srow[:, :], ps_s[:, :], 1.0 / D, ALU.mult)
            v.drain()
            v.tensor_single_scalar(vrow[:, :], ps_q[:, :], 1.0 / D, ALU.mult)
            v.drain()
            v.tensor_tensor(mrow[:, :], srow[:, :], srow[:, :], ALU.mult)
            v.drain()
            i = v.tensor_tensor(vrow[:, :], vrow[:, :], mrow[:, :], ALU.subtract)
            v.drain()
            ctx.inc(var, i, 1)
            ctx.wait(v, rst, c + 1)
            i = v.tensor_tensor(mrow[:, :], srow[:, :], rrow[:, :], ALU.mult)
            v.drain()
            ctx.inc(mr, i, 1)
            ctx.wait(v, bc, c + 1)
            v.tensor_copy(rbs[:, :], ps_rb[:, :])
            v.drain()
            i = v.tensor_copy(mbs[:, :], ps_mb[:, :])
            v.drain()
            ctx.inc(rbc, i, 1)
            if c >= 1:
                ctx.wait(v, stname, 16 * c)
            v.tensor_tensor(xcv[c % 2], xcv[c % 2], bcast(rbs), ALU.mult)
            v.drain()
            i = v.tensor_tensor(xnv, xcv[c % 2], bcast(mbs), ALU.subtract)
            v.drain()
            ctx.inc(ap, i, 1)
    block.vector(_vec)


def emit_p2_wload(ctx, block, nc, es, of_tiles, wqkv, bqkv, tag=""):
    nof = len(of_tiles)
    wsb = [es.enter_context(nc.sbuf_tensor(f"p2_w{i}{tag}", [128, 32 * of_tiles[i][1]], BF16))
           for i in range(nof)]
    bsb = es.enter_context(nc.sbuf_tensor(f"p2_b{tag}", [128, nof], F32))
    wl, bl = f"p2wl{tag}", f"p2bl{tag}"
    ctx.persistent.update({wl, bl})

    def _g1(g):
        for i, (r0, w) in enumerate(of_tiles):
            ctx.inc(wl, g.dma_start(out=wsb[i][:, :], in_=wqkv[:, 32 * r0:32 * (r0 + w)]), 16)
        for i, (r0, w) in enumerate(of_tiles):
            ctx.inc(bl, g.dma_start(out=bsb[0:w, i:i + 1], in_=bqkv[r0:r0 + w, :]), 16)
    block.gpsimd(_g1)
    return wsb, bsb


def emit_p2(ctx, block, nc, es, bufs, of_tiles, wqkv, bqkv, xn_gs, qkv_loc, tag="",
            ccx_name=None, prew=None):
    nof = len(of_tiles)
    if prew is None:
        wsb, bsb = emit_p2_wload(ctx, block, nc, es, of_tiles, wqkv, bqkv, tag)
    else:
        wsb, bsb = prew
    xnc = [es.enter_context(nc.sbuf_tensor(f"p2_xn{i}{tag}", [128, 32 * 512], BF16))
           for i in range(2)]
    stg = [es.enter_context(nc.sbuf_tensor(f"p2_stg{i}{tag}", [128, 512], BF16))
           for i in range(4)]
    pss = [es.enter_context(nc.psum_tensor(f"p2_ps{i}{tag}", [128, 512], F32))
           for i in range(4)]
    wl, bl, mm, ev = [f"p2{n}{tag}" for n in ["wl", "bl", "mm", "ev"]]
    xl = [f"p2xl{i}{tag}" for i in range(2)]
    st = [f"p2st{i}{tag}" for i in range(4)]

    chunks = [(r, 0) for r in range(8)] + [(r, 1) for r in range(8)]

    def _sync(s):
        for tp in range(8):
            if tp >= 1:
                ctx.wait(s, mm, tp * nof)
            for h in range(2):
                r, half = chunks[2 * tp + h]
                if ccx_name is not None and (2 * tp + h) in (0, 8):
                    ctx.wait(s, ccx_name, half + 1)
                src = xn_gs[half][r * D:(r + 1) * D, :] \
                    .rearrange("(t p) c -> p t c", p=128)
                ctx.inc(xl[h], s.dma_start(
                    out=xnc[h].rearrange("p (t c) -> p t c", t=32), in_=src), 16)
    block.sync(_sync)

    def _tensor(t):
        for tp in range(8):
            for h in range(2):
                ctx.wait(t, xl[h], 16 * (tp + 1))
            for i, (r0, w) in enumerate(of_tiles):
                k = tp * nof + i
                if tp == 0:
                    ctx.wait(t, wl, 16 * (i + 1))
                if k >= 2:
                    ctx.wait(t, ev, 2 * (k - 1))
                wv = wsb[i].rearrange("p (t o) -> p t o", t=32)
                for f in range(32):
                    for h in range(2):
                        ii = t.matmul(pss[(k % 2) * 2 + h][0:w, :], wv[:, f, :],
                                      xnc[h].rearrange("p (t c) -> p t c", t=32)[:, f, :],
                                      start=(f == 0), stop=(f == 31))
                ctx.inc(mm, ii, 1)
    block.tensor(_tensor)

    def _vec(v):
        ctx.wait(v, bl, 16 * nof)
        for k in range(8 * nof):
            i_of = k % nof
            w = of_tiles[i_of][1]
            ctx.wait(v, mm, k + 1)
            for h in range(2):
                e = 2 * k + h
                if e >= 4:
                    ctx.wait(v, st[e % 4], 16 * (e // 4))
                ii = v.tensor_scalar(stg[e % 4][0:w, :], pss[(k % 2) * 2 + h][0:w, :],
                                     bsb[0:w, i_of:i_of + 1], None, ALU.add)
                v.drain()
                ctx.inc(ev, ii, 1)
    block.vector(_vec)

    def _sc(sc):
        for k in range(8 * nof):
            tp, i_of = divmod(k, nof)
            r0, w = of_tiles[i_of]
            for h in range(2):
                e = 2 * k + h
                r, half = chunks[2 * tp + h]
                col = (2 * r + half) * 512
                ctx.wait(sc, ev, e + 1)
                ctx.inc(st[e % 4], sc.dma_start(
                    out=qkv_loc[r0:r0 + w, col:col + 512],
                    in_=stg[e % 4][0:w, :]), 16)
        for i4 in range(4):
            nst = (16 * nof + 3 - i4) // 4
            ctx.wait(sc, st[i4], 16 * nst)
    block.scalar(_sc)


def emit_p3(ctx, block, nc, es, bufs, layer, qkv_loc, ao_loc, tag=""):
    if layer == 0:
        dh, units = 64, [(0, 0), (1, 0)]
        qrow, krow, vrow = 0, 64, 128
        scale = 1.0
        ebias = -M0
        rec_bias = -math.log(math.sqrt(512.0))
        causal = False
    else:
        dh, units = 128, [(s, hl) for hl in range(4) for s in range(2)]
        qrow, krow, vrow = 0, 512, 1024
        scale = 1.0 / math.sqrt(128.0)
        ebias = -M1
        rec_bias = 0.0
        causal = True

    qsb = [es.enter_context(nc.sbuf_tensor(f"p3_q{i}{tag}", [128, 4096], BF16)) for i in range(2)]
    ksb = [es.enter_context(nc.sbuf_tensor(f"p3_k{i}{tag}", [128, 4096], BF16)) for i in range(2)]
    vsb = [es.enter_context(nc.sbuf_tensor(f"p3_v{i}{tag}", [128, 4096], BF16)) for i in range(2)]
    vt = es.enter_context(nc.sbuf_tensor(f"p3_vt{tag}", [128, 32 * dh], BF16))
    expS = [es.enter_context(nc.sbuf_tensor(f"p3_e{i}{tag}", [128, 32 * 512], BF16))
            for i in range(2)]
    lnb = [es.enter_context(nc.sbuf_tensor(f"p3_ln{i}{tag}", [128, 512], F32)) for i in range(2)]
    rec = [es.enter_context(nc.sbuf_tensor(f"p3_rec{i}{tag}", [128, 512], F32)) for i in range(2)]
    aos = [es.enter_context(nc.sbuf_tensor(f"p3_ao{i}{tag}", [128, 512], BF16)) for i in range(2)]
    esum = [es.enter_context(nc.sbuf_tensor(f"p3_es{i}{tag}", [128, 512], F32)) for i in range(2)]
    ps_s = [es.enter_context(nc.psum_tensor(f"p3_pss{i}{tag}", [128, 512], F32)) for i in range(4)]
    ps_o = es.enter_context(nc.psum_tensor(f"p3_pso{tag}", [128, 512], F32))
    ps_d = es.enter_context(nc.psum_tensor(f"p3_psd{tag}", [128, 512], F32))
    ps_t = [es.enter_context(nc.psum_tensor(f"p3_pst{i}{tag}", [128, 128], BF16)) for i in range(2)]
    vtm, vte, smm, sev, mk, av, rc, no, ds = [f"p3{n}{tag}" for n in
        ["vtm", "vte", "smm", "sev", "mk", "av", "rc", "no", "ds"]]
    qkvl = [f"p3qkv{i}{tag}" for i in range(2)]
    ast = [f"p3ast{i}{tag}" for i in range(2)]
    vtv = vt.rearrange("p (t d) -> p t d", t=32)
    ev_ = [b.rearrange("p (t c) -> p t c", t=32) for b in expS]
    NU = len(units)

    def kcm(qc):
        return min(32, 4 * (qc + 1)) if causal else 32

    KSUM = sum(kcm(q) for q in range(8))

    def SC(u, qc):
        return u * KSUM + sum(kcm(q) for q in range(qc))

    def QI(u, qc):
        return u * 8 + qc

    if causal:
        def _g(g):
            for u in range(NU):
                for qc in range(8):
                    for p in range(4):
                        kc = 4 * qc + p
                        ctx.wait(g, sev, SC(u, qc) + kc + 1)
                        i = g.affine_select(out=ev_[qc % 2][:, kc, :],
                                            in_=ev_[qc % 2][:, kc, :],
                                            compare_op=ALU.is_ge, fill=0.0,
                                            base=-128 * p,
                                            pattern=[[1, 512]], channel_multiplier=-1)
                        ctx.inc(mk, i, 1)
                    g.drain()
        block.gpsimd(_g)

    def _sync(s):
        def load(u):
            sq_, hl = units[u]
            c0 = sq_ * 4096
            ub = u % 2
            for row0, sb in ((qrow, qsb[ub]), (krow, ksb[ub]), (vrow, vsb[ub])):
                ctx.inc(qkvl[ub], s.dma_start(out=sb[0:dh, :],
                         in_=qkv_loc[row0 + hl * dh:row0 + (hl + 1) * dh, c0:c0 + 4096]), 16)
        load(0)
        if NU > 1:
            load(1)
        AOFloc = 64 if layer == 0 else 512
        for u, (sq_, hl) in enumerate(units):
            c0 = sq_ * 4096
            for qc in range(8):
                q = QI(u, qc)
                ctx.wait(s, no, q + 1)
                g0 = c0 + qc * 512
                blk = g0 // 1024
                r0 = blk * AOFloc + hl * dh
                ctx.inc(ast[q % 2], s.dma_start(
                    out=ao_loc[r0:r0 + dh, (g0 % 1024):(g0 % 1024) + 512],
                    in_=aos[q % 2][0:dh, :]), 16)
            if u + 2 < NU:
                ctx.wait(s, av, 8 * (u + 1))
                load(u + 2)
        ctx.wait(s, ast[0], 16 * 4 * NU)
        ctx.wait(s, ast[1], 16 * 4 * NU)
        ctx.inc(f"p3_done{tag}", s.nop(), 1)
    block.sync(_sync)

    def _tensor(t):
        ctx.wait(t, "init", 1)

        def emit_sprime(u, qc, ub):
            km = kcm(qc)
            for kc in range(km):
                n = SC(u, qc) + kc
                if n >= 4:
                    ctx.wait(t, sev, n - 3)
                i = t.matmul(ps_s[n % 4][:, :],
                             ksb[ub][0:dh, kc * 128:(kc + 1) * 128],
                             qsb[ub][0:dh, qc * 512:(qc + 1) * 512],
                             start=True, stop=True)
                ctx.inc(smm, i, 1)

        def emit_av(u, qc, ub):
            q = QI(u, qc)
            km = kcm(qc)
            ctx.wait(t, sev, SC(u, qc) + km)
            if causal:
                ctx.wait(t, mk, 4 * (q + 1))
            if qc == 0:
                ctx.wait(t, vte, 32 * (u + 1))
            ctx.wait(t, rc, q)
            ctx.wait(t, no, q)
            for kc in range(km):
                t.matmul(ps_o[0:dh, :], vtv[:, kc, :], ev_[qc % 2][:, kc, :],
                         start=(kc == 0), stop=(kc == km - 1))
            ctx.wait(t, ds, q + 1)
            i = t.matmul(ps_d[0:dh, :], bufs["ones_k_f32"][:, 0:dh], esum[q % 2][:, :],
                         start=True, stop=True)
            ctx.inc(av, i, 1)

        for u in range(NU):
            ub = u % 2
            ctx.wait(t, qkvl[ub], 48 * (u // 2 + 1))
            for kc in range(32):
                if u > 0 or kc >= 2:
                    ctx.wait(t, vte, 32 * u + kc - 1)
                i = t.transpose(ps_t[kc % 2][:, 0:dh], vsb[ub][0:dh, kc * 128:(kc + 1) * 128],
                                bufs["ident"][0:dh, 0:dh])
                ctx.inc(vtm, i, 1)
            emit_sprime(u, 0, ub)
            for qc in range(1, 8):
                emit_av(u, qc - 1, ub)
                emit_sprime(u, qc, ub)
            emit_av(u, 7, ub)
    block.tensor(_tensor)

    def _scalar(sc):
        def emit_rc(j):
            jc = j % 8
            ctx.wait(sc, av, j + 1)
            if j >= 2:
                ctx.wait(sc, no, j - 1)
            sc.activation(lnb[jc % 2][0:dh, :], ps_d[0:dh, :], AF.Ln)
            sc.drain()
            i = sc.activation(rec[jc % 2][0:dh, :], lnb[jc % 2][0:dh, :], AF.Exp,
                              scale=f32(-1.0), bias=f32(rec_bias))
            sc.drain()
            ctx.inc(rc, i, 1)

        for u in range(NU):
            for qc in range(8):
                q = QI(u, qc)
                km = kcm(qc)
                for kc in range(km):
                    n = SC(u, qc) + kc
                    ctx.wait(sc, smm, n + 1)
                    if kc == 0 and q >= 2:
                        ctx.wait(sc, av, q - 1)
                    i = sc.activation(ev_[qc % 2][:, kc, :], ps_s[n % 4][:, :],
                                      AF.Exp, scale=f32(scale), bias=f32(ebias))
                    ctx.inc(sev, i, 1)
                if qc >= 1:
                    emit_rc(q - 1)
            emit_rc(QI(u, 7))
    block.scalar(_scalar)

    def _vec(v):
        ctx.wait(v, "init", 1)
        for u, (sq_, hl) in enumerate(units):
            ub = u % 2
            if u > 0:
                ctx.wait(v, av, 8 * u)
            for kc in range(32):
                ctx.wait(v, vtm, 32 * u + kc + 1)
                i = v.tensor_copy(vtv[:, kc, :], ps_t[kc % 2][:, 0:dh])
                v.drain()
                ctx.inc(vte, i, 1)
            for qc in range(8):
                q = QI(u, qc)
                km = kcm(qc)
                if q >= 2:
                    ctx.wait(v, av, q - 1)
                for kc in range(km):
                    if causal and kc == km - 4:
                        ctx.wait(v, mk, 4 * (q + 1))
                    ctx.wait(v, sev, SC(u, qc) + kc + 1)
                    if kc == 0:
                        v.tensor_copy(esum[q % 2][:, :], ev_[qc % 2][:, kc, :])
                    else:
                        v.tensor_tensor(esum[q % 2][:, :], esum[q % 2][:, :],
                                        ev_[qc % 2][:, kc, :], ALU.add)
                    i = v.drain()
                ctx.inc(ds, i, 1)
                ctx.wait(v, rc, q + 1)
                if q >= 2:
                    ctx.wait(v, ast[q % 2], 16 * (q // 2))
                i = v.tensor_tensor(aos[q % 2][0:dh, :], ps_o[0:dh, :],
                                    rec[q % 2][0:dh, :], ALU.mult)
                v.drain()
                ctx.inc(no, i, 1)
    block.vector(_vec)


def emit_p4(ctx, block, nc, es, bufs, AOF, ao_src, xn_locs, woT, wfT, brows,
            xsrc, xdst, tag="", ao_cols_dyn=False):
    na = AOF // 128
    nf = na + 32
    act = es.enter_context(nc.sbuf_tensor(f"p4_act{tag}", [128, nf * 1024], BF16))
    wsb = [es.enter_context(nc.sbuf_tensor(f"p4_w{i}{tag}", [128, nf * 128], BF16)) for i in range(2)]
    brsb = [es.enter_context(nc.sbuf_tensor(f"p4_br{i}{tag}", [1, 4096], BF16))
            for i in range(len(brows))]
    xt = [es.enter_context(nc.sbuf_tensor(f"p4_x{i}{tag}", [128, 1024], F32)) for i in range(2)]
    stg = [es.enter_context(nc.sbuf_tensor(f"p4_s{i}{tag}", [128, 1024], F32)) for i in range(2)]
    pss = [es.enter_context(nc.psum_tensor(f"p4_ps{i}{tag}", [128, 512], F32)) for i in range(2)]
    aol, xnl, brl, mm, ev = [f"p4{n}{tag}" for n in ["ao", "xn", "br", "mm", "ev"]]
    wl = [f"p4wl{i}{tag}" for i in range(2)]
    xld = [f"p4x{i}{tag}" for i in range(2)]
    st = [f"p4st{i}{tag}" for i in range(2)]
    nbr = len(brows)
    actv = act.rearrange("p (t c) -> p t c", t=nf)

    def _g(g):
        for i, br in enumerate(brows):
            ctx.inc(brl, g.dma_start(out=brsb[i][:, :], in_=br[:, :]), 16)
        for f in range(32):
            for hh in range(2):
                ctx.inc(xnl, g.dma_start(
                    out=actv[:, na + f, hh * 512:(hh + 1) * 512],
                    in_=xn_locs[hh][f * 128:(f + 1) * 128, :]), 16)
        for of in range(32):
            if of >= 2:
                ctx.wait(g, mm, 2 * (of - 1))
            wv = wsb[of % 2].rearrange("p (t o) -> p t o", t=nf)
            ctx.inc(wl[of % 2], g.dma_start(
                out=wv[:, 0:na, :],
                in_=woT[:, of * na * 128:(of + 1) * na * 128]
                    .rearrange("p (t o) -> p t o", t=na)), 16)
            for fbg in range(4):
                ctx.inc(wl[of % 2], g.dma_start(
                    out=wv[:, na + fbg * 8:na + (fbg + 1) * 8, :],
                    in_=wfT[fbg * 1024:(fbg + 1) * 1024, of * 128:(of + 1) * 128]
                        .rearrange("(t p) o -> p t o", p=128)), 16)
    block.gpsimd(_g)

    def _sync(s):
        if ao_cols_dyn:
            pid = ctx.pid if getattr(ctx, "pid", None) is not None else s.partition_id()
            srcap = ao_src[:, bass.ts(pid, 1024)].rearrange("(t p) c -> p t c", p=128)
        else:
            srcap = ao_src.rearrange("(t p) c -> p t c", p=128)
        ctx.inc(aol, s.dma_start(out=actv[:, 0:na, :], in_=srcap), 16)
        for of in range(34):
            if of >= 2:
                ctx.wait(s, ev, of - 1)
            if of < 32:
                ctx.inc(xld[of % 2], s.dma_start(out=xt[of % 2][:, :],
                                                 in_=xsrc[of * 128:(of + 1) * 128, :]), 16)
            if of >= 2:
                o2 = of - 2
                ctx.inc(st[o2 % 2], s.dma_start(out=xdst[o2 * 128:(o2 + 1) * 128, :],
                                                in_=stg[o2 % 2][:, :]), 16)
        ctx.wait(s, st[0], 16 * 16)
        ctx.wait(s, st[1], 16 * 16)
        ctx.inc(f"p4_done{tag}", s.nop(), 1)
    block.sync(_sync)

    def _tensor(t):
        ctx.wait(t, "init", 1)
        ctx.wait(t, aol, 16)
        ctx.wait(t, xnl, 16 * 64)
        ctx.wait(t, brl, 16 * nbr)
        for of in range(32):
            ctx.wait(t, wl[of % 2], 80 * (of // 2 + 1))
            if of >= 1:
                ctx.wait(t, ev, of)
            wv = wsb[of % 2].rearrange("p (t o) -> p t o", t=nf)
            for f in range(nf):
                for tc in range(2):
                    t.matmul(pss[tc][:, :], wv[:, f, :],
                             actv[:, f, tc * 512:(tc + 1) * 512],
                             start=(f == 0), stop=False)
            for tc in range(2):
                for i in range(nbr):
                    ii = t.matmul(pss[tc][:, :],
                                  brsb[i][:, of * 128:(of + 1) * 128],
                                  bufs["ones_row_bf"][:, :],
                                  start=False, stop=(i == nbr - 1))
            ctx.inc(mm, ii, 2)
    block.tensor(_tensor)

    def _vec(v):
        for of in range(32):
            ctx.wait(v, mm, 2 * (of + 1))
            ctx.wait(v, xld[of % 2], 16 * (of // 2 + 1))
            if of >= 2:
                ctx.wait(v, st[of % 2], 16 * (of // 2))
            for tc in range(2):
                i = v.tensor_tensor(stg[of % 2][:, tc * 512:(tc + 1) * 512],
                                    pss[tc][:, :], xt[of % 2][:, tc * 512:(tc + 1) * 512],
                                    ALU.add)
                v.drain()
            ctx.inc(ev, i, 1)
    block.vector(_vec)


def host_prep(inp):
    out = [dict() for _ in range(NCORES)]
    x = np.ascontiguousarray(np.asarray(inp["L0_attn_x"], np.float32).reshape(TOK, D))
    for c in range(NCORES):
        out[c]["x0"] = np.ascontiguousarray(x[TPC * c:TPC * (c + 1)].T)

    def _tile_w(wT, widths):
        parts = []
        r0 = 0
        for w in widths:
            parts.append(wT.reshape(32, 128, -1)[:, :, r0:r0 + w]
                         .transpose(1, 0, 2).reshape(128, 32 * w))
            r0 += w
        return np.ascontiguousarray(np.concatenate(parts, 1))

    def _tile_of(wT):
        na = wT.shape[0] // 128
        return np.ascontiguousarray(
            wT.reshape(na, 128, 32, 128).transpose(1, 2, 0, 3).reshape(128, 32 * na * 128))

    wqk = np.asarray(inp["L0_attn_Wqk_w"], np.float32)
    bqk = np.asarray(inp["L0_attn_Wqk_b"], np.float32)
    wv0 = np.asarray(inp["L0_attn_Wv_w"], np.float32)
    bv0 = np.asarray(inp["L0_attn_Wv_b"], np.float32)
    for c in range(NCORES):
        qr = [128 * c + 2 * e for e in range(64)]
        kr = [128 * c + 2 * e + 1 for e in range(64)]
        w = np.concatenate([wqk[qr], wqk[kr], wv0[64 * c:64 * c + 64]], 0)
        b = np.concatenate([bqk[qr], bqk[kr], bv0[64 * c:64 * c + 64]], 0)
        out[c]["wqkv0"] = np.ascontiguousarray(w.T)
        out[c]["wqkv0t"] = _tile_w(w.T, [128, 64])
        out[c]["bqkv0"] = np.ascontiguousarray(b[:, None])

    wq = np.asarray(inp["L1_attn_Wq"], np.float32)
    wk = np.asarray(inp["L1_attn_Wk"], np.float32)
    wv = np.asarray(inp["L1_attn_Wv"], np.float32)
    bq = np.asarray(inp["L1_attn_bq"], np.float32)
    bk = np.asarray(inp["L1_attn_bk"], np.float32)
    bv = np.asarray(inp["L1_attn_bv"], np.float32)
    for c in range(NCORES):
        sl = slice(512 * c, 512 * c + 512)
        w = np.concatenate([wq[sl], wk[sl], wv[sl]], 0)
        b = np.concatenate([bq[sl], bk[sl], bv[sl]], 0)
        out[c]["wqkv1"] = np.ascontiguousarray(w.T)
        out[c]["wqkv1t"] = _tile_w(w.T, [128] * 12)
        out[c]["bqkv1"] = np.ascontiguousarray(b[:, None])

    wo0T = np.ascontiguousarray(np.asarray(inp["L0_attn_Wo_w"], np.float32).T)
    bo0 = np.ascontiguousarray(np.asarray(inp["L0_attn_Wo_b"], np.float32)[None, :])
    wo1T = np.ascontiguousarray(np.asarray(inp["L1_attn_Wo"], np.float32).T)
    bo1 = np.ascontiguousarray(np.asarray(inp["L1_attn_bo"], np.float32)[None, :])
    bf1 = np.ascontiguousarray(np.asarray(inp["L1_ffn_bias"], np.float32)[None, :])
    wo0t = _tile_of(wo0T)
    wo1t = _tile_of(wo1T)
    for c in range(NCORES):
        out[c]["wo0T"] = wo0T
        out[c]["wo0t"] = wo0t
        out[c]["bo0"] = bo0
        out[c]["wo1T"] = wo1T
        out[c]["wo1t"] = wo1t
        out[c]["bo1"] = bo1
        out[c]["bf1"] = bf1

    qw0T = np.asarray(inp["L0_ffn_qweight"], np.int32).reshape(D, D).T
    am0T = np.asarray(inp["L0_ffn_absmax"], np.float32).reshape(D, 64).T
    qw1 = np.asarray(inp["L1_ffn_qweight"], np.int32)
    zp1 = np.ascontiguousarray(np.asarray(inp["L1_ffn_weight_zero_point"], np.float32).reshape(1, D))
    sc1 = np.ascontiguousarray(np.asarray(inp["L1_ffn_weight_scale"], np.float32).reshape(1, D))
    for c in range(NCORES):
        out[c]["qw0"] = np.ascontiguousarray(qw0T[512 * c:512 * c + 512])
        out[c]["am0"] = np.ascontiguousarray(am0T[8 * c:8 * c + 8])
        sl = np.ascontiguousarray(qw1[:, 384 * c:384 * c + 384].T)
        out[c]["qw1"] = np.ascontiguousarray(np.stack([sl[0::3], sl[1::3], sl[2::3]], 0))
        out[c]["zp1"] = zp1
        out[c]["sc1"] = sc1
    return out


def _mk_nc():
    return bass.Bass("TRN2", target_bir_lowering=False)


def build_prep():
    nc = _mk_nc()
    dr = dict(
        qw0=nc.dram_tensor("qw0", [512, 4096], I32, kind="ExternalInput").ap(),
        am0=nc.dram_tensor("am0", [8, 4096], F32, kind="ExternalInput").ap(),
        qw1=nc.dram_tensor("qw1", [3, 128, 4096], I32, kind="ExternalInput").ap(),
        zp1=nc.dram_tensor("zp1", [1, 4096], F32, kind="ExternalInput").ap(),
        sc1=nc.dram_tensor("sc1", [1, 4096], F32, kind="ExternalInput").ap(),
        w0t_loc=nc.dram_tensor("w0t_loc", [512, 4096], BF16, kind="ExternalOutput").ap(),
        w1t_loc=nc.dram_tensor("w1t_loc", [512, 4096], BF16, kind="ExternalOutput").ap(),
    )
    with contextlib.ExitStack() as es:
        ctx = Ctx(nc, es)
        with nc.Block() as block:
            emit_prep(ctx, block, nc, es, dr)
    return nc


def build_p1():
    nc = _mk_nc()
    xsrc = nc.dram_tensor("xsrc", [D, TPC], F32, kind="ExternalInput").ap()
    xn0 = nc.dram_tensor("xn0", [D, 512], BF16, kind="ExternalOutput").ap()
    xn1 = nc.dram_tensor("xn1", [D, 512], BF16, kind="ExternalOutput").ap()
    with contextlib.ExitStack() as es:
        ctx = Ctx(nc, es)
        bufs = alloc_init_bufs(nc, es, False)
        with nc.Block() as block:
            emit_init(ctx, block, bufs, False)
            emit_p1(ctx, block, nc, es, bufs, xsrc, (xn0, xn1))
    return nc


def build_p2(layer):
    nc = _mk_nc()
    OF = 192 if layer == 0 else 1536
    of_tiles = [(0, 128), (128, 64)] if layer == 0 else [(i * 128, 128) for i in range(12)]
    wqkv = nc.dram_tensor("wqkv", [128, 32 * OF], F32, kind="ExternalInput").ap()
    bqkv = nc.dram_tensor("bqkv", [OF, 1], F32, kind="ExternalInput").ap()
    xn_g0 = nc.dram_tensor("xn_g0", [NCORES * D, 512], BF16, kind="ExternalInput").ap()
    xn_g1 = nc.dram_tensor("xn_g1", [NCORES * D, 512], BF16, kind="ExternalInput").ap()
    qkv_loc = nc.dram_tensor("qkv_loc", [OF, TOK], BF16, kind="ExternalOutput").ap()
    with contextlib.ExitStack() as es:
        ctx = Ctx(nc, es)
        bufs = alloc_init_bufs(nc, es, False)
        with nc.Block() as block:
            emit_init(ctx, block, bufs, False)
            emit_p2(ctx, block, nc, es, bufs, of_tiles, wqkv, bqkv, (xn_g0, xn_g1), qkv_loc)
    return nc


def build_p3(layer):
    nc = _mk_nc()
    OF = 192 if layer == 0 else 1536
    AOF = 64 if layer == 0 else 512
    qkv_loc = nc.dram_tensor("qkv_loc", [OF, TOK], BF16, kind="ExternalInput").ap()
    ao_loc = nc.dram_tensor("ao_loc", [8 * AOF, TPC], BF16, kind="ExternalOutput").ap()
    with contextlib.ExitStack() as es:
        ctx = Ctx(nc, es)
        bufs = alloc_init_bufs(nc, es, layer == 1)
        with nc.Block() as block:
            emit_init(ctx, block, bufs, layer == 1)
            emit_p3(ctx, block, nc, es, bufs, layer, qkv_loc, ao_loc)
    return nc


def build_p4(layer):
    nc = _mk_nc()
    AOF = 512 if layer == 0 else 4096
    nbr = 1 if layer == 0 else 2
    na = AOF // 128
    ao = nc.dram_tensor("ao", [AOF, TPC], BF16, kind="ExternalInput").ap()
    xn0 = nc.dram_tensor("xn0", [D, 512], BF16, kind="ExternalInput").ap()
    xn1 = nc.dram_tensor("xn1", [D, 512], BF16, kind="ExternalInput").ap()
    woT = nc.dram_tensor("woT", [128, 32 * na * 128], F32, kind="ExternalInput").ap()
    wfT = nc.dram_tensor("wfT", [D, D], BF16, kind="ExternalInput").ap()
    brows = [nc.dram_tensor(f"br{i}", [1, D], F32, kind="ExternalInput").ap()
             for i in range(nbr)]
    xsrc = nc.dram_tensor("xsrc", [D, TPC], F32, kind="ExternalInput").ap()
    xdst = nc.dram_tensor("xdst", [D, TPC], F32, kind="ExternalOutput").ap()
    with contextlib.ExitStack() as es:
        ctx = Ctx(nc, es)
        bufs = alloc_init_bufs(nc, es, False)
        with nc.Block() as block:
            emit_init(ctx, block, bufs, False)
            emit_p4(ctx, block, nc, es, bufs, AOF, ao, (xn0, xn1), woT, wfT, brows,
                    xsrc, xdst, ao_cols_dyn=False)
    return nc


_NC_CACHE = {}


def _get(name, fn, *a):
    key = (name,) + a
    if key not in _NC_CACHE:
        _NC_CACHE[key] = fn(*a)
    return _NC_CACHE[key]


def _run(nc, in_maps):
    return run_bass_kernel_spmd(nc, in_maps, core_ids=list(range(NCORES))).results


def kernel_unfused(**inputs):
    pc = host_prep(inputs)

    nc = _get("prep", build_prep)
    res = _run(nc, [{k: pc[c][k] for k in ["qw0", "am0", "qw1", "zp1", "sc1"]}
                    for c in range(NCORES)])
    w0t = np.concatenate([res[c]["w0t_loc"] for c in range(NCORES)], 0)
    w1t = np.concatenate([res[c]["w1t_loc"] for c in range(NCORES)], 0)

    xcur = [pc[c]["x0"] for c in range(NCORES)]
    for layer in range(2):
        nc = _get("p1", build_p1)
        res = _run(nc, [{"xsrc": xcur[c]} for c in range(NCORES)])
        xn0 = [res[c]["xn0"] for c in range(NCORES)]
        xn1 = [res[c]["xn1"] for c in range(NCORES)]
        xn_g0 = np.concatenate(xn0, 0)
        xn_g1 = np.concatenate(xn1, 0)

        nc = _get("p2", build_p2, layer)
        res = _run(nc, [{"wqkv": pc[c][f"wqkv{layer}t"], "bqkv": pc[c][f"bqkv{layer}"],
                         "xn_g0": xn_g0, "xn_g1": xn_g1} for c in range(NCORES)])
        qkv = [res[c]["qkv_loc"] for c in range(NCORES)]

        nc = _get("p3", build_p3, layer)
        res = _run(nc, [{"qkv_loc": qkv[c]} for c in range(NCORES)])
        AOFloc = 64 if layer == 0 else 512
        ao_per_core = [np.concatenate([res[i]["ao_loc"][c * AOFloc:(c + 1) * AOFloc]
                                       for i in range(NCORES)], 0) for c in range(NCORES)]

        nc = _get("p4", build_p4, layer)
        wf = w0t if layer == 0 else w1t
        in_maps = []
        for c in range(NCORES):
            m = {"ao": ao_per_core[c],
                 "xn0": xn0[c], "xn1": xn1[c], "woT": pc[c][f"wo{layer}t"], "wfT": wf,
                 "xsrc": xcur[c]}
            if layer == 0:
                m["br0"] = pc[c]["bo0"]
            else:
                m["br0"] = pc[c]["bo1"]
                m["br1"] = pc[c]["bf1"]
            in_maps.append(m)
        res = _run(nc, in_maps)
        xcur = [res[c]["xdst"] for c in range(NCORES)]

    full = np.empty((TOK, D), np.float32)
    for c in range(NCORES):
        full[TPC * c:TPC * (c + 1)] = xcur[c].T
    return full.reshape(2, SEQ, D)


def build_fused():
    nc = _mk_nc()
    ei, eo = "ExternalInput", "ExternalOutput"
    dr = {}
    dr["x0"] = nc.dram_tensor("x0", [D, TPC], F32, kind=ei).ap()
    dr["wqkv0"] = nc.dram_tensor("wqkv0", [128, 32 * 192], F32, kind=ei).ap()
    dr["bqkv0"] = nc.dram_tensor("bqkv0", [192, 1], F32, kind=ei).ap()
    dr["wqkv1"] = nc.dram_tensor("wqkv1", [128, 32 * 1536], F32, kind=ei).ap()
    dr["bqkv1"] = nc.dram_tensor("bqkv1", [1536, 1], F32, kind=ei).ap()
    dr["wo0t"] = nc.dram_tensor("wo0t", [128, 32 * 4 * 128], F32, kind=ei).ap()
    dr["wo1t"] = nc.dram_tensor("wo1t", [128, 32 * 32 * 128], F32, kind=ei).ap()
    dr["bo0"] = nc.dram_tensor("bo0", [1, D], F32, kind=ei).ap()
    dr["bo1"] = nc.dram_tensor("bo1", [1, D], F32, kind=ei).ap()
    dr["bf1"] = nc.dram_tensor("bf1", [1, D], F32, kind=ei).ap()
    dr["qw0"] = nc.dram_tensor("qw0", [512, 4096], I32, kind=ei).ap()
    dr["am0"] = nc.dram_tensor("am0", [8, 4096], F32, kind=ei).ap()
    dr["qw1"] = nc.dram_tensor("qw1", [3, 128, 4096], I32, kind=ei).ap()
    dr["zp1"] = nc.dram_tensor("zp1", [1, 4096], F32, kind=ei).ap()
    dr["sc1"] = nc.dram_tensor("sc1", [1, 4096], F32, kind=ei).ap()
    dr["xout"] = nc.dram_tensor("xout", [D, TPC], F32, kind=eo).ap()
    dr["w0t_loc"] = nc.dram_tensor("w0t_loc", [512, 4096], BF16).ap()
    dr["w1t_loc"] = nc.dram_tensor("w1t_loc", [512, 4096], BF16).ap()
    dr["w0t"] = nc.dram_tensor("w0t", [D, D], BF16).ap()
    dr["w1t"] = nc.dram_tensor("w1t", [D, D], BF16).ap()
    dr["xn_loc0"] = nc.dram_tensor("xn_loc0", [D, 512], BF16).ap()
    dr["xn_loc1"] = nc.dram_tensor("xn_loc1", [D, 512], BF16).ap()
    dr["xn_g0"] = nc.dram_tensor("xn_g0", [NCORES * D, 512], BF16).ap()
    dr["xn_g1"] = nc.dram_tensor("xn_g1", [NCORES * D, 512], BF16).ap()
    dr["qkv0"] = nc.dram_tensor("qkv0", [192, TOK], BF16).ap()
    dr["qkv1"] = nc.dram_tensor("qkv1", [1536, TOK], BF16).ap()
    dr["ao0_loc"] = nc.dram_tensor("ao0_loc", [512, TPC], BF16).ap()
    dr["ao0_g"] = nc.dram_tensor("ao0_g", [512, TPC], BF16).ap()
    dr["ao1_loc"] = nc.dram_tensor("ao1_loc", [4096, TPC], BF16).ap()
    dr["ao1_g"] = nc.dram_tensor("ao1_g", [4096, TPC], BF16).ap()
    dr["xmid"] = nc.dram_tensor("xmid", [D, TPC], F32).ap()

    of0 = [(0, 128), (128, 64)]
    of1 = [(i * 128, 128) for i in range(12)]
    rg = [list(range(NCORES))]

    with contextlib.ExitStack() as es:
        ctx = Ctx(nc, es)
        bufs = alloc_init_bufs(nc, es, False)
        with nc.Block() as block:
            emit_init(ctx, block, bufs, False, dyn_src=dr["xn_loc0"])
            es2a = contextlib.ExitStack()
            prew0 = emit_p2_wload(ctx, block, nc, es2a, of0,
                                  dr["wqkv0"], dr["bqkv0"], tag="2a")
            esp = contextlib.ExitStack()
            prep = PrepV(ctx, nc, esp, dr, rg, "p1st1a")
            prep.emit_loads(block)
            PREPX = ("ccw", "ppl", "ppst", "ppv", "prep_done")

            for layer, tg in ((0, "a"), (1, "b")):
                xsrc = dr["x0"] if layer == 0 else dr["xmid"]
                xdst = dr["xmid"] if layer == 0 else dr["xout"]
                qkv = dr[f"qkv{layer}"]
                aoloc = dr[f"ao{layer}_loc"]
                aog = dr[f"ao{layer}_g"]
                with contextlib.ExitStack() as es1:
                    emit_p1(ctx, block, nc, es1, bufs, xsrc,
                            (dr["xn_loc0"], dr["xn_loc1"]), tag="1" + tg)
                    if layer == 0:
                        prep.emit_rest(block)
                    else:
                        def _gx(g):
                            for hh in range(2):
                                ctx.wait(g, f"p1st1{tg}", 64 * (hh + 1))
                                ctx.inc("ccx" + tg, g.collective_compute(
                                    "AllGather", ALU.bypass, replica_groups=rg,
                                    ins=[dr[f"xn_loc{hh}"][:, :]],
                                    outs=[dr[f"xn_g{hh}"][:, :]]), 1)
                        block.gpsimd(_gx)
                    emit_barrier(ctx, block, exclude=PREPX + ("ccx" + tg,))
                with contextlib.ExitStack() as es2:
                    emit_p2(ctx, block, nc, es2, bufs,
                            of0 if layer == 0 else of1,
                            dr[f"wqkv{layer}"], dr[f"bqkv{layer}"],
                            (dr["xn_g0"], dr["xn_g1"]), qkv, tag="2" + tg,
                            ccx_name="ccx" + tg,
                            prew=prew0 if layer == 0 else None)
                    if layer == 0:
                        emit_barrier(ctx, block, exclude=("ccw",))
                    else:
                        emit_barrier(ctx, block, exclude=PREPX)
                if layer == 0:
                    prep.emit_ccw(block)
                    esp.close()
                    es2a.close()
                with contextlib.ExitStack() as es3:
                    emit_p3(ctx, block, nc, es3, bufs, layer, qkv, aoloc, tag="3" + tg)
                    emit_barrier(ctx, block, exclude=PREPX)
                def _ga(g):
                    ctx.inc("cca" + tg, g.collective_compute(
                        "AllToAll", ALU.bypass, replica_groups=rg,
                        ins=[aoloc[:, :]], outs=[aog[:, :]]), 1)
                block.gpsimd(_ga)
                emit_barrier(ctx, block)
                with contextlib.ExitStack() as es4:
                    brows = [dr["bo0"]] if layer == 0 else [dr["bo1"], dr["bf1"]]
                    emit_p4(ctx, block, nc, es4, bufs, 512 if layer == 0 else 4096,
                            aog, (dr["xn_loc0"], dr["xn_loc1"]), dr[f"wo{layer}t"],
                            dr[f"w{layer}t"], brows, xsrc, xdst,
                            tag="4" + tg, ao_cols_dyn=False)
                    if layer == 0:
                        emit_barrier(ctx, block)
    return nc


def kernel(**inputs):
    pc = host_prep(inputs)
    nc = _get("fused", build_fused)
    keys = ["x0", "bqkv0", "bqkv1", "bo0", "bo1", "bf1", "qw0", "am0",
            "qw1", "zp1", "sc1"]
    in_maps = []
    for c in range(NCORES):
        m = {k: pc[c][k] for k in keys}
        m["wqkv0"] = pc[c]["wqkv0t"]
        m["wqkv1"] = pc[c]["wqkv1t"]
        m["wo0t"] = pc[c]["wo0t"]
        m["wo1t"] = pc[c]["wo1t"]
        in_maps.append(m)
    res = _run(nc, in_maps)
    full = np.empty((TOK, D), np.float32)
    for c in range(NCORES):
        full[TPC * c:TPC * (c + 1)] = res[c]["xout"].T
    return full.reshape(2, SEQ, D)


# version 20
# speedup vs baseline: 1.0132x; 1.0132x over previous
import contextlib
import math
import os
os.environ.setdefault("TRN_ALLOW_ICI_ALL_TO_ALL", "1")
import numpy as np
import ml_dtypes
import concourse.bass as bass
import concourse.mybir as mybir
from concourse.bass_utils import run_bass_kernel_spmd

F32 = mybir.dt.float32
BF16 = mybir.dt.bfloat16
I32 = mybir.dt.int32
AF = mybir.ActivationFunctionType
ALU = mybir.AluOpType
f32 = np.float32

NCORES = 8
D = 4096
TOK = 8192
TPC = TOK // NCORES
SEQ = 4096

NF4 = [-1.0, -0.6961928009986877, -0.5250730514526367, -0.39491748809814453,
       -0.28444138169288635, -0.18477343022823334, -0.09105003625154495, 0.0,
       0.07958029955625534, 0.16093020141124725, 0.24611230194568634,
       0.33791524171829224, 0.44070982933044434, 0.5626170039176941,
       0.7229568362236023, 1.0]

M0 = 75.0
M1 = 12.0
LN_EPS = 1e-5


class Ctx:

    def __init__(self, nc, es):
        self.nc = nc
        self.es = es
        self.hw = []
        self.hwcnt = []
        self.free = []
        self.bind = {}
        self.active = set()
        self.persistent = set(["init", "bar"])
        self.nbar = 0

    def _hw_new(self):
        h = self.es.enter_context(self.nc.semaphore(f"s{len(self.hw)}"))
        self.hw.append(h)
        self.hwcnt.append(0)
        return len(self.hw) - 1

    def _bind(self, name):
        if name not in self.bind:
            idx = self.free.pop() if self.free else self._hw_new()
            self.bind[name] = (idx, self.hwcnt[idx])
            self.active.add(name)
        return self.bind[name]

    def sem(self, name):
        return self.hw[self._bind(name)[0]]

    def inc(self, name, inst, by=1):
        idx, base = self._bind(name)
        inst.then_inc(self.hw[idx], by)
        self.hwcnt[idx] += by
        return inst

    def wait(self, eng, name, value):
        if value > 0:
            idx, base = self._bind(name)
            eng.wait_ge(self.hw[idx], base + value)

    def retire_phase(self):
        for name in list(self.active):
            if name in self.persistent or name.startswith("cc"):
                continue
            self.active.discard(name)
            idx, _ = self.bind.pop(name)
            self.free.append(idx)


def emit_barrier(ctx, block, exclude=()):
    ctx.nbar += 1
    bidx, bbase = ctx._bind("bar")
    k = ctx.hwcnt[bidx] + 1
    excl_idx = set()
    for name in exclude:
        if name in ctx.bind:
            excl_idx.add(ctx.bind[name][0])
    excl_idx.add(bidx)

    def _s(s):
        for idx in range(len(ctx.hw)):
            if idx in excl_idx:
                continue
            if ctx.hwcnt[idx] > 0:
                s.wait_ge(ctx.hw[idx], ctx.hwcnt[idx])
        s.sem_inc(ctx.hw[bidx], 1)
    block.sync(_s)
    ctx.hwcnt[bidx] = k
    for emit in (block.scalar, block.tensor, block.vector, block.gpsimd):
        def _w(e):
            e.wait_ge(ctx.hw[bidx], k)
        emit(_w)
    ctx.retire_phase()


def alloc_init_bufs(nc, es, with_masks):
    bufs = {}
    bufs["ones_col_f32"] = es.enter_context(nc.sbuf_tensor("ones_col_f32", [128, 1], F32))
    bufs["ones_row_f32"] = es.enter_context(nc.sbuf_tensor("ones_row_f32", [1, 128], F32))
    bufs["ones_row_bf"] = es.enter_context(nc.sbuf_tensor("ones_row_bf", [1, 512], BF16))
    bufs["ones_k_bf"] = es.enter_context(nc.sbuf_tensor("ones_k_bf", [128, 128], BF16))
    bufs["ones_k_f32"] = es.enter_context(nc.sbuf_tensor("ones_k_f32", [128, 128], F32))
    bufs["ident"] = es.enter_context(nc.sbuf_tensor("ident", [128, 128], BF16))
    bufs["negident"] = es.enter_context(nc.sbuf_tensor("negident", [128, 128], BF16))
    if with_masks:
        bufs["masks"] = [es.enter_context(nc.sbuf_tensor(f"mask{p}", [128, 512], BF16))
                         for p in range(4)]
    return bufs


def emit_init(ctx, block, bufs, with_masks, dyn_src=None):
    if dyn_src is not None:
        def _s(s):
            ctx.pid = s.partition_id()
            ctx.pid_off = ctx.pid * TPC
            ctx.inc("init0", s.dma_start(out=bufs["ident"][0:1, 0:16],
                                         in_=dyn_src[0:1, bass.ts(ctx.pid, 16)]), 16)
        block.sync(_s)

    def _(g):
        if dyn_src is not None:
            ctx.wait(g, "init0", 16)
        g.memset(bufs["ones_col_f32"][:, :], 1.0)
        g.memset(bufs["ones_row_f32"][:, :], 1.0)
        g.memset(bufs["ones_row_bf"][:, :], 1.0)
        g.memset(bufs["ones_k_bf"][:, :], 1.0)
        g.memset(bufs["ones_k_f32"][:, :], 1.0)
        g.memset(bufs["ident"][:, :], 0.0)
        g.memset(bufs["negident"][:, :], 0.0)
        g.drain()
        g.affine_select(out=bufs["ident"][:, :], in_=bufs["ident"][:, :],
                        compare_op=ALU.not_equal, fill=1.0, base=0,
                        pattern=[[-1, 128]], channel_multiplier=1)
        g.affine_select(out=bufs["negident"][:, :], in_=bufs["negident"][:, :],
                        compare_op=ALU.not_equal, fill=-1.0, base=0,
                        pattern=[[-1, 128]], channel_multiplier=1)
        if with_masks:
            for p in range(4):
                g.memset(bufs["masks"][p][:, :], 1.0)
            g.drain()
            for p in range(4):
                g.affine_select(out=bufs["masks"][p][:, :], in_=bufs["masks"][p][:, :],
                                compare_op=ALU.is_ge, fill=0.0, base=-128 * p,
                                pattern=[[1, 512]], channel_multiplier=-1)
        g.drain()
        ctx.inc("init", g.nop(), 1)
    block.gpsimd(_)

    def _sc(sc):
        ctx.wait(sc, "init", 1)
        sc.activation(bufs["ones_col_f32"][0:1, 0:1], bufs["ones_col_f32"][0:1, 0:1],
                      AF.Square)
        sc.drain()
    block.scalar(_sc)


class PrepV:
    def __init__(self, ctx, nc, es, dr, rg, stsem):
        self.ctx, self.nc, self.dr, self.rg, self.stsem = ctx, nc, dr, rg, stsem
        self.qwb = [es.enter_context(nc.sbuf_tensor(f"pv_qw{i}", [128, 4096], I32))
                    for i in range(2)]
        self.qf = es.enter_context(nc.sbuf_tensor("pv_qf", [128, 4096], F32))
        self.acc = es.enter_context(nc.sbuf_tensor("pv_acc", [128, 4096], F32))
        self.tmp = es.enter_context(nc.sbuf_tensor("pv_tmp", [128, 4096], F32))
        self.amb = es.enter_context(nc.sbuf_tensor("pv_amb", [128, 4096], F32))
        self.wout = es.enter_context(nc.sbuf_tensor("pv_w", [128, 4096], BF16))
        ctx.persistent.update({"ppl", "ppst", "ppv", "prep_done", stsem})
        self.nl = 0
        self.qtgt, self.atgt, self.ptgt = {}, {}, {}
        self.ztgt = self.stgt = None

    def _ldq(self, eng, t):
        self.ctx.inc("ppl", eng.dma_start(out=self.qwb[t % 2][:, :],
                     in_=self.dr["qw0"][128 * t:128 * (t + 1), :]), 16)
        self.nl += 16
        self.qtgt[t] = self.nl

    def _lda(self, eng, t):
        for h in range(2):
            self.ctx.inc("ppl", eng.dma_start(
                out=self.amb[64 * h:64 * (h + 1), :],
                in_=self.dr["am0"][2 * t + h:2 * t + h + 1, :].broadcast_to([64, 4096])), 16)
        self.nl += 32
        self.atgt[t] = self.nl

    def emit_loads(self, block):
        def _s(s):
            self._ldq(s, 0)
            self._ldq(s, 1)
            self._lda(s, 0)
        block.sync(_s)

    def emit_rest(self, block):
        ctx, dr = self.ctx, self.dr
        specs = [(0, False), (0, True), (1, False), (2, False)]

        def _sc(sc):
            def ldp(jj):
                ctx.inc("ppl", sc.dma_start(out=self.qwb[jj % 2][:, :],
                         in_=dr["qw1"][specs[jj][0], :, :]), 16)
                self.nl += 16
                self.ptgt[jj] = self.nl
            for k in range(1, 9):
                ctx.wait(sc, "ppv", k)
                if k <= 2:
                    self._ldq(sc, k + 1)
                if k <= 3:
                    self._lda(sc, k)
                if k == 3:
                    ldp(0)
                if k == 4:
                    ldp(1)
                    ctx.inc("ppl", sc.dma_start(out=self.amb[:, :],
                             in_=dr["zp1"].broadcast_to([128, 4096])), 16)
                    self.nl += 16
                    self.ztgt = self.nl
                    ctx.inc("ppl", sc.dma_start(out=self.tmp[:, :],
                             in_=dr["sc1"].broadcast_to([128, 4096])), 16)
                    self.nl += 16
                    self.stgt = self.nl
                if k == 5:
                    ldp(2)
                if k == 6:
                    ldp(3)
                if k <= 4:
                    t = k - 1
                    ctx.inc("ppst", sc.dma_start(
                        out=dr["w0t_loc"][128 * t:128 * (t + 1), :],
                        in_=self.wout[:, :]), 16)
                else:
                    jj = k - 5
                    dst = dr["w1t_loc"].rearrange("(g j) n -> j g n", j=4)[jj]
                    ctx.inc("ppst", sc.dma_start(out=dst, in_=self.wout[:, :]), 16)
        block.scalar(_sc)

        nf4 = [(float(j), float(c)) for j, c in enumerate(NF4) if c != 0.0]

        def _v(v):
            for t in range(4):
                ctx.wait(v, "ppl", self.qtgt[t])
                b = t % 2
                v.tensor_copy(self.qf[:, :], self.qwb[b][:, :])
                v.drain()
                first = True
                for j, c in nf4:
                    dstp = self.acc if first else self.tmp
                    v.tensor_scalar(dstp[:, :], self.qf[:, :], j, c,
                                    ALU.is_equal, ALU.mult)
                    v.drain()
                    if not first:
                        v.tensor_tensor(self.acc[:, :], self.acc[:, :],
                                        self.tmp[:, :], ALU.add)
                        v.drain()
                    first = False
                ctx.wait(v, "ppl", self.atgt[t])
                if t >= 1:
                    ctx.wait(v, "ppst", 16 * t)
                i = v.tensor_tensor(self.wout[:, :], self.acc[:, :],
                                    self.amb[:, :], ALU.mult)
                v.drain()
                ctx.inc("ppv", i, 1)
            for jj, (d, shift) in enumerate(specs):
                ctx.wait(v, "ppl", self.ptgt[jj])
                if jj == 0:
                    ctx.wait(v, "ppl", self.stgt)
                b = jj % 2
                if shift:
                    v.tensor_scalar(self.qwb[b][:, :], self.qwb[b][:, :], 6, 63,
                                    ALU.logical_shift_right, ALU.bitwise_and)
                else:
                    v.tensor_single_scalar(self.qwb[b][:, :], self.qwb[b][:, :],
                                           63, ALU.bitwise_and)
                v.drain()
                v.tensor_copy(self.qf[:, :], self.qwb[b][:, :])
                v.drain()
                v.tensor_tensor(self.qf[:, :], self.qf[:, :], self.amb[:, :],
                                ALU.subtract)
                v.drain()
                ctx.wait(v, "ppst", 16 * (4 + jj))
                i = v.tensor_tensor(self.wout[:, :], self.qf[:, :],
                                    self.tmp[:, :], ALU.mult)
                v.drain()
                ctx.inc("ppv", i, 1)
        block.vector(_v)

        def _g(g):
            ctx.wait(g, self.stsem, 64)
            ctx.inc("ccxa", g.collective_compute(
                "AllGather", ALU.bypass, replica_groups=self.rg,
                ins=[dr["xn_loc0"][:, :]], outs=[dr["xn_g0"][:, :]]), 1)
            ctx.wait(g, self.stsem, 128)
            ctx.inc("ccxa", g.collective_compute(
                "AllGather", ALU.bypass, replica_groups=self.rg,
                ins=[dr["xn_loc1"][:, :]], outs=[dr["xn_g1"][:, :]]), 1)
            ctx.wait(g, "ppst", 16 * 8)
            ctx.inc("prep_done", g.nop(), 1)
        block.gpsimd(_g)

    def emit_ccw(self, block):
        ctx, dr = self.ctx, self.dr

        def _g(g):
            ctx.wait(g, "ppst", 16 * 8)
            ctx.inc("ccw", g.collective_compute(
                "AllGather", ALU.bypass, replica_groups=self.rg,
                ins=[dr["w0t_loc"][:, :]], outs=[dr["w0t"][:, :]]), 1)
            ctx.inc("ccw", g.collective_compute(
                "AllGather", ALU.bypass, replica_groups=self.rg,
                ins=[dr["w1t_loc"][:, :]], outs=[dr["w1t"][:, :]]), 1)
        block.gpsimd(_g)


def emit_prep(ctx, block, nc, es, dr, tag=""):
    qw = es.enter_context(nc.sbuf_tensor(f"pp_qw{tag}", [128, 4096], I32))
    qf = es.enter_context(nc.sbuf_tensor(f"pp_qf{tag}", [128, 4096], F32))
    acc = es.enter_context(nc.sbuf_tensor(f"pp_acc{tag}", [128, 4096], F32))
    tmp = es.enter_context(nc.sbuf_tensor(f"pp_tmp{tag}", [128, 4096], F32))
    amb = es.enter_context(nc.sbuf_tensor(f"pp_amb{tag}", [128, 4096], F32))
    wout = es.enter_context(nc.sbuf_tensor(f"pp_w{tag}", [128, 4096], BF16))
    q3 = [es.enter_context(nc.sbuf_tensor(f"pp_q3_{d}{tag}", [128, 4096], I32)) for d in range(3)]
    zpb = es.enter_context(nc.sbuf_tensor(f"pp_zpb{tag}", [128, 4096], F32))
    scb = es.enter_context(nc.sbuf_tensor(f"pp_scb{tag}", [128, 4096], F32))
    u = es.enter_context(nc.sbuf_tensor(f"pp_u{tag}", [128, 4096], I32))
    uf = es.enter_context(nc.sbuf_tensor(f"pp_uf{tag}", [128, 4096], F32))
    ppl, ppv, ppst = f"ppl{tag}", f"ppv{tag}", f"ppst{tag}"

    def _sync(s):
        for t in range(4):
            if t > 0:
                ctx.wait(s, ppv, t)
            ctx.inc(ppl, s.dma_start(out=qw[:, :],
                     in_=dr["qw0"][128 * t:128 * t + 128, :]), 16)
            for half in range(2):
                ctx.inc(ppl, s.dma_start(
                    out=amb[64 * half:64 * half + 64, :],
                    in_=dr["am0"][2 * t + half:2 * t + half + 1, :].broadcast_to([64, 4096])), 16)
            ctx.wait(s, ppv, t + 1)
            ctx.inc(ppst, s.dma_start(out=dr["w0t_loc"][128 * t:128 * t + 128, :],
                                      in_=wout[:, :]), 16)
        for d in range(3):
            ctx.inc(ppl, s.dma_start(out=q3[d][:, :], in_=dr["qw1"][d, :, :]), 16)
        ctx.inc(ppl, s.dma_start(out=zpb[:, :], in_=dr["zp1"].broadcast_to([128, 4096])), 16)
        ctx.inc(ppl, s.dma_start(out=scb[:, :], in_=dr["sc1"].broadcast_to([128, 4096])), 16)
        for j in range(4):
            ctx.wait(s, ppv, 5 + j)
            dst = dr["w1t_loc"].rearrange("(g j) n -> j g n", j=4)[j]
            ctx.inc(ppst, s.dma_start(out=dst, in_=wout[:, :]), 16)
        ctx.wait(s, ppst, 16 * 8)
        ctx.inc(f"prep_done{tag}", s.nop(), 1)
    block.sync(_sync)

    def _vec(v):
        for t in range(4):
            ctx.wait(v, ppl, 48 * (t + 1))
            if t > 0:
                ctx.wait(v, ppst, 16 * t)
            v.tensor_copy(qf[:, :], qw[:, :])
            v.drain()
            first = True
            for j, c in enumerate(NF4):
                if c == 0.0:
                    continue
                dstp = acc if first else tmp
                v.tensor_scalar(dstp[:, :], qf[:, :], float(j), float(c),
                                ALU.is_equal, ALU.mult)
                v.drain()
                if not first:
                    v.tensor_tensor(acc[:, :], acc[:, :], tmp[:, :], ALU.add)
                    v.drain()
                first = False
            i = v.tensor_tensor(wout[:, :], acc[:, :], amb[:, :], ALU.mult)
            ctx.inc(ppv, i, 1)
            v.drain()
        ctx.wait(v, ppl, 192 + 80)
        specs = [(0, False), (0, True), (1, False), (2, False)]
        for j, (d, shift) in enumerate(specs):
            ctx.wait(v, ppst, 16 * (4 + j))
            if shift:
                v.tensor_scalar(u[:, :], q3[d][:, :], 6, 63,
                                ALU.logical_shift_right, ALU.bitwise_and)
            else:
                v.tensor_single_scalar(u[:, :], q3[d][:, :], 63, ALU.bitwise_and)
            v.drain()
            v.tensor_copy(uf[:, :], u[:, :])
            v.drain()
            v.tensor_tensor(uf[:, :], uf[:, :], zpb[:, :], ALU.subtract)
            v.drain()
            i = v.tensor_tensor(wout[:, :], uf[:, :], scb[:, :], ALU.mult)
            ctx.inc(ppv, i, 1)
            v.drain()
    block.vector(_vec)


def emit_p1(ctx, block, nc, es, bufs, xsrc, xn_locs, tag=""):
    NCH, CT = 8, 128
    xc = [es.enter_context(nc.sbuf_tensor(f"p1_x{i}{tag}", [128, 32 * CT], F32))
          for i in range(2)]
    sq = es.enter_context(nc.sbuf_tensor(f"p1_sq{tag}", [128, 32 * CT], F32))
    xn = es.enter_context(nc.sbuf_tensor(f"p1_xn{tag}", [128, 32 * CT], BF16))
    srow = es.enter_context(nc.sbuf_tensor(f"p1_srow{tag}", [1, CT], F32))
    vrow = es.enter_context(nc.sbuf_tensor(f"p1_vrow{tag}", [1, CT], F32))
    lnrow = es.enter_context(nc.sbuf_tensor(f"p1_lnrow{tag}", [1, CT], F32))
    rrow = es.enter_context(nc.sbuf_tensor(f"p1_rrow{tag}", [1, CT], F32))
    mrow = es.enter_context(nc.sbuf_tensor(f"p1_mrow{tag}", [1, CT], F32))
    rbs = es.enter_context(nc.sbuf_tensor(f"p1_rbs{tag}", [128, CT], F32))
    mbs = es.enter_context(nc.sbuf_tensor(f"p1_mbs{tag}", [128, CT], F32))
    ps_s = es.enter_context(nc.psum_tensor(f"p1_pss{tag}", [1, CT], F32))
    ps_q = es.enter_context(nc.psum_tensor(f"p1_psq{tag}", [1, CT], F32))
    ps_rb = es.enter_context(nc.psum_tensor(f"p1_psrb{tag}", [128, CT], F32))
    ps_mb = es.enter_context(nc.psum_tensor(f"p1_psmb{tag}", [128, CT], F32))
    ld, sqs, mms, var, rst, mr, bc, rbc, ap, st = [f"p1{n}{tag}" for n in
        ["ld", "sq", "mm", "var", "rst", "mr", "bc", "rbc", "ap", "st"]]
    stname = "p1st" + tag
    xcv = [b.rearrange("p (t c) -> p t c", t=32) for b in xc]
    sqv = sq.rearrange("p (t c) -> p t c", t=32)
    xnv = xn.rearrange("p (t c) -> p t c", t=32)

    def bcast(b):
        return b.rearrange("p (o c) -> p o c", o=1).broadcast_to([128, 32, CT])

    def _sync(s):
        for c in range(NCH):
            if c >= 2:
                ctx.wait(s, ap, c - 1)
            src = xsrc[:, CT * c:CT * (c + 1)].rearrange("(t p) c -> p t c", p=128)
            ctx.inc(ld, s.dma_start(out=xcv[c % 2], in_=src), 16)
            if c >= 1:
                ctx.wait(s, ap, c)
                h, sub = divmod(c - 1, NCH // 2)
                dst = xn_locs[h][:, CT * sub:CT * (sub + 1)] \
                    .rearrange("(t p) c -> p t c", p=128)
                ctx.inc(stname, s.dma_start(out=dst, in_=xnv), 16)
        ctx.wait(s, ap, NCH)
        h, sub = divmod(NCH - 1, NCH // 2)
        dst = xn_locs[h][:, CT * sub:CT * (sub + 1)].rearrange("(t p) c -> p t c", p=128)
        ctx.inc(stname, s.dma_start(out=dst, in_=xnv), 16)
        ctx.wait(s, stname, 16 * NCH)
        ctx.inc(f"p1_done{tag}", s.nop(), 1)
    block.sync(_sync)

    def _scalar(sc):
        for c in range(NCH):
            ctx.wait(sc, ld, 16 * (c + 1))
            if c >= 1:
                ctx.wait(sc, mms, c)
            i = sc.activation(sq[:, :], xc[c % 2][:, :], AF.Square)
            ctx.inc(sqs, i, 1)
            ctx.wait(sc, var, c + 1)
            if c >= 1:
                ctx.wait(sc, bc, c)
            sc.activation(lnrow[:, :], vrow[:, :], AF.Ln, bias=f32(LN_EPS))
            sc.drain()
            i = sc.activation(rrow[:, :], lnrow[:, :], AF.Exp, scale=f32(-0.5))
            sc.drain()
            ctx.inc(rst, i, 1)
    block.scalar(_scalar)

    def _tensor(t):
        ctx.wait(t, "init", 1)
        for c in range(NCH):
            ctx.wait(t, ld, 16 * (c + 1))
            ctx.wait(t, sqs, c + 1)
            if c >= 1:
                ctx.wait(t, var, c)
            for f in range(32):
                t.matmul(ps_s[:, :], bufs["ones_col_f32"][:, :], xcv[c % 2][:, f, :],
                         start=(f == 0), stop=(f == 31))
            for f in range(32):
                i = t.matmul(ps_q[:, :], bufs["ones_col_f32"][:, :], sqv[:, f, :],
                             start=(f == 0), stop=(f == 31))
            ctx.inc(mms, i, 1)
            ctx.wait(t, mr, c + 1)
            if c >= 1:
                ctx.wait(t, rbc, c)
            t.matmul(ps_rb[:, :], bufs["ones_row_f32"][:, :], rrow[:, :],
                     start=True, stop=True)
            i = t.matmul(ps_mb[:, :], bufs["ones_row_f32"][:, :], mrow[:, :],
                         start=True, stop=True)
            ctx.inc(bc, i, 1)
    block.tensor(_tensor)

    def _vec(v):
        for c in range(NCH):
            ctx.wait(v, mms, c + 1)
            v.tensor_single_scalar(srow[:, :], ps_s[:, :], 1.0 / D, ALU.mult)
            v.drain()
            v.tensor_single_scalar(vrow[:, :], ps_q[:, :], 1.0 / D, ALU.mult)
            v.drain()
            v.tensor_tensor(mrow[:, :], srow[:, :], srow[:, :], ALU.mult)
            v.drain()
            i = v.tensor_tensor(vrow[:, :], vrow[:, :], mrow[:, :], ALU.subtract)
            v.drain()
            ctx.inc(var, i, 1)
            ctx.wait(v, rst, c + 1)
            i = v.tensor_tensor(mrow[:, :], srow[:, :], rrow[:, :], ALU.mult)
            v.drain()
            ctx.inc(mr, i, 1)
            ctx.wait(v, bc, c + 1)
            v.tensor_copy(rbs[:, :], ps_rb[:, :])
            v.drain()
            i = v.tensor_copy(mbs[:, :], ps_mb[:, :])
            v.drain()
            ctx.inc(rbc, i, 1)
            if c >= 1:
                ctx.wait(v, stname, 16 * c)
            v.tensor_tensor(xcv[c % 2], xcv[c % 2], bcast(rbs), ALU.mult)
            v.drain()
            i = v.tensor_tensor(xnv, xcv[c % 2], bcast(mbs), ALU.subtract)
            v.drain()
            ctx.inc(ap, i, 1)
    block.vector(_vec)


def emit_p2_wload(ctx, block, nc, es, of_tiles, wqkv, bqkv, tag=""):
    nof = len(of_tiles)
    wsb = [es.enter_context(nc.sbuf_tensor(f"p2_w{i}{tag}", [128, 32 * of_tiles[i][1]], BF16))
           for i in range(nof)]
    bsb = es.enter_context(nc.sbuf_tensor(f"p2_b{tag}", [128, nof], F32))
    wl, bl = f"p2wl{tag}", f"p2bl{tag}"
    ctx.persistent.update({wl, bl})

    def _g1(g):
        for i, (r0, w) in enumerate(of_tiles):
            ctx.inc(wl, g.dma_start(out=wsb[i][:, :], in_=wqkv[:, 32 * r0:32 * (r0 + w)]), 16)
        for i, (r0, w) in enumerate(of_tiles):
            ctx.inc(bl, g.dma_start(out=bsb[0:w, i:i + 1], in_=bqkv[r0:r0 + w, :]), 16)
    block.gpsimd(_g1)
    return wsb, bsb


def emit_p2(ctx, block, nc, es, bufs, of_tiles, wqkv, bqkv, xn_gs, qkv_loc, tag="",
            ccx_name=None, prew=None):
    nof = len(of_tiles)
    if prew is None:
        wsb, bsb = emit_p2_wload(ctx, block, nc, es, of_tiles, wqkv, bqkv, tag)
    else:
        wsb, bsb = prew
    xnc = [es.enter_context(nc.sbuf_tensor(f"p2_xn{i}{tag}", [128, 32 * 512], BF16))
           for i in range(2)]
    stg = [es.enter_context(nc.sbuf_tensor(f"p2_stg{i}{tag}", [128, 512], BF16))
           for i in range(4)]
    pss = [es.enter_context(nc.psum_tensor(f"p2_ps{i}{tag}", [128, 512], F32))
           for i in range(4)]
    wl, bl, mm, ev = [f"p2{n}{tag}" for n in ["wl", "bl", "mm", "ev"]]
    xl = [f"p2xl{i}{tag}" for i in range(2)]
    st = [f"p2st{i}{tag}" for i in range(4)]

    chunks = [(r, 0) for r in range(8)] + [(r, 1) for r in range(8)]

    def _sync(s):
        for tp in range(8):
            if tp >= 1:
                ctx.wait(s, mm, tp * nof)
            for h in range(2):
                r, half = chunks[2 * tp + h]
                if ccx_name is not None and (2 * tp + h) in (0, 8):
                    ctx.wait(s, ccx_name, half + 1)
                src = xn_gs[half][r * D:(r + 1) * D, :] \
                    .rearrange("(t p) c -> p t c", p=128)
                ctx.inc(xl[h], s.dma_start(
                    out=xnc[h].rearrange("p (t c) -> p t c", t=32), in_=src), 16)
    block.sync(_sync)

    def _tensor(t):
        for tp in range(8):
            for h in range(2):
                ctx.wait(t, xl[h], 16 * (tp + 1))
            for i, (r0, w) in enumerate(of_tiles):
                k = tp * nof + i
                if tp == 0:
                    ctx.wait(t, wl, 16 * (i + 1))
                if k >= 2:
                    ctx.wait(t, ev, 2 * (k - 1))
                wv = wsb[i].rearrange("p (t o) -> p t o", t=32)
                for f in range(32):
                    for h in range(2):
                        ii = t.matmul(pss[(k % 2) * 2 + h][0:w, :], wv[:, f, :],
                                      xnc[h].rearrange("p (t c) -> p t c", t=32)[:, f, :],
                                      start=(f == 0), stop=(f == 31))
                ctx.inc(mm, ii, 1)
    block.tensor(_tensor)

    def _vec(v):
        ctx.wait(v, bl, 16 * nof)
        for k in range(8 * nof):
            i_of = k % nof
            w = of_tiles[i_of][1]
            ctx.wait(v, mm, k + 1)
            for h in range(2):
                e = 2 * k + h
                if e >= 4:
                    ctx.wait(v, st[e % 4], 16 * (e // 4))
                ii = v.tensor_scalar(stg[e % 4][0:w, :], pss[(k % 2) * 2 + h][0:w, :],
                                     bsb[0:w, i_of:i_of + 1], None, ALU.add)
                v.drain()
                ctx.inc(ev, ii, 1)
    block.vector(_vec)

    def _sc(sc):
        for k in range(8 * nof):
            tp, i_of = divmod(k, nof)
            r0, w = of_tiles[i_of]
            for h in range(2):
                e = 2 * k + h
                r, half = chunks[2 * tp + h]
                col = (2 * r + half) * 512
                ctx.wait(sc, ev, e + 1)
                ctx.inc(st[e % 4], sc.dma_start(
                    out=qkv_loc[r0:r0 + w, col:col + 512],
                    in_=stg[e % 4][0:w, :]), 16)
        for i4 in range(4):
            nst = (16 * nof + 3 - i4) // 4
            ctx.wait(sc, st[i4], 16 * nst)
    block.scalar(_sc)


def emit_p3(ctx, block, nc, es, bufs, layer, qkv_loc, ao_loc, tag=""):
    if layer == 0:
        dh, units = 64, [(0, 0), (1, 0)]
        qrow, krow, vrow = 0, 64, 128
        scale = 1.0
        ebias = -M0
        escale = 1.0 / math.sqrt(512.0)
        causal = False
    else:
        dh, units = 128, [(s, hl) for hl in range(4) for s in range(2)]
        qrow, krow, vrow = 0, 512, 1024
        scale = 1.0 / math.sqrt(128.0)
        ebias = -M1
        escale = 1.0
        causal = True

    qsb = [es.enter_context(nc.sbuf_tensor(f"p3_q{i}{tag}", [128, 4096], BF16)) for i in range(2)]
    ksb = [es.enter_context(nc.sbuf_tensor(f"p3_k{i}{tag}", [128, 4096], BF16)) for i in range(2)]
    vsb = [es.enter_context(nc.sbuf_tensor(f"p3_v{i}{tag}", [128, 4096], BF16)) for i in range(2)]
    vt = es.enter_context(nc.sbuf_tensor(f"p3_vt{tag}", [128, 32 * dh], BF16))
    expS = [es.enter_context(nc.sbuf_tensor(f"p3_e{i}{tag}", [128, 32 * 512], BF16))
            for i in range(2)]
    dsb = [es.enter_context(nc.sbuf_tensor(f"p3_ds{i}{tag}", [128, 512], F32)) for i in range(2)]
    aos = [es.enter_context(nc.sbuf_tensor(f"p3_ao{i}{tag}", [128, 512], BF16)) for i in range(2)]
    esum = [es.enter_context(nc.sbuf_tensor(f"p3_es{i}{tag}", [128, 512], F32)) for i in range(2)]
    ps_s = [es.enter_context(nc.psum_tensor(f"p3_pss{i}{tag}", [128, 512], F32)) for i in range(4)]
    ps_o = es.enter_context(nc.psum_tensor(f"p3_pso{tag}", [128, 512], F32))
    ps_d = es.enter_context(nc.psum_tensor(f"p3_psd{tag}", [128, 512], F32))
    ps_t = [es.enter_context(nc.psum_tensor(f"p3_pst{i}{tag}", [128, 128], BF16)) for i in range(2)]
    vtm, vte, smm, sev, mk, av, no, ds, rc = [f"p3{n}{tag}" for n in
        ["vtm", "vte", "smm", "sev", "mk", "av", "no", "ds", "rc"]]
    qkvl = [f"p3qkv{i}{tag}" for i in range(2)]
    ast = [f"p3ast{i}{tag}" for i in range(2)]
    vtv = vt.rearrange("p (t d) -> p t d", t=32)
    ev_ = [b.rearrange("p (t c) -> p t c", t=32) for b in expS]
    NU = len(units)

    def kcm(qc):
        return min(32, 4 * (qc + 1)) if causal else 32

    KSUM = sum(kcm(q) for q in range(8))

    def SC(u, qc):
        return u * KSUM + sum(kcm(q) for q in range(qc))

    def QI(u, qc):
        return u * 8 + qc

    if causal:
        def _g(g):
            for u in range(NU):
                for qc in range(8):
                    for p in range(4):
                        kc = 4 * qc + p
                        ctx.wait(g, sev, SC(u, qc) + kc + 1)
                        i = g.affine_select(out=ev_[qc % 2][:, kc, :],
                                            in_=ev_[qc % 2][:, kc, :],
                                            compare_op=ALU.is_ge, fill=0.0,
                                            base=-128 * p,
                                            pattern=[[1, 512]], channel_multiplier=-1)
                        ctx.inc(mk, i, 1)
                    g.drain()
        block.gpsimd(_g)

    def _sync(s):
        def load(u):
            sq_, hl = units[u]
            c0 = sq_ * 4096
            ub = u % 2
            for row0, sb in ((qrow, qsb[ub]), (krow, ksb[ub]), (vrow, vsb[ub])):
                ctx.inc(qkvl[ub], s.dma_start(out=sb[0:dh, :],
                         in_=qkv_loc[row0 + hl * dh:row0 + (hl + 1) * dh, c0:c0 + 4096]), 16)
        load(0)
        if NU > 1:
            load(1)
        AOFloc = 64 if layer == 0 else 512
        for u, (sq_, hl) in enumerate(units):
            c0 = sq_ * 4096
            for qc in range(8):
                q = QI(u, qc)
                ctx.wait(s, no, q + 1)
                g0 = c0 + qc * 512
                blk = g0 // 1024
                r0 = blk * AOFloc + hl * dh
                ctx.inc(ast[q % 2], s.dma_start(
                    out=ao_loc[r0:r0 + dh, (g0 % 1024):(g0 % 1024) + 512],
                    in_=aos[q % 2][0:dh, :]), 16)
            if u + 2 < NU:
                ctx.wait(s, av, 8 * (u + 1))
                load(u + 2)
        ctx.wait(s, ast[0], 16 * 4 * NU)
        ctx.wait(s, ast[1], 16 * 4 * NU)
        ctx.inc(f"p3_done{tag}", s.nop(), 1)
    block.sync(_sync)

    def _tensor(t):
        ctx.wait(t, "init", 1)

        def emit_sprime(u, qc, ub):
            km = kcm(qc)
            for kc in range(km):
                n = SC(u, qc) + kc
                if n >= 4:
                    ctx.wait(t, sev, n - 3)
                i = t.matmul(ps_s[n % 4][:, :],
                             ksb[ub][0:dh, kc * 128:(kc + 1) * 128],
                             qsb[ub][0:dh, qc * 512:(qc + 1) * 512],
                             start=True, stop=True)
                ctx.inc(smm, i, 1)

        def emit_av(u, qc, ub):
            q = QI(u, qc)
            km = kcm(qc)
            ctx.wait(t, sev, SC(u, qc) + km)
            if causal:
                ctx.wait(t, mk, 4 * (q + 1))
            if qc == 0:
                ctx.wait(t, vte, 32 * (u + 1))
            ctx.wait(t, no, q)
            for kc in range(km):
                t.matmul(ps_o[0:dh, :], vtv[:, kc, :], ev_[qc % 2][:, kc, :],
                         start=(kc == 0), stop=(kc == km - 1))
            ctx.wait(t, ds, q + 1)
            i = t.matmul(ps_d[0:dh, :], bufs["ones_k_f32"][:, 0:dh], esum[q % 2][:, :],
                         start=True, stop=True)
            ctx.inc(av, i, 1)

        for u in range(NU):
            ub = u % 2
            ctx.wait(t, qkvl[ub], 48 * (u // 2 + 1))
            for kc in range(32):
                if u > 0 or kc >= 2:
                    ctx.wait(t, vte, 32 * u + kc - 1)
                i = t.transpose(ps_t[kc % 2][:, 0:dh], vsb[ub][0:dh, kc * 128:(kc + 1) * 128],
                                bufs["ident"][0:dh, 0:dh])
                ctx.inc(vtm, i, 1)
            emit_sprime(u, 0, ub)
            for qc in range(1, 8):
                emit_av(u, qc - 1, ub)
                emit_sprime(u, qc, ub)
            emit_av(u, 7, ub)
    block.tensor(_tensor)

    def _scalar(sc):
        for u in range(NU):
            for qc in range(8):
                q = QI(u, qc)
                km = kcm(qc)
                for kc in range(km):
                    n = SC(u, qc) + kc
                    ctx.wait(sc, smm, n + 1)
                    if kc == 0 and q >= 2:
                        ctx.wait(sc, av, q - 1)
                    i = sc.activation(ev_[qc % 2][:, kc, :], ps_s[n % 4][:, :],
                                      AF.Exp, scale=f32(scale), bias=f32(ebias))
                    ctx.inc(sev, i, 1)
    block.scalar(_scalar)

    def _vec(v):
        ctx.wait(v, "init", 1)

        def emit_aos(j):
            ctx.wait(v, av, j + 1)
            if j >= 2:
                ctx.wait(v, ast[j % 2], 16 * (j // 2))
            v.reciprocal(dsb[j % 2][0:dh, :], ps_d[0:dh, :])
            v.drain()
            i = v.scalar_tensor_tensor(aos[j % 2][0:dh, :], ps_o[0:dh, :],
                                       float(escale), dsb[j % 2][0:dh, :],
                                       ALU.mult, ALU.mult)
            v.drain()
            ctx.inc(no, i, 1)

        for u, (sq_, hl) in enumerate(units):
            ub = u % 2
            if u > 0:
                ctx.wait(v, av, 8 * u)
            for kc in range(32):
                ctx.wait(v, vtm, 32 * u + kc + 1)
                i = v.tensor_copy(vtv[:, kc, :], ps_t[kc % 2][:, 0:dh])
                v.drain()
                ctx.inc(vte, i, 1)
            for qc in range(8):
                q = QI(u, qc)
                km = kcm(qc)
                if q >= 2:
                    ctx.wait(v, av, q - 1)
                for kc in range(km):
                    if causal and kc == km - 4:
                        ctx.wait(v, mk, 4 * (q + 1))
                    ctx.wait(v, sev, SC(u, qc) + kc + 1)
                    if kc == 0:
                        v.tensor_copy(esum[q % 2][:, :], ev_[qc % 2][:, kc, :])
                    else:
                        v.tensor_tensor(esum[q % 2][:, :], esum[q % 2][:, :],
                                        ev_[qc % 2][:, kc, :], ALU.add)
                    i = v.drain()
                ctx.inc(ds, i, 1)
                if qc >= 1:
                    emit_aos(q - 1)
            emit_aos(QI(u, 7))
    block.vector(_vec)


def emit_p4(ctx, block, nc, es, bufs, AOF, ao_src, xn_locs, woT, wfT, brows,
            xsrc, xdst, tag="", ao_cols_dyn=False):
    na = AOF // 128
    nf = na + 32
    act = es.enter_context(nc.sbuf_tensor(f"p4_act{tag}", [128, nf * 1024], BF16))
    wsb = [es.enter_context(nc.sbuf_tensor(f"p4_w{i}{tag}", [128, nf * 128], BF16)) for i in range(2)]
    brsb = [es.enter_context(nc.sbuf_tensor(f"p4_br{i}{tag}", [1, 4096], BF16))
            for i in range(len(brows))]
    xt = [es.enter_context(nc.sbuf_tensor(f"p4_x{i}{tag}", [128, 1024], F32)) for i in range(2)]
    stg = [es.enter_context(nc.sbuf_tensor(f"p4_s{i}{tag}", [128, 1024], F32)) for i in range(2)]
    pss = [es.enter_context(nc.psum_tensor(f"p4_ps{i}{tag}", [128, 512], F32)) for i in range(2)]
    aol, xnl, brl, mm, ev = [f"p4{n}{tag}" for n in ["ao", "xn", "br", "mm", "ev"]]
    wl = [f"p4wl{i}{tag}" for i in range(2)]
    xld = [f"p4x{i}{tag}" for i in range(2)]
    st = [f"p4st{i}{tag}" for i in range(2)]
    nbr = len(brows)
    actv = act.rearrange("p (t c) -> p t c", t=nf)

    def _g(g):
        for i, br in enumerate(brows):
            ctx.inc(brl, g.dma_start(out=brsb[i][:, :], in_=br[:, :]), 16)
        for f in range(32):
            for hh in range(2):
                ctx.inc(xnl, g.dma_start(
                    out=actv[:, na + f, hh * 512:(hh + 1) * 512],
                    in_=xn_locs[hh][f * 128:(f + 1) * 128, :]), 16)
        for of in range(32):
            if of >= 2:
                ctx.wait(g, mm, 2 * (of - 1))
            wv = wsb[of % 2].rearrange("p (t o) -> p t o", t=nf)
            ctx.inc(wl[of % 2], g.dma_start(
                out=wv[:, 0:na, :],
                in_=woT[:, of * na * 128:(of + 1) * na * 128]
                    .rearrange("p (t o) -> p t o", t=na)), 16)
            for fbg in range(4):
                ctx.inc(wl[of % 2], g.dma_start(
                    out=wv[:, na + fbg * 8:na + (fbg + 1) * 8, :],
                    in_=wfT[fbg * 1024:(fbg + 1) * 1024, of * 128:(of + 1) * 128]
                        .rearrange("(t p) o -> p t o", p=128)), 16)
    block.gpsimd(_g)

    def _sync(s):
        if ao_cols_dyn:
            pid = ctx.pid if getattr(ctx, "pid", None) is not None else s.partition_id()
            srcap = ao_src[:, bass.ts(pid, 1024)].rearrange("(t p) c -> p t c", p=128)
        else:
            srcap = ao_src.rearrange("(t p) c -> p t c", p=128)
        ctx.inc(aol, s.dma_start(out=actv[:, 0:na, :], in_=srcap), 16)
        for of in range(34):
            if of >= 2:
                ctx.wait(s, ev, of - 1)
            if of < 32:
                ctx.inc(xld[of % 2], s.dma_start(out=xt[of % 2][:, :],
                                                 in_=xsrc[of * 128:(of + 1) * 128, :]), 16)
            if of >= 2:
                o2 = of - 2
                ctx.inc(st[o2 % 2], s.dma_start(out=xdst[o2 * 128:(o2 + 1) * 128, :],
                                                in_=stg[o2 % 2][:, :]), 16)
        ctx.wait(s, st[0], 16 * 16)
        ctx.wait(s, st[1], 16 * 16)
        ctx.inc(f"p4_done{tag}", s.nop(), 1)
    block.sync(_sync)

    def _tensor(t):
        ctx.wait(t, "init", 1)
        ctx.wait(t, aol, 16)
        ctx.wait(t, xnl, 16 * 64)
        ctx.wait(t, brl, 16 * nbr)
        for of in range(32):
            ctx.wait(t, wl[of % 2], 80 * (of // 2 + 1))
            if of >= 1:
                ctx.wait(t, ev, of)
            wv = wsb[of % 2].rearrange("p (t o) -> p t o", t=nf)
            for f in range(nf):
                for tc in range(2):
                    t.matmul(pss[tc][:, :], wv[:, f, :],
                             actv[:, f, tc * 512:(tc + 1) * 512],
                             start=(f == 0), stop=False)
            for tc in range(2):
                for i in range(nbr):
                    ii = t.matmul(pss[tc][:, :],
                                  brsb[i][:, of * 128:(of + 1) * 128],
                                  bufs["ones_row_bf"][:, :],
                                  start=False, stop=(i == nbr - 1))
            ctx.inc(mm, ii, 2)
    block.tensor(_tensor)

    def _vec(v):
        for of in range(32):
            ctx.wait(v, mm, 2 * (of + 1))
            ctx.wait(v, xld[of % 2], 16 * (of // 2 + 1))
            if of >= 2:
                ctx.wait(v, st[of % 2], 16 * (of // 2))
            for tc in range(2):
                i = v.tensor_tensor(stg[of % 2][:, tc * 512:(tc + 1) * 512],
                                    pss[tc][:, :], xt[of % 2][:, tc * 512:(tc + 1) * 512],
                                    ALU.add)
                v.drain()
            ctx.inc(ev, i, 1)
    block.vector(_vec)


def host_prep(inp):
    out = [dict() for _ in range(NCORES)]
    x = np.ascontiguousarray(np.asarray(inp["L0_attn_x"], np.float32).reshape(TOK, D))
    for c in range(NCORES):
        out[c]["x0"] = np.ascontiguousarray(x[TPC * c:TPC * (c + 1)].T)

    def _tile_w(wT, widths):
        parts = []
        r0 = 0
        for w in widths:
            parts.append(wT.reshape(32, 128, -1)[:, :, r0:r0 + w]
                         .transpose(1, 0, 2).reshape(128, 32 * w))
            r0 += w
        return np.ascontiguousarray(np.concatenate(parts, 1))

    def _tile_of(wT):
        na = wT.shape[0] // 128
        return np.ascontiguousarray(
            wT.reshape(na, 128, 32, 128).transpose(1, 2, 0, 3).reshape(128, 32 * na * 128))

    wqk = np.asarray(inp["L0_attn_Wqk_w"], np.float32)
    bqk = np.asarray(inp["L0_attn_Wqk_b"], np.float32)
    wv0 = np.asarray(inp["L0_attn_Wv_w"], np.float32)
    bv0 = np.asarray(inp["L0_attn_Wv_b"], np.float32)
    for c in range(NCORES):
        qr = [128 * c + 2 * e for e in range(64)]
        kr = [128 * c + 2 * e + 1 for e in range(64)]
        w = np.concatenate([wqk[qr], wqk[kr], wv0[64 * c:64 * c + 64]], 0)
        b = np.concatenate([bqk[qr], bqk[kr], bv0[64 * c:64 * c + 64]], 0)
        out[c]["wqkv0"] = np.ascontiguousarray(w.T)
        out[c]["wqkv0t"] = _tile_w(w.T, [128, 64])
        out[c]["bqkv0"] = np.ascontiguousarray(b[:, None])

    wq = np.asarray(inp["L1_attn_Wq"], np.float32)
    wk = np.asarray(inp["L1_attn_Wk"], np.float32)
    wv = np.asarray(inp["L1_attn_Wv"], np.float32)
    bq = np.asarray(inp["L1_attn_bq"], np.float32)
    bk = np.asarray(inp["L1_attn_bk"], np.float32)
    bv = np.asarray(inp["L1_attn_bv"], np.float32)
    for c in range(NCORES):
        sl = slice(512 * c, 512 * c + 512)
        w = np.concatenate([wq[sl], wk[sl], wv[sl]], 0)
        b = np.concatenate([bq[sl], bk[sl], bv[sl]], 0)
        out[c]["wqkv1"] = np.ascontiguousarray(w.T)
        out[c]["wqkv1t"] = _tile_w(w.T, [128] * 12)
        out[c]["bqkv1"] = np.ascontiguousarray(b[:, None])

    wo0T = np.ascontiguousarray(np.asarray(inp["L0_attn_Wo_w"], np.float32).T)
    bo0 = np.ascontiguousarray(np.asarray(inp["L0_attn_Wo_b"], np.float32)[None, :])
    wo1T = np.ascontiguousarray(np.asarray(inp["L1_attn_Wo"], np.float32).T)
    bo1 = np.ascontiguousarray(np.asarray(inp["L1_attn_bo"], np.float32)[None, :])
    bf1 = np.ascontiguousarray(np.asarray(inp["L1_ffn_bias"], np.float32)[None, :])
    wo0t = _tile_of(wo0T)
    wo1t = _tile_of(wo1T)
    for c in range(NCORES):
        out[c]["wo0T"] = wo0T
        out[c]["wo0t"] = wo0t
        out[c]["bo0"] = bo0
        out[c]["wo1T"] = wo1T
        out[c]["wo1t"] = wo1t
        out[c]["bo1"] = bo1
        out[c]["bf1"] = bf1

    qw0T = np.asarray(inp["L0_ffn_qweight"], np.int32).reshape(D, D).T
    am0T = np.asarray(inp["L0_ffn_absmax"], np.float32).reshape(D, 64).T
    qw1 = np.asarray(inp["L1_ffn_qweight"], np.int32)
    zp1 = np.ascontiguousarray(np.asarray(inp["L1_ffn_weight_zero_point"], np.float32).reshape(1, D))
    sc1 = np.ascontiguousarray(np.asarray(inp["L1_ffn_weight_scale"], np.float32).reshape(1, D))
    for c in range(NCORES):
        out[c]["qw0"] = np.ascontiguousarray(qw0T[512 * c:512 * c + 512])
        out[c]["am0"] = np.ascontiguousarray(am0T[8 * c:8 * c + 8])
        sl = np.ascontiguousarray(qw1[:, 384 * c:384 * c + 384].T)
        out[c]["qw1"] = np.ascontiguousarray(np.stack([sl[0::3], sl[1::3], sl[2::3]], 0))
        out[c]["zp1"] = zp1
        out[c]["sc1"] = sc1
    return out


def _mk_nc():
    return bass.Bass("TRN2", target_bir_lowering=False)


def build_prep():
    nc = _mk_nc()
    dr = dict(
        qw0=nc.dram_tensor("qw0", [512, 4096], I32, kind="ExternalInput").ap(),
        am0=nc.dram_tensor("am0", [8, 4096], F32, kind="ExternalInput").ap(),
        qw1=nc.dram_tensor("qw1", [3, 128, 4096], I32, kind="ExternalInput").ap(),
        zp1=nc.dram_tensor("zp1", [1, 4096], F32, kind="ExternalInput").ap(),
        sc1=nc.dram_tensor("sc1", [1, 4096], F32, kind="ExternalInput").ap(),
        w0t_loc=nc.dram_tensor("w0t_loc", [512, 4096], BF16, kind="ExternalOutput").ap(),
        w1t_loc=nc.dram_tensor("w1t_loc", [512, 4096], BF16, kind="ExternalOutput").ap(),
    )
    with contextlib.ExitStack() as es:
        ctx = Ctx(nc, es)
        with nc.Block() as block:
            emit_prep(ctx, block, nc, es, dr)
    return nc


def build_p1():
    nc = _mk_nc()
    xsrc = nc.dram_tensor("xsrc", [D, TPC], F32, kind="ExternalInput").ap()
    xn0 = nc.dram_tensor("xn0", [D, 512], BF16, kind="ExternalOutput").ap()
    xn1 = nc.dram_tensor("xn1", [D, 512], BF16, kind="ExternalOutput").ap()
    with contextlib.ExitStack() as es:
        ctx = Ctx(nc, es)
        bufs = alloc_init_bufs(nc, es, False)
        with nc.Block() as block:
            emit_init(ctx, block, bufs, False)
            emit_p1(ctx, block, nc, es, bufs, xsrc, (xn0, xn1))
    return nc


def build_p2(layer):
    nc = _mk_nc()
    OF = 192 if layer == 0 else 1536
    of_tiles = [(0, 128), (128, 64)] if layer == 0 else [(i * 128, 128) for i in range(12)]
    wqkv = nc.dram_tensor("wqkv", [128, 32 * OF], F32, kind="ExternalInput").ap()
    bqkv = nc.dram_tensor("bqkv", [OF, 1], F32, kind="ExternalInput").ap()
    xn_g0 = nc.dram_tensor("xn_g0", [NCORES * D, 512], BF16, kind="ExternalInput").ap()
    xn_g1 = nc.dram_tensor("xn_g1", [NCORES * D, 512], BF16, kind="ExternalInput").ap()
    qkv_loc = nc.dram_tensor("qkv_loc", [OF, TOK], BF16, kind="ExternalOutput").ap()
    with contextlib.ExitStack() as es:
        ctx = Ctx(nc, es)
        bufs = alloc_init_bufs(nc, es, False)
        with nc.Block() as block:
            emit_init(ctx, block, bufs, False)
            emit_p2(ctx, block, nc, es, bufs, of_tiles, wqkv, bqkv, (xn_g0, xn_g1), qkv_loc)
    return nc


def build_p3(layer):
    nc = _mk_nc()
    OF = 192 if layer == 0 else 1536
    AOF = 64 if layer == 0 else 512
    qkv_loc = nc.dram_tensor("qkv_loc", [OF, TOK], BF16, kind="ExternalInput").ap()
    ao_loc = nc.dram_tensor("ao_loc", [8 * AOF, TPC], BF16, kind="ExternalOutput").ap()
    with contextlib.ExitStack() as es:
        ctx = Ctx(nc, es)
        bufs = alloc_init_bufs(nc, es, layer == 1)
        with nc.Block() as block:
            emit_init(ctx, block, bufs, layer == 1)
            emit_p3(ctx, block, nc, es, bufs, layer, qkv_loc, ao_loc)
    return nc


def build_p4(layer):
    nc = _mk_nc()
    AOF = 512 if layer == 0 else 4096
    nbr = 1 if layer == 0 else 2
    na = AOF // 128
    ao = nc.dram_tensor("ao", [AOF, TPC], BF16, kind="ExternalInput").ap()
    xn0 = nc.dram_tensor("xn0", [D, 512], BF16, kind="ExternalInput").ap()
    xn1 = nc.dram_tensor("xn1", [D, 512], BF16, kind="ExternalInput").ap()
    woT = nc.dram_tensor("woT", [128, 32 * na * 128], F32, kind="ExternalInput").ap()
    wfT = nc.dram_tensor("wfT", [D, D], BF16, kind="ExternalInput").ap()
    brows = [nc.dram_tensor(f"br{i}", [1, D], F32, kind="ExternalInput").ap()
             for i in range(nbr)]
    xsrc = nc.dram_tensor("xsrc", [D, TPC], F32, kind="ExternalInput").ap()
    xdst = nc.dram_tensor("xdst", [D, TPC], F32, kind="ExternalOutput").ap()
    with contextlib.ExitStack() as es:
        ctx = Ctx(nc, es)
        bufs = alloc_init_bufs(nc, es, False)
        with nc.Block() as block:
            emit_init(ctx, block, bufs, False)
            emit_p4(ctx, block, nc, es, bufs, AOF, ao, (xn0, xn1), woT, wfT, brows,
                    xsrc, xdst, ao_cols_dyn=False)
    return nc


_NC_CACHE = {}


def _get(name, fn, *a):
    key = (name,) + a
    if key not in _NC_CACHE:
        _NC_CACHE[key] = fn(*a)
    return _NC_CACHE[key]


def _run(nc, in_maps):
    return run_bass_kernel_spmd(nc, in_maps, core_ids=list(range(NCORES))).results


def kernel_unfused(**inputs):
    pc = host_prep(inputs)

    nc = _get("prep", build_prep)
    res = _run(nc, [{k: pc[c][k] for k in ["qw0", "am0", "qw1", "zp1", "sc1"]}
                    for c in range(NCORES)])
    w0t = np.concatenate([res[c]["w0t_loc"] for c in range(NCORES)], 0)
    w1t = np.concatenate([res[c]["w1t_loc"] for c in range(NCORES)], 0)

    xcur = [pc[c]["x0"] for c in range(NCORES)]
    for layer in range(2):
        nc = _get("p1", build_p1)
        res = _run(nc, [{"xsrc": xcur[c]} for c in range(NCORES)])
        xn0 = [res[c]["xn0"] for c in range(NCORES)]
        xn1 = [res[c]["xn1"] for c in range(NCORES)]
        xn_g0 = np.concatenate(xn0, 0)
        xn_g1 = np.concatenate(xn1, 0)

        nc = _get("p2", build_p2, layer)
        res = _run(nc, [{"wqkv": pc[c][f"wqkv{layer}t"], "bqkv": pc[c][f"bqkv{layer}"],
                         "xn_g0": xn_g0, "xn_g1": xn_g1} for c in range(NCORES)])
        qkv = [res[c]["qkv_loc"] for c in range(NCORES)]

        nc = _get("p3", build_p3, layer)
        res = _run(nc, [{"qkv_loc": qkv[c]} for c in range(NCORES)])
        AOFloc = 64 if layer == 0 else 512
        ao_per_core = [np.concatenate([res[i]["ao_loc"][c * AOFloc:(c + 1) * AOFloc]
                                       for i in range(NCORES)], 0) for c in range(NCORES)]

        nc = _get("p4", build_p4, layer)
        wf = w0t if layer == 0 else w1t
        in_maps = []
        for c in range(NCORES):
            m = {"ao": ao_per_core[c],
                 "xn0": xn0[c], "xn1": xn1[c], "woT": pc[c][f"wo{layer}t"], "wfT": wf,
                 "xsrc": xcur[c]}
            if layer == 0:
                m["br0"] = pc[c]["bo0"]
            else:
                m["br0"] = pc[c]["bo1"]
                m["br1"] = pc[c]["bf1"]
            in_maps.append(m)
        res = _run(nc, in_maps)
        xcur = [res[c]["xdst"] for c in range(NCORES)]

    full = np.empty((TOK, D), np.float32)
    for c in range(NCORES):
        full[TPC * c:TPC * (c + 1)] = xcur[c].T
    return full.reshape(2, SEQ, D)


def build_fused():
    nc = _mk_nc()
    ei, eo = "ExternalInput", "ExternalOutput"
    dr = {}
    dr["x0"] = nc.dram_tensor("x0", [D, TPC], F32, kind=ei).ap()
    dr["wqkv0"] = nc.dram_tensor("wqkv0", [128, 32 * 192], F32, kind=ei).ap()
    dr["bqkv0"] = nc.dram_tensor("bqkv0", [192, 1], F32, kind=ei).ap()
    dr["wqkv1"] = nc.dram_tensor("wqkv1", [128, 32 * 1536], F32, kind=ei).ap()
    dr["bqkv1"] = nc.dram_tensor("bqkv1", [1536, 1], F32, kind=ei).ap()
    dr["wo0t"] = nc.dram_tensor("wo0t", [128, 32 * 4 * 128], F32, kind=ei).ap()
    dr["wo1t"] = nc.dram_tensor("wo1t", [128, 32 * 32 * 128], F32, kind=ei).ap()
    dr["bo0"] = nc.dram_tensor("bo0", [1, D], F32, kind=ei).ap()
    dr["bo1"] = nc.dram_tensor("bo1", [1, D], F32, kind=ei).ap()
    dr["bf1"] = nc.dram_tensor("bf1", [1, D], F32, kind=ei).ap()
    dr["qw0"] = nc.dram_tensor("qw0", [512, 4096], I32, kind=ei).ap()
    dr["am0"] = nc.dram_tensor("am0", [8, 4096], F32, kind=ei).ap()
    dr["qw1"] = nc.dram_tensor("qw1", [3, 128, 4096], I32, kind=ei).ap()
    dr["zp1"] = nc.dram_tensor("zp1", [1, 4096], F32, kind=ei).ap()
    dr["sc1"] = nc.dram_tensor("sc1", [1, 4096], F32, kind=ei).ap()
    dr["xout"] = nc.dram_tensor("xout", [D, TPC], F32, kind=eo).ap()
    dr["w0t_loc"] = nc.dram_tensor("w0t_loc", [512, 4096], BF16).ap()
    dr["w1t_loc"] = nc.dram_tensor("w1t_loc", [512, 4096], BF16).ap()
    dr["w0t"] = nc.dram_tensor("w0t", [D, D], BF16).ap()
    dr["w1t"] = nc.dram_tensor("w1t", [D, D], BF16).ap()
    dr["xn_loc0"] = nc.dram_tensor("xn_loc0", [D, 512], BF16).ap()
    dr["xn_loc1"] = nc.dram_tensor("xn_loc1", [D, 512], BF16).ap()
    dr["xn_g0"] = nc.dram_tensor("xn_g0", [NCORES * D, 512], BF16).ap()
    dr["xn_g1"] = nc.dram_tensor("xn_g1", [NCORES * D, 512], BF16).ap()
    dr["qkv0"] = nc.dram_tensor("qkv0", [192, TOK], BF16).ap()
    dr["qkv1"] = nc.dram_tensor("qkv1", [1536, TOK], BF16).ap()
    dr["ao0_loc"] = nc.dram_tensor("ao0_loc", [512, TPC], BF16).ap()
    dr["ao0_g"] = nc.dram_tensor("ao0_g", [512, TPC], BF16).ap()
    dr["ao1_loc"] = nc.dram_tensor("ao1_loc", [4096, TPC], BF16).ap()
    dr["ao1_g"] = nc.dram_tensor("ao1_g", [4096, TPC], BF16).ap()
    dr["xmid"] = nc.dram_tensor("xmid", [D, TPC], F32).ap()

    of0 = [(0, 128), (128, 64)]
    of1 = [(i * 128, 128) for i in range(12)]
    rg = [list(range(NCORES))]

    with contextlib.ExitStack() as es:
        ctx = Ctx(nc, es)
        bufs = alloc_init_bufs(nc, es, False)
        with nc.Block() as block:
            emit_init(ctx, block, bufs, False, dyn_src=dr["xn_loc0"])
            es2a = contextlib.ExitStack()
            prew0 = emit_p2_wload(ctx, block, nc, es2a, of0,
                                  dr["wqkv0"], dr["bqkv0"], tag="2a")
            esp = contextlib.ExitStack()
            prep = PrepV(ctx, nc, esp, dr, rg, "p1st1a")
            prep.emit_loads(block)
            PREPX = ("ccw", "ppl", "ppst", "ppv", "prep_done")

            for layer, tg in ((0, "a"), (1, "b")):
                xsrc = dr["x0"] if layer == 0 else dr["xmid"]
                xdst = dr["xmid"] if layer == 0 else dr["xout"]
                qkv = dr[f"qkv{layer}"]
                aoloc = dr[f"ao{layer}_loc"]
                aog = dr[f"ao{layer}_g"]
                with contextlib.ExitStack() as es1:
                    emit_p1(ctx, block, nc, es1, bufs, xsrc,
                            (dr["xn_loc0"], dr["xn_loc1"]), tag="1" + tg)
                    if layer == 0:
                        prep.emit_rest(block)
                    else:
                        def _gx(g):
                            for hh in range(2):
                                ctx.wait(g, f"p1st1{tg}", 64 * (hh + 1))
                                ctx.inc("ccx" + tg, g.collective_compute(
                                    "AllGather", ALU.bypass, replica_groups=rg,
                                    ins=[dr[f"xn_loc{hh}"][:, :]],
                                    outs=[dr[f"xn_g{hh}"][:, :]]), 1)
                        block.gpsimd(_gx)
                    emit_barrier(ctx, block, exclude=PREPX + ("ccx" + tg,))
                with contextlib.ExitStack() as es2:
                    emit_p2(ctx, block, nc, es2, bufs,
                            of0 if layer == 0 else of1,
                            dr[f"wqkv{layer}"], dr[f"bqkv{layer}"],
                            (dr["xn_g0"], dr["xn_g1"]), qkv, tag="2" + tg,
                            ccx_name="ccx" + tg,
                            prew=prew0 if layer == 0 else None)
                    if layer == 0:
                        emit_barrier(ctx, block, exclude=("ccw",))
                    else:
                        emit_barrier(ctx, block, exclude=PREPX)
                if layer == 0:
                    prep.emit_ccw(block)
                    esp.close()
                    es2a.close()
                with contextlib.ExitStack() as es3:
                    emit_p3(ctx, block, nc, es3, bufs, layer, qkv, aoloc, tag="3" + tg)
                    emit_barrier(ctx, block, exclude=PREPX)
                def _ga(g):
                    ctx.inc("cca" + tg, g.collective_compute(
                        "AllToAll", ALU.bypass, replica_groups=rg,
                        ins=[aoloc[:, :]], outs=[aog[:, :]]), 1)
                block.gpsimd(_ga)
                emit_barrier(ctx, block)
                with contextlib.ExitStack() as es4:
                    brows = [dr["bo0"]] if layer == 0 else [dr["bo1"], dr["bf1"]]
                    emit_p4(ctx, block, nc, es4, bufs, 512 if layer == 0 else 4096,
                            aog, (dr["xn_loc0"], dr["xn_loc1"]), dr[f"wo{layer}t"],
                            dr[f"w{layer}t"], brows, xsrc, xdst,
                            tag="4" + tg, ao_cols_dyn=False)
                    if layer == 0:
                        emit_barrier(ctx, block)
    return nc


def kernel(**inputs):
    pc = host_prep(inputs)
    nc = _get("fused", build_fused)
    keys = ["x0", "bqkv0", "bqkv1", "bo0", "bo1", "bf1", "qw0", "am0",
            "qw1", "zp1", "sc1"]
    in_maps = []
    for c in range(NCORES):
        m = {k: pc[c][k] for k in keys}
        m["wqkv0"] = pc[c]["wqkv0t"]
        m["wqkv1"] = pc[c]["wqkv1t"]
        m["wo0t"] = pc[c]["wo0t"]
        m["wo1t"] = pc[c]["wo1t"]
        in_maps.append(m)
    res = _run(nc, in_maps)
    full = np.empty((TOK, D), np.float32)
    for c in range(NCORES):
        full[TPC * c:TPC * (c + 1)] = res[c]["xout"].T
    return full.reshape(2, SEQ, D)


# version 21
# speedup vs baseline: 1.0927x; 1.0784x over previous
import contextlib
import math
import os
os.environ.setdefault("TRN_ALLOW_ICI_ALL_TO_ALL", "1")
import numpy as np
import ml_dtypes
import concourse.bass as bass
import concourse.mybir as mybir
from concourse.bass_utils import run_bass_kernel_spmd

F32 = mybir.dt.float32
BF16 = mybir.dt.bfloat16
I32 = mybir.dt.int32
AF = mybir.ActivationFunctionType
ALU = mybir.AluOpType
f32 = np.float32

NCORES = 8
D = 4096
TOK = 8192
TPC = TOK // NCORES
SEQ = 4096

NF4 = [-1.0, -0.6961928009986877, -0.5250730514526367, -0.39491748809814453,
       -0.28444138169288635, -0.18477343022823334, -0.09105003625154495, 0.0,
       0.07958029955625534, 0.16093020141124725, 0.24611230194568634,
       0.33791524171829224, 0.44070982933044434, 0.5626170039176941,
       0.7229568362236023, 1.0]

M0 = 75.0
M1 = 12.0
LN_EPS = 1e-5


class Ctx:

    def __init__(self, nc, es):
        self.nc = nc
        self.es = es
        self.hw = []
        self.hwcnt = []
        self.free = []
        self.bind = {}
        self.active = set()
        self.persistent = set(["init", "bar"])
        self.nbar = 0

    def _hw_new(self):
        h = self.es.enter_context(self.nc.semaphore(f"s{len(self.hw)}"))
        self.hw.append(h)
        self.hwcnt.append(0)
        return len(self.hw) - 1

    def _bind(self, name):
        if name not in self.bind:
            idx = self.free.pop() if self.free else self._hw_new()
            self.bind[name] = (idx, self.hwcnt[idx])
            self.active.add(name)
        return self.bind[name]

    def sem(self, name):
        return self.hw[self._bind(name)[0]]

    def inc(self, name, inst, by=1):
        idx, base = self._bind(name)
        inst.then_inc(self.hw[idx], by)
        self.hwcnt[idx] += by
        return inst

    def wait(self, eng, name, value):
        if value > 0:
            idx, base = self._bind(name)
            eng.wait_ge(self.hw[idx], base + value)

    def retire_phase(self):
        for name in list(self.active):
            if name in self.persistent or name.startswith("cc"):
                continue
            self.active.discard(name)
            idx, _ = self.bind.pop(name)
            self.free.append(idx)


def emit_barrier(ctx, block, exclude=()):
    ctx.nbar += 1
    bidx, bbase = ctx._bind("bar")
    k = ctx.hwcnt[bidx] + 1
    excl_idx = set()
    for name in exclude:
        if name in ctx.bind:
            excl_idx.add(ctx.bind[name][0])
    excl_idx.add(bidx)

    def _s(s):
        for idx in range(len(ctx.hw)):
            if idx in excl_idx:
                continue
            if ctx.hwcnt[idx] > 0:
                s.wait_ge(ctx.hw[idx], ctx.hwcnt[idx])
        s.sem_inc(ctx.hw[bidx], 1)
    block.sync(_s)
    ctx.hwcnt[bidx] = k
    for emit in (block.scalar, block.tensor, block.vector, block.gpsimd):
        def _w(e):
            e.wait_ge(ctx.hw[bidx], k)
        emit(_w)
    ctx.retire_phase()


def alloc_init_bufs(nc, es, with_masks):
    bufs = {}
    bufs["ones_col_f32"] = es.enter_context(nc.sbuf_tensor("ones_col_f32", [128, 1], F32))
    bufs["ones_row_f32"] = es.enter_context(nc.sbuf_tensor("ones_row_f32", [1, 128], F32))
    bufs["ones_row_bf"] = es.enter_context(nc.sbuf_tensor("ones_row_bf", [1, 512], BF16))
    bufs["ones_k_bf"] = es.enter_context(nc.sbuf_tensor("ones_k_bf", [128, 128], BF16))
    bufs["ones_k_f32"] = es.enter_context(nc.sbuf_tensor("ones_k_f32", [128, 128], F32))
    bufs["ident"] = es.enter_context(nc.sbuf_tensor("ident", [128, 128], BF16))
    bufs["negident"] = es.enter_context(nc.sbuf_tensor("negident", [128, 128], BF16))
    if with_masks:
        bufs["masks"] = [es.enter_context(nc.sbuf_tensor(f"mask{p}", [128, 512], BF16))
                         for p in range(4)]
    return bufs


def emit_init(ctx, block, bufs, with_masks, dyn_src=None):
    if dyn_src is not None:
        def _s(s):
            ctx.pid = s.partition_id()
            ctx.pid_off = ctx.pid * TPC
            ctx.inc("init0", s.dma_start(out=bufs["ones_row_f32"][0:1, 0:16],
                                         in_=dyn_src[0:1, bass.ts(ctx.pid, 16)]), 16)
        block.sync(_s)

    def _(g):
        if dyn_src is not None:
            ctx.wait(g, "init0", 16)
        g.memset(bufs["ones_col_f32"][:, :], 1.0)
        g.memset(bufs["ones_row_f32"][:, :], 1.0)
        g.memset(bufs["ones_row_bf"][:, :], 1.0)
        g.memset(bufs["ones_k_bf"][:, :], 1.0)
        g.memset(bufs["ones_k_f32"][:, :], 1.0)
        g.memset(bufs["ident"][:, :], 0.0)
        g.memset(bufs["negident"][:, :], 0.0)
        g.drain()
        g.affine_select(out=bufs["ident"][:, :], in_=bufs["ident"][:, :],
                        compare_op=ALU.not_equal, fill=1.0, base=0,
                        pattern=[[-1, 128]], channel_multiplier=1)
        g.affine_select(out=bufs["negident"][:, :], in_=bufs["negident"][:, :],
                        compare_op=ALU.not_equal, fill=-1.0, base=0,
                        pattern=[[-1, 128]], channel_multiplier=1)
        if with_masks:
            for p in range(4):
                g.memset(bufs["masks"][p][:, :], 1.0)
            g.drain()
            for p in range(4):
                g.affine_select(out=bufs["masks"][p][:, :], in_=bufs["masks"][p][:, :],
                                compare_op=ALU.is_ge, fill=0.0, base=-128 * p,
                                pattern=[[1, 512]], channel_multiplier=-1)
        g.drain()
        ctx.inc("init", g.nop(), 1)
    block.gpsimd(_)

    def _sc(sc):
        ctx.wait(sc, "init", 1)
        sc.activation(bufs["ones_col_f32"][0:1, 0:1], bufs["ones_col_f32"][0:1, 0:1],
                      AF.Square)
        sc.drain()
    block.scalar(_sc)


class PrepV:
    def __init__(self, ctx, nc, es, dr, rg, stsem):
        self.ctx, self.nc, self.dr, self.rg, self.stsem = ctx, nc, dr, rg, stsem
        self.qwb = [es.enter_context(nc.sbuf_tensor(f"pv_qw{i}", [128, 4096], I32))
                    for i in range(2)]
        self.qf = es.enter_context(nc.sbuf_tensor("pv_qf", [128, 4096], F32))
        self.acc = es.enter_context(nc.sbuf_tensor("pv_acc", [128, 4096], F32))
        self.tmp = es.enter_context(nc.sbuf_tensor("pv_tmp", [128, 4096], F32))
        self.amb = es.enter_context(nc.sbuf_tensor("pv_amb", [128, 4096], F32))
        self.wout = es.enter_context(nc.sbuf_tensor("pv_w", [128, 4096], BF16))
        ctx.persistent.update({"ppl", "ppst", "ppv", "prep_done", stsem})
        self.nl = 0
        self.qtgt, self.atgt, self.ptgt = {}, {}, {}
        self.ztgt = self.stgt = None

    def _ldq(self, eng, t):
        self.ctx.inc("ppl", eng.dma_start(out=self.qwb[t % 2][:, :],
                     in_=self.dr["qw0"][128 * t:128 * (t + 1), :]), 16)
        self.nl += 16
        self.qtgt[t] = self.nl

    def _lda(self, eng, t):
        for h in range(2):
            self.ctx.inc("ppl", eng.dma_start(
                out=self.amb[64 * h:64 * (h + 1), :],
                in_=self.dr["am0"][2 * t + h:2 * t + h + 1, :].broadcast_to([64, 4096])), 16)
        self.nl += 32
        self.atgt[t] = self.nl

    def emit_loads(self, block):
        def _s(s):
            self._ldq(s, 0)
            self._ldq(s, 1)
            self._lda(s, 0)
        block.sync(_s)

    def emit_rest(self, block):
        ctx, dr = self.ctx, self.dr
        specs = [(0, False), (0, True), (1, False), (2, False)]

        def _sc(sc):
            def ldp(jj):
                ctx.inc("ppl", sc.dma_start(out=self.qwb[jj % 2][:, :],
                         in_=dr["qw1"][specs[jj][0], :, :]), 16)
                self.nl += 16
                self.ptgt[jj] = self.nl
            for k in range(1, 9):
                ctx.wait(sc, "ppv", k)
                if k <= 2:
                    self._ldq(sc, k + 1)
                if k <= 3:
                    self._lda(sc, k)
                if k == 3:
                    ldp(0)
                if k == 4:
                    ldp(1)
                    ctx.inc("ppl", sc.dma_start(out=self.amb[:, :],
                             in_=dr["zp1"].broadcast_to([128, 4096])), 16)
                    self.nl += 16
                    self.ztgt = self.nl
                    ctx.inc("ppl", sc.dma_start(out=self.tmp[:, :],
                             in_=dr["sc1"].broadcast_to([128, 4096])), 16)
                    self.nl += 16
                    self.stgt = self.nl
                if k == 5:
                    ldp(2)
                if k == 6:
                    ldp(3)
                if k <= 4:
                    t = k - 1
                    ctx.inc("ppst", sc.dma_start(
                        out=dr["w0t_loc"][128 * t:128 * (t + 1), :],
                        in_=self.wout[:, :]), 16)
                else:
                    jj = k - 5
                    dst = dr["w1t_loc"].rearrange("(g j) n -> j g n", j=4)[jj]
                    ctx.inc("ppst", sc.dma_start(out=dst, in_=self.wout[:, :]), 16)
        block.scalar(_sc)

        nf4 = [(float(j), float(c)) for j, c in enumerate(NF4) if c != 0.0]

        def _v(v):
            for t in range(4):
                ctx.wait(v, "ppl", self.qtgt[t])
                b = t % 2
                v.tensor_copy(self.qf[:, :], self.qwb[b][:, :])
                v.drain()
                first = True
                for j, c in nf4:
                    dstp = self.acc if first else self.tmp
                    v.tensor_scalar(dstp[:, :], self.qf[:, :], j, c,
                                    ALU.is_equal, ALU.mult)
                    v.drain()
                    if not first:
                        v.tensor_tensor(self.acc[:, :], self.acc[:, :],
                                        self.tmp[:, :], ALU.add)
                        v.drain()
                    first = False
                ctx.wait(v, "ppl", self.atgt[t])
                if t >= 1:
                    ctx.wait(v, "ppst", 16 * t)
                i = v.tensor_tensor(self.wout[:, :], self.acc[:, :],
                                    self.amb[:, :], ALU.mult)
                v.drain()
                ctx.inc("ppv", i, 1)
            for jj, (d, shift) in enumerate(specs):
                ctx.wait(v, "ppl", self.ptgt[jj])
                if jj == 0:
                    ctx.wait(v, "ppl", self.stgt)
                b = jj % 2
                if shift:
                    v.tensor_scalar(self.qwb[b][:, :], self.qwb[b][:, :], 6, 63,
                                    ALU.logical_shift_right, ALU.bitwise_and)
                else:
                    v.tensor_single_scalar(self.qwb[b][:, :], self.qwb[b][:, :],
                                           63, ALU.bitwise_and)
                v.drain()
                v.tensor_copy(self.qf[:, :], self.qwb[b][:, :])
                v.drain()
                v.tensor_tensor(self.qf[:, :], self.qf[:, :], self.amb[:, :],
                                ALU.subtract)
                v.drain()
                ctx.wait(v, "ppst", 16 * (4 + jj))
                i = v.tensor_tensor(self.wout[:, :], self.qf[:, :],
                                    self.tmp[:, :], ALU.mult)
                v.drain()
                ctx.inc("ppv", i, 1)
        block.vector(_v)

        def _g(g):
            ctx.wait(g, self.stsem, 64)
            ctx.inc("ccxa", g.collective_compute(
                "AllGather", ALU.bypass, replica_groups=self.rg,
                ins=[dr["xn_loc0"][:, :]], outs=[dr["xn_g0"][:, :]]), 1)
            ctx.wait(g, self.stsem, 128)
            ctx.inc("ccxa", g.collective_compute(
                "AllGather", ALU.bypass, replica_groups=self.rg,
                ins=[dr["xn_loc1"][:, :]], outs=[dr["xn_g1"][:, :]]), 1)
            ctx.wait(g, "ppst", 16 * 8)
            ctx.inc("prep_done", g.nop(), 1)
        block.gpsimd(_g)

    def emit_ccw(self, block):
        ctx, dr = self.ctx, self.dr

        def _g(g):
            ctx.wait(g, "ppst", 16 * 8)
            ctx.inc("ccw", g.collective_compute(
                "AllGather", ALU.bypass, replica_groups=self.rg,
                ins=[dr["w0t_loc"][:, :]], outs=[dr["w0t"][:, :]]), 1)
            ctx.inc("ccw", g.collective_compute(
                "AllGather", ALU.bypass, replica_groups=self.rg,
                ins=[dr["w1t_loc"][:, :]], outs=[dr["w1t"][:, :]]), 1)
        block.gpsimd(_g)


def emit_prep(ctx, block, nc, es, dr, tag=""):
    qw = es.enter_context(nc.sbuf_tensor(f"pp_qw{tag}", [128, 4096], I32))
    qf = es.enter_context(nc.sbuf_tensor(f"pp_qf{tag}", [128, 4096], F32))
    acc = es.enter_context(nc.sbuf_tensor(f"pp_acc{tag}", [128, 4096], F32))
    tmp = es.enter_context(nc.sbuf_tensor(f"pp_tmp{tag}", [128, 4096], F32))
    amb = es.enter_context(nc.sbuf_tensor(f"pp_amb{tag}", [128, 4096], F32))
    wout = es.enter_context(nc.sbuf_tensor(f"pp_w{tag}", [128, 4096], BF16))
    q3 = [es.enter_context(nc.sbuf_tensor(f"pp_q3_{d}{tag}", [128, 4096], I32)) for d in range(3)]
    zpb = es.enter_context(nc.sbuf_tensor(f"pp_zpb{tag}", [128, 4096], F32))
    scb = es.enter_context(nc.sbuf_tensor(f"pp_scb{tag}", [128, 4096], F32))
    u = es.enter_context(nc.sbuf_tensor(f"pp_u{tag}", [128, 4096], I32))
    uf = es.enter_context(nc.sbuf_tensor(f"pp_uf{tag}", [128, 4096], F32))
    ppl, ppv, ppst = f"ppl{tag}", f"ppv{tag}", f"ppst{tag}"

    def _sync(s):
        for t in range(4):
            if t > 0:
                ctx.wait(s, ppv, t)
            ctx.inc(ppl, s.dma_start(out=qw[:, :],
                     in_=dr["qw0"][128 * t:128 * t + 128, :]), 16)
            for half in range(2):
                ctx.inc(ppl, s.dma_start(
                    out=amb[64 * half:64 * half + 64, :],
                    in_=dr["am0"][2 * t + half:2 * t + half + 1, :].broadcast_to([64, 4096])), 16)
            ctx.wait(s, ppv, t + 1)
            ctx.inc(ppst, s.dma_start(out=dr["w0t_loc"][128 * t:128 * t + 128, :],
                                      in_=wout[:, :]), 16)
        for d in range(3):
            ctx.inc(ppl, s.dma_start(out=q3[d][:, :], in_=dr["qw1"][d, :, :]), 16)
        ctx.inc(ppl, s.dma_start(out=zpb[:, :], in_=dr["zp1"].broadcast_to([128, 4096])), 16)
        ctx.inc(ppl, s.dma_start(out=scb[:, :], in_=dr["sc1"].broadcast_to([128, 4096])), 16)
        for j in range(4):
            ctx.wait(s, ppv, 5 + j)
            dst = dr["w1t_loc"].rearrange("(g j) n -> j g n", j=4)[j]
            ctx.inc(ppst, s.dma_start(out=dst, in_=wout[:, :]), 16)
        ctx.wait(s, ppst, 16 * 8)
        ctx.inc(f"prep_done{tag}", s.nop(), 1)
    block.sync(_sync)

    def _vec(v):
        for t in range(4):
            ctx.wait(v, ppl, 48 * (t + 1))
            if t > 0:
                ctx.wait(v, ppst, 16 * t)
            v.tensor_copy(qf[:, :], qw[:, :])
            v.drain()
            first = True
            for j, c in enumerate(NF4):
                if c == 0.0:
                    continue
                dstp = acc if first else tmp
                v.tensor_scalar(dstp[:, :], qf[:, :], float(j), float(c),
                                ALU.is_equal, ALU.mult)
                v.drain()
                if not first:
                    v.tensor_tensor(acc[:, :], acc[:, :], tmp[:, :], ALU.add)
                    v.drain()
                first = False
            i = v.tensor_tensor(wout[:, :], acc[:, :], amb[:, :], ALU.mult)
            ctx.inc(ppv, i, 1)
            v.drain()
        ctx.wait(v, ppl, 192 + 80)
        specs = [(0, False), (0, True), (1, False), (2, False)]
        for j, (d, shift) in enumerate(specs):
            ctx.wait(v, ppst, 16 * (4 + j))
            if shift:
                v.tensor_scalar(u[:, :], q3[d][:, :], 6, 63,
                                ALU.logical_shift_right, ALU.bitwise_and)
            else:
                v.tensor_single_scalar(u[:, :], q3[d][:, :], 63, ALU.bitwise_and)
            v.drain()
            v.tensor_copy(uf[:, :], u[:, :])
            v.drain()
            v.tensor_tensor(uf[:, :], uf[:, :], zpb[:, :], ALU.subtract)
            v.drain()
            i = v.tensor_tensor(wout[:, :], uf[:, :], scb[:, :], ALU.mult)
            ctx.inc(ppv, i, 1)
            v.drain()
    block.vector(_vec)


def emit_p1(ctx, block, nc, es, bufs, xsrc, xn_locs, tag=""):
    NCH, CT = 8, 128
    xc = [es.enter_context(nc.sbuf_tensor(f"p1_x{i}{tag}", [128, 32 * CT], F32))
          for i in range(2)]
    sq = es.enter_context(nc.sbuf_tensor(f"p1_sq{tag}", [128, 32 * CT], F32))
    xn = es.enter_context(nc.sbuf_tensor(f"p1_xn{tag}", [128, 32 * CT], BF16))
    srow = es.enter_context(nc.sbuf_tensor(f"p1_srow{tag}", [1, CT], F32))
    vrow = es.enter_context(nc.sbuf_tensor(f"p1_vrow{tag}", [1, CT], F32))
    lnrow = es.enter_context(nc.sbuf_tensor(f"p1_lnrow{tag}", [1, CT], F32))
    rrow = es.enter_context(nc.sbuf_tensor(f"p1_rrow{tag}", [1, CT], F32))
    mrow = es.enter_context(nc.sbuf_tensor(f"p1_mrow{tag}", [1, CT], F32))
    rbs = es.enter_context(nc.sbuf_tensor(f"p1_rbs{tag}", [128, CT], F32))
    mbs = es.enter_context(nc.sbuf_tensor(f"p1_mbs{tag}", [128, CT], F32))
    ps_s = es.enter_context(nc.psum_tensor(f"p1_pss{tag}", [1, CT], F32))
    ps_q = es.enter_context(nc.psum_tensor(f"p1_psq{tag}", [1, CT], F32))
    ps_rb = es.enter_context(nc.psum_tensor(f"p1_psrb{tag}", [128, CT], F32))
    ps_mb = es.enter_context(nc.psum_tensor(f"p1_psmb{tag}", [128, CT], F32))
    ld, sqs, mms, var, rst, mr, bc, rbc, ap, st = [f"p1{n}{tag}" for n in
        ["ld", "sq", "mm", "var", "rst", "mr", "bc", "rbc", "ap", "st"]]
    stname = "p1st" + tag
    xcv = [b.rearrange("p (t c) -> p t c", t=32) for b in xc]
    sqv = sq.rearrange("p (t c) -> p t c", t=32)
    xnv = xn.rearrange("p (t c) -> p t c", t=32)

    def bcast(b):
        return b.rearrange("p (o c) -> p o c", o=1).broadcast_to([128, 32, CT])

    def _sync(s):
        for c in range(NCH):
            if c >= 2:
                ctx.wait(s, ap, c - 1)
            src = xsrc[:, CT * c:CT * (c + 1)].rearrange("(t p) c -> p t c", p=128)
            ctx.inc(ld, s.dma_start(out=xcv[c % 2], in_=src), 16)
            if c >= 1:
                ctx.wait(s, ap, c)
                h, sub = divmod(c - 1, NCH // 2)
                dst = xn_locs[h][:, CT * sub:CT * (sub + 1)] \
                    .rearrange("(t p) c -> p t c", p=128)
                ctx.inc(stname, s.dma_start(out=dst, in_=xnv), 16)
        ctx.wait(s, ap, NCH)
        h, sub = divmod(NCH - 1, NCH // 2)
        dst = xn_locs[h][:, CT * sub:CT * (sub + 1)].rearrange("(t p) c -> p t c", p=128)
        ctx.inc(stname, s.dma_start(out=dst, in_=xnv), 16)
        ctx.wait(s, stname, 16 * NCH)
        ctx.inc(f"p1_done{tag}", s.nop(), 1)
    block.sync(_sync)

    def _scalar(sc):
        for c in range(NCH):
            ctx.wait(sc, ld, 16 * (c + 1))
            if c >= 1:
                ctx.wait(sc, mms, c)
            i = sc.activation(sq[:, :], xc[c % 2][:, :], AF.Square)
            ctx.inc(sqs, i, 1)
            ctx.wait(sc, var, c + 1)
            if c >= 1:
                ctx.wait(sc, bc, c)
            sc.activation(lnrow[:, :], vrow[:, :], AF.Ln, bias=f32(LN_EPS))
            sc.drain()
            i = sc.activation(rrow[:, :], lnrow[:, :], AF.Exp, scale=f32(-0.5))
            sc.drain()
            ctx.inc(rst, i, 1)
    block.scalar(_scalar)

    def _tensor(t):
        ctx.wait(t, "init", 1)
        for c in range(NCH):
            ctx.wait(t, ld, 16 * (c + 1))
            ctx.wait(t, sqs, c + 1)
            if c >= 1:
                ctx.wait(t, var, c)
            for f in range(32):
                t.matmul(ps_s[:, :], bufs["ones_col_f32"][:, :], xcv[c % 2][:, f, :],
                         start=(f == 0), stop=(f == 31))
            for f in range(32):
                i = t.matmul(ps_q[:, :], bufs["ones_col_f32"][:, :], sqv[:, f, :],
                             start=(f == 0), stop=(f == 31))
            ctx.inc(mms, i, 1)
            ctx.wait(t, mr, c + 1)
            if c >= 1:
                ctx.wait(t, rbc, c)
            t.matmul(ps_rb[:, :], bufs["ones_row_f32"][:, :], rrow[:, :],
                     start=True, stop=True)
            i = t.matmul(ps_mb[:, :], bufs["ones_row_f32"][:, :], mrow[:, :],
                         start=True, stop=True)
            ctx.inc(bc, i, 1)
    block.tensor(_tensor)

    def _vec(v):
        for c in range(NCH):
            ctx.wait(v, mms, c + 1)
            v.tensor_single_scalar(srow[:, :], ps_s[:, :], 1.0 / D, ALU.mult)
            v.drain()
            v.tensor_single_scalar(vrow[:, :], ps_q[:, :], 1.0 / D, ALU.mult)
            v.drain()
            v.tensor_tensor(mrow[:, :], srow[:, :], srow[:, :], ALU.mult)
            v.drain()
            i = v.tensor_tensor(vrow[:, :], vrow[:, :], mrow[:, :], ALU.subtract)
            v.drain()
            ctx.inc(var, i, 1)
            ctx.wait(v, rst, c + 1)
            i = v.tensor_tensor(mrow[:, :], srow[:, :], rrow[:, :], ALU.mult)
            v.drain()
            ctx.inc(mr, i, 1)
            ctx.wait(v, bc, c + 1)
            v.tensor_copy(rbs[:, :], ps_rb[:, :])
            v.drain()
            i = v.tensor_copy(mbs[:, :], ps_mb[:, :])
            v.drain()
            ctx.inc(rbc, i, 1)
            if c >= 1:
                ctx.wait(v, stname, 16 * c)
            v.tensor_tensor(xcv[c % 2], xcv[c % 2], bcast(rbs), ALU.mult)
            v.drain()
            i = v.tensor_tensor(xnv, xcv[c % 2], bcast(mbs), ALU.subtract)
            v.drain()
            ctx.inc(ap, i, 1)
    block.vector(_vec)


def emit_p2_wload(ctx, block, nc, es, of_tiles, wqkv, bqkv, tag=""):
    nof = len(of_tiles)
    wsb = [es.enter_context(nc.sbuf_tensor(f"p2_w{i}{tag}", [128, 32 * of_tiles[i][1]], BF16))
           for i in range(nof)]
    bsb = es.enter_context(nc.sbuf_tensor(f"p2_b{tag}", [128, nof], F32))
    wl, bl = f"p2wl{tag}", f"p2bl{tag}"
    ctx.persistent.update({wl, bl})

    def _g1(g):
        for i, (r0, w) in enumerate(of_tiles):
            ctx.inc(wl, g.dma_start(out=wsb[i][:, :], in_=wqkv[:, 32 * r0:32 * (r0 + w)]), 16)
        for i, (r0, w) in enumerate(of_tiles):
            ctx.inc(bl, g.dma_start(out=bsb[0:w, i:i + 1], in_=bqkv[r0:r0 + w, :]), 16)
    block.gpsimd(_g1)
    return wsb, bsb


def emit_p2(ctx, block, nc, es, bufs, of_tiles, wqkv, bqkv, xn_gs, qkv_loc, tag="",
            ccx_name=None, prew=None):
    nof = len(of_tiles)
    if prew is None:
        wsb, bsb = emit_p2_wload(ctx, block, nc, es, of_tiles, wqkv, bqkv, tag)
    else:
        wsb, bsb = prew
    xnc = [es.enter_context(nc.sbuf_tensor(f"p2_xn{i}{tag}", [128, 32 * 512], BF16))
           for i in range(2)]
    stg = [es.enter_context(nc.sbuf_tensor(f"p2_stg{i}{tag}", [128, 512], BF16))
           for i in range(4)]
    pss = [es.enter_context(nc.psum_tensor(f"p2_ps{i}{tag}", [128, 512], F32))
           for i in range(4)]
    wl, bl, mm, ev = [f"p2{n}{tag}" for n in ["wl", "bl", "mm", "ev"]]
    xl = [f"p2xl{i}{tag}" for i in range(2)]
    st = [f"p2st{i}{tag}" for i in range(4)]

    chunks = [(r, 0) for r in range(8)] + [(r, 1) for r in range(8)]

    def _sync(s):
        for tp in range(8):
            if tp >= 1:
                ctx.wait(s, mm, tp * nof)
            for h in range(2):
                r, half = chunks[2 * tp + h]
                if ccx_name is not None and (2 * tp + h) in (0, 8):
                    ctx.wait(s, ccx_name, half + 1)
                src = xn_gs[half][r * D:(r + 1) * D, :] \
                    .rearrange("(t p) c -> p t c", p=128)
                ctx.inc(xl[h], s.dma_start(
                    out=xnc[h].rearrange("p (t c) -> p t c", t=32), in_=src), 16)
    block.sync(_sync)

    def _tensor(t):
        for tp in range(8):
            for h in range(2):
                ctx.wait(t, xl[h], 16 * (tp + 1))
            for i, (r0, w) in enumerate(of_tiles):
                k = tp * nof + i
                if tp == 0:
                    ctx.wait(t, wl, 16 * (i + 1))
                if k >= 2:
                    ctx.wait(t, ev, 2 * (k - 1))
                wv = wsb[i].rearrange("p (t o) -> p t o", t=32)
                for f in range(32):
                    for h in range(2):
                        ii = t.matmul(pss[(k % 2) * 2 + h][0:w, :], wv[:, f, :],
                                      xnc[h].rearrange("p (t c) -> p t c", t=32)[:, f, :],
                                      start=(f == 0), stop=(f == 31))
                ctx.inc(mm, ii, 1)
    block.tensor(_tensor)

    def _vec(v):
        ctx.wait(v, bl, 16 * nof)
        for k in range(8 * nof):
            i_of = k % nof
            w = of_tiles[i_of][1]
            ctx.wait(v, mm, k + 1)
            for h in range(2):
                e = 2 * k + h
                if e >= 4:
                    ctx.wait(v, st[e % 4], 16 * (e // 4))
                ii = v.tensor_scalar(stg[e % 4][0:w, :], pss[(k % 2) * 2 + h][0:w, :],
                                     bsb[0:w, i_of:i_of + 1], None, ALU.add)
                v.drain()
                ctx.inc(ev, ii, 1)
    block.vector(_vec)

    def _sc(sc):
        for k in range(8 * nof):
            tp, i_of = divmod(k, nof)
            r0, w = of_tiles[i_of]
            for h in range(2):
                e = 2 * k + h
                r, half = chunks[2 * tp + h]
                col = (2 * r + half) * 512
                ctx.wait(sc, ev, e + 1)
                ctx.inc(st[e % 4], sc.dma_start(
                    out=qkv_loc[r0:r0 + w, col:col + 512],
                    in_=stg[e % 4][0:w, :]), 16)
        for i4 in range(4):
            nst = (16 * nof + 3 - i4) // 4
            ctx.wait(sc, st[i4], 16 * nst)
    block.scalar(_sc)


def emit_p3(ctx, block, nc, es, bufs, layer, qkv_loc, ao_loc, tag=""):
    if layer == 0:
        dh, units = 64, [(0, 0), (1, 0)]
        qrow, krow, vrow = 0, 64, 128
        scale = 1.0
        ebias = -M0
        rec_bias = -math.log(math.sqrt(512.0))
        causal = False
    else:
        dh, units = 128, [(s, hl) for hl in range(4) for s in range(2)]
        qrow, krow, vrow = 0, 512, 1024
        scale = 1.0 / math.sqrt(128.0)
        ebias = -M1
        rec_bias = 0.0
        causal = True

    qsb = [es.enter_context(nc.sbuf_tensor(f"p3_q{i}{tag}", [128, 4096], BF16)) for i in range(2)]
    ksb = [es.enter_context(nc.sbuf_tensor(f"p3_k{i}{tag}", [128, 4096], BF16)) for i in range(2)]
    vsb = [es.enter_context(nc.sbuf_tensor(f"p3_v{i}{tag}", [128, 4096], BF16)) for i in range(2)]
    vt = es.enter_context(nc.sbuf_tensor(f"p3_vt{tag}", [128, 32 * dh], BF16))
    expS = es.enter_context(nc.sbuf_tensor(f"p3_e{tag}", [128, 32 * 512], BF16))
    lnb = [es.enter_context(nc.sbuf_tensor(f"p3_ln{i}{tag}", [128, 512], F32)) for i in range(2)]
    rec = [es.enter_context(nc.sbuf_tensor(f"p3_rec{i}{tag}", [128, 512], F32)) for i in range(2)]
    aos = [es.enter_context(nc.sbuf_tensor(f"p3_ao{i}{tag}", [128, 512], BF16)) for i in range(2)]
    esum = [es.enter_context(nc.sbuf_tensor(f"p3_es{i}{tag}", [128, 512], F32)) for i in range(2)]
    ps_s = [es.enter_context(nc.psum_tensor(f"p3_pss{i}{tag}", [128, 512], F32)) for i in range(4)]
    ps_o = es.enter_context(nc.psum_tensor(f"p3_pso{tag}", [128, 512], F32))
    ps_d = es.enter_context(nc.psum_tensor(f"p3_psd{tag}", [128, 512], F32))
    ps_t = [es.enter_context(nc.psum_tensor(f"p3_pst{i}{tag}", [128, 128], BF16)) for i in range(2)]
    vtm, vte, smm, sev, mk, av, rc, no, ds = [f"p3{n}{tag}" for n in
        ["vtm", "vte", "smm", "sev", "mk", "av", "rc", "no", "ds"]]
    qkvl = [f"p3qkv{i}{tag}" for i in range(2)]
    ast = [f"p3ast{i}{tag}" for i in range(2)]
    vtv = vt.rearrange("p (t d) -> p t d", t=32)
    ev_ = expS.rearrange("p (t c) -> p t c", t=32)
    evt_ = expS.rearrange("p (t c) -> p c t", t=32)
    NU = len(units)

    def kcm(qc):
        return min(32, 4 * (qc + 1)) if causal else 32

    KSUM = sum(kcm(q) for q in range(8))

    def SC(u, qc):
        return u * KSUM + sum(kcm(q) for q in range(qc))

    def QI(u, qc):
        return u * 8 + qc

    if causal:
        def _g(g):
            for u in range(NU):
                for qc in range(8):
                    for p in range(4):
                        kc = 4 * qc + p
                        ctx.wait(g, sev, SC(u, qc) + kc + 1)
                        i = g.affine_select(out=ev_[:, kc, :], in_=ev_[:, kc, :],
                                            compare_op=ALU.is_ge, fill=0.0,
                                            base=-128 * p,
                                            pattern=[[1, 512]], channel_multiplier=-1)
                        ctx.inc(mk, i, 1)
                    g.drain()
        block.gpsimd(_g)

    def _sync(s):
        def load(u):
            sq_, hl = units[u]
            c0 = sq_ * 4096
            ub = u % 2
            for row0, sb in ((qrow, qsb[ub]), (krow, ksb[ub]), (vrow, vsb[ub])):
                ctx.inc(qkvl[ub], s.dma_start(out=sb[0:dh, :],
                         in_=qkv_loc[row0 + hl * dh:row0 + (hl + 1) * dh, c0:c0 + 4096]), 16)
        load(0)
        if NU > 1:
            load(1)
        AOFloc = 64 if layer == 0 else 512
        for u, (sq_, hl) in enumerate(units):
            c0 = sq_ * 4096
            for qc in range(8):
                q = QI(u, qc)
                ctx.wait(s, no, q + 1)
                g0 = c0 + qc * 512
                blk = g0 // 1024
                r0 = blk * AOFloc + hl * dh
                ctx.inc(ast[q % 2], s.dma_start(
                    out=ao_loc[r0:r0 + dh, (g0 % 1024):(g0 % 1024) + 512],
                    in_=aos[q % 2][0:dh, :]), 16)
            if u + 2 < NU:
                ctx.wait(s, av, 8 * (u + 1))
                load(u + 2)
        ctx.wait(s, ast[0], 16 * 4 * NU)
        ctx.wait(s, ast[1], 16 * 4 * NU)
        ctx.inc(f"p3_done{tag}", s.nop(), 1)
    block.sync(_sync)

    def _tensor(t):
        ctx.wait(t, "init", 1)

        def emit_sprime(u, qc, ub):
            km = kcm(qc)
            for kc in range(km):
                n = SC(u, qc) + kc
                if n >= 4:
                    ctx.wait(t, sev, n - 3)
                i = t.matmul(ps_s[n % 4][:, :],
                             ksb[ub][0:dh, kc * 128:(kc + 1) * 128],
                             qsb[ub][0:dh, qc * 512:(qc + 1) * 512],
                             start=True, stop=True)
                ctx.inc(smm, i, 1)

        def emit_av(u, qc, ub):
            q = QI(u, qc)
            km = kcm(qc)
            ctx.wait(t, sev, SC(u, qc) + km)
            if causal:
                ctx.wait(t, mk, 4 * (q + 1))
            if qc == 0:
                ctx.wait(t, vte, 32 * (u + 1))
            ctx.wait(t, rc, q)
            ctx.wait(t, no, q)
            for kc in range(km):
                t.matmul(ps_o[0:dh, :], vtv[:, kc, :], ev_[:, kc, :],
                         start=(kc == 0), stop=(kc == km - 1))
            ctx.wait(t, ds, q + 1)
            i = t.matmul(ps_d[0:dh, :], bufs["ones_k_f32"][:, 0:dh], esum[q % 2][:, :],
                         start=True, stop=True)
            ctx.inc(av, i, 1)

        for u in range(NU):
            ub = u % 2
            ctx.wait(t, qkvl[ub], 48 * (u // 2 + 1))
            for kc in range(32):
                if u > 0 or kc >= 2:
                    ctx.wait(t, vte, 32 * u + kc - 1)
                i = t.transpose(ps_t[kc % 2][:, 0:dh], vsb[ub][0:dh, kc * 128:(kc + 1) * 128],
                                bufs["ident"][0:dh, 0:dh])
                ctx.inc(vtm, i, 1)
            emit_sprime(u, 0, ub)
            for qc in range(1, 8):
                emit_av(u, qc - 1, ub)
                emit_sprime(u, qc, ub)
            emit_av(u, 7, ub)
    block.tensor(_tensor)

    def _scalar(sc):
        for u in range(NU):
            for qc in range(8):
                q = QI(u, qc)
                km = kcm(qc)
                for kc in range(km):
                    n = SC(u, qc) + kc
                    ctx.wait(sc, smm, n + 1)
                    if q > 0 and kc == 0:
                        ctx.wait(sc, av, q)
                    i = sc.activation(ev_[:, kc, :], ps_s[n % 4][:, :], AF.Exp,
                                      scale=f32(scale), bias=f32(ebias))
                    ctx.inc(sev, i, 1)
                ctx.wait(sc, av, q + 1)
                if q >= 2:
                    ctx.wait(sc, no, q - 1)
                sc.activation(lnb[q % 2][0:dh, :], ps_d[0:dh, :], AF.Ln)
                sc.drain()
                i = sc.activation(rec[q % 2][0:dh, :], lnb[q % 2][0:dh, :], AF.Exp,
                                  scale=f32(-1.0), bias=f32(rec_bias))
                sc.drain()
                ctx.inc(rc, i, 1)
    block.scalar(_scalar)

    def _vec(v):
        ctx.wait(v, "init", 1)
        for u, (sq_, hl) in enumerate(units):
            ub = u % 2
            if u > 0:
                ctx.wait(v, av, 8 * u)
            for kc in range(32):
                ctx.wait(v, vtm, 32 * u + kc + 1)
                i = v.tensor_copy(vtv[:, kc, :], ps_t[kc % 2][:, 0:dh])
                v.drain()
                ctx.inc(vte, i, 1)
            for qc in range(8):
                q = QI(u, qc)
                km = kcm(qc)
                if q >= 2:
                    ctx.wait(v, av, q - 1)
                for kc in range(km):
                    if causal and kc == km - 4:
                        ctx.wait(v, mk, 4 * (q + 1))
                    ctx.wait(v, sev, SC(u, qc) + kc + 1)
                    if kc == 0:
                        v.tensor_copy(esum[q % 2][:, :], ev_[:, kc, :])
                    else:
                        v.tensor_tensor(esum[q % 2][:, :], esum[q % 2][:, :],
                                        ev_[:, kc, :], ALU.add)
                    i = v.drain()
                ctx.inc(ds, i, 1)
                ctx.wait(v, rc, q + 1)
                if q >= 2:
                    ctx.wait(v, ast[q % 2], 16 * (q // 2))
                i = v.tensor_tensor(aos[q % 2][0:dh, :], ps_o[0:dh, :],
                                    rec[q % 2][0:dh, :], ALU.mult)
                v.drain()
                ctx.inc(no, i, 1)
    block.vector(_vec)


def emit_p4(ctx, block, nc, es, bufs, AOF, ao_src, xn_locs, woT, wfT, brows,
            xsrc, xdst, tag="", ao_cols_dyn=False):
    na = AOF // 128
    nf = na + 32
    act = es.enter_context(nc.sbuf_tensor(f"p4_act{tag}", [128, nf * 1024], BF16))
    wsb = [es.enter_context(nc.sbuf_tensor(f"p4_w{i}{tag}", [128, nf * 128], BF16)) for i in range(2)]
    brsb = [es.enter_context(nc.sbuf_tensor(f"p4_br{i}{tag}", [1, 4096], BF16))
            for i in range(len(brows))]
    xt = [es.enter_context(nc.sbuf_tensor(f"p4_x{i}{tag}", [128, 1024], F32)) for i in range(2)]
    stg = [es.enter_context(nc.sbuf_tensor(f"p4_s{i}{tag}", [128, 1024], F32)) for i in range(2)]
    pss = [es.enter_context(nc.psum_tensor(f"p4_ps{i}{tag}", [128, 512], F32)) for i in range(2)]
    aol, xnl, brl, mm, ev = [f"p4{n}{tag}" for n in ["ao", "xn", "br", "mm", "ev"]]
    wl = [f"p4wl{i}{tag}" for i in range(2)]
    xld = [f"p4x{i}{tag}" for i in range(2)]
    st = [f"p4st{i}{tag}" for i in range(2)]
    nbr = len(brows)
    actv = act.rearrange("p (t c) -> p t c", t=nf)

    def _g(g):
        for i, br in enumerate(brows):
            ctx.inc(brl, g.dma_start(out=brsb[i][:, :], in_=br[:, :]), 16)
        for f in range(32):
            for hh in range(2):
                ctx.inc(xnl, g.dma_start(
                    out=actv[:, na + f, hh * 512:(hh + 1) * 512],
                    in_=xn_locs[hh][f * 128:(f + 1) * 128, :]), 16)
        for of in range(32):
            if of >= 2:
                ctx.wait(g, mm, 2 * (of - 1))
            wv = wsb[of % 2].rearrange("p (t o) -> p t o", t=nf)
            ctx.inc(wl[of % 2], g.dma_start(
                out=wv[:, 0:na, :],
                in_=woT[:, of * na * 128:(of + 1) * na * 128]
                    .rearrange("p (t o) -> p t o", t=na)), 16)
            for fbg in range(4):
                ctx.inc(wl[of % 2], g.dma_start(
                    out=wv[:, na + fbg * 8:na + (fbg + 1) * 8, :],
                    in_=wfT[fbg * 1024:(fbg + 1) * 1024, of * 128:(of + 1) * 128]
                        .rearrange("(t p) o -> p t o", p=128)), 16)
    block.gpsimd(_g)

    def _sync(s):
        if ao_cols_dyn:
            pid = ctx.pid if getattr(ctx, "pid", None) is not None else s.partition_id()
            srcap = ao_src[:, bass.ts(pid, 1024)].rearrange("(t p) c -> p t c", p=128)
        else:
            srcap = ao_src.rearrange("(t p) c -> p t c", p=128)
        ctx.inc(aol, s.dma_start(out=actv[:, 0:na, :], in_=srcap), 16)
        for of in range(34):
            if of >= 2:
                ctx.wait(s, ev, of - 1)
            if of < 32:
                ctx.inc(xld[of % 2], s.dma_start(out=xt[of % 2][:, :],
                                                 in_=xsrc[of * 128:(of + 1) * 128, :]), 16)
            if of >= 2:
                o2 = of - 2
                ctx.inc(st[o2 % 2], s.dma_start(out=xdst[o2 * 128:(o2 + 1) * 128, :],
                                                in_=stg[o2 % 2][:, :]), 16)
        ctx.wait(s, st[0], 16 * 16)
        ctx.wait(s, st[1], 16 * 16)
        ctx.inc(f"p4_done{tag}", s.nop(), 1)
    block.sync(_sync)

    def _tensor(t):
        ctx.wait(t, "init", 1)
        ctx.wait(t, aol, 16)
        ctx.wait(t, xnl, 16 * 64)
        ctx.wait(t, brl, 16 * nbr)
        for of in range(32):
            ctx.wait(t, wl[of % 2], 80 * (of // 2 + 1))
            if of >= 1:
                ctx.wait(t, ev, of)
            wv = wsb[of % 2].rearrange("p (t o) -> p t o", t=nf)
            for f in range(nf):
                for tc in range(2):
                    t.matmul(pss[tc][:, :], wv[:, f, :],
                             actv[:, f, tc * 512:(tc + 1) * 512],
                             start=(f == 0), stop=False)
            for tc in range(2):
                for i in range(nbr):
                    ii = t.matmul(pss[tc][:, :],
                                  brsb[i][:, of * 128:(of + 1) * 128],
                                  bufs["ones_row_bf"][:, :],
                                  start=False, stop=(i == nbr - 1))
            ctx.inc(mm, ii, 2)
    block.tensor(_tensor)

    def _vec(v):
        for of in range(32):
            ctx.wait(v, mm, 2 * (of + 1))
            ctx.wait(v, xld[of % 2], 16 * (of // 2 + 1))
            if of >= 2:
                ctx.wait(v, st[of % 2], 16 * (of // 2))
            for tc in range(2):
                i = v.tensor_tensor(stg[of % 2][:, tc * 512:(tc + 1) * 512],
                                    pss[tc][:, :], xt[of % 2][:, tc * 512:(tc + 1) * 512],
                                    ALU.add)
                v.drain()
            ctx.inc(ev, i, 1)
    block.vector(_vec)


def host_prep(inp):
    out = [dict() for _ in range(NCORES)]
    x = np.ascontiguousarray(np.asarray(inp["L0_attn_x"], np.float32).reshape(TOK, D))
    for c in range(NCORES):
        out[c]["x0"] = np.ascontiguousarray(x[TPC * c:TPC * (c + 1)].T)

    def _tile_w(wT, widths):
        parts = []
        r0 = 0
        for w in widths:
            parts.append(wT.reshape(32, 128, -1)[:, :, r0:r0 + w]
                         .transpose(1, 0, 2).reshape(128, 32 * w))
            r0 += w
        return np.ascontiguousarray(np.concatenate(parts, 1))

    def _tile_of(wT):
        na = wT.shape[0] // 128
        return np.ascontiguousarray(
            wT.reshape(na, 128, 32, 128).transpose(1, 2, 0, 3).reshape(128, 32 * na * 128))

    wqk = np.asarray(inp["L0_attn_Wqk_w"], np.float32)
    bqk = np.asarray(inp["L0_attn_Wqk_b"], np.float32)
    wv0 = np.asarray(inp["L0_attn_Wv_w"], np.float32)
    bv0 = np.asarray(inp["L0_attn_Wv_b"], np.float32)
    for c in range(NCORES):
        qr = [128 * c + 2 * e for e in range(64)]
        kr = [128 * c + 2 * e + 1 for e in range(64)]
        w = np.concatenate([wqk[qr], wqk[kr], wv0[64 * c:64 * c + 64]], 0)
        b = np.concatenate([bqk[qr], bqk[kr], bv0[64 * c:64 * c + 64]], 0)
        out[c]["wqkv0"] = np.ascontiguousarray(w.T)
        out[c]["wqkv0t"] = _tile_w(w.T, [128, 64])
        out[c]["bqkv0"] = np.ascontiguousarray(b[:, None])

    wq = np.asarray(inp["L1_attn_Wq"], np.float32)
    wk = np.asarray(inp["L1_attn_Wk"], np.float32)
    wv = np.asarray(inp["L1_attn_Wv"], np.float32)
    bq = np.asarray(inp["L1_attn_bq"], np.float32)
    bk = np.asarray(inp["L1_attn_bk"], np.float32)
    bv = np.asarray(inp["L1_attn_bv"], np.float32)
    for c in range(NCORES):
        sl = slice(512 * c, 512 * c + 512)
        w = np.concatenate([wq[sl], wk[sl], wv[sl]], 0)
        b = np.concatenate([bq[sl], bk[sl], bv[sl]], 0)
        out[c]["wqkv1"] = np.ascontiguousarray(w.T)
        out[c]["wqkv1t"] = _tile_w(w.T, [128] * 12)
        out[c]["bqkv1"] = np.ascontiguousarray(b[:, None])

    wo0T = np.ascontiguousarray(np.asarray(inp["L0_attn_Wo_w"], np.float32).T)
    bo0 = np.ascontiguousarray(np.asarray(inp["L0_attn_Wo_b"], np.float32)[None, :])
    wo1T = np.ascontiguousarray(np.asarray(inp["L1_attn_Wo"], np.float32).T)
    bo1 = np.ascontiguousarray(np.asarray(inp["L1_attn_bo"], np.float32)[None, :])
    bf1 = np.ascontiguousarray(np.asarray(inp["L1_ffn_bias"], np.float32)[None, :])
    wo0t = _tile_of(wo0T)
    wo1t = _tile_of(wo1T)
    for c in range(NCORES):
        out[c]["wo0T"] = wo0T
        out[c]["wo0t"] = wo0t
        out[c]["bo0"] = bo0
        out[c]["wo1T"] = wo1T
        out[c]["wo1t"] = wo1t
        out[c]["bo1"] = bo1
        out[c]["bf1"] = bf1

    qw0T = np.asarray(inp["L0_ffn_qweight"], np.int32).reshape(D, D).T
    am0T = np.asarray(inp["L0_ffn_absmax"], np.float32).reshape(D, 64).T
    qw1 = np.asarray(inp["L1_ffn_qweight"], np.int32)
    zp1 = np.ascontiguousarray(np.asarray(inp["L1_ffn_weight_zero_point"], np.float32).reshape(1, D))
    sc1 = np.ascontiguousarray(np.asarray(inp["L1_ffn_weight_scale"], np.float32).reshape(1, D))
    for c in range(NCORES):
        out[c]["qw0"] = np.ascontiguousarray(qw0T[512 * c:512 * c + 512])
        out[c]["am0"] = np.ascontiguousarray(am0T[8 * c:8 * c + 8])
        sl = np.ascontiguousarray(qw1[:, 384 * c:384 * c + 384].T)
        out[c]["qw1"] = np.ascontiguousarray(np.stack([sl[0::3], sl[1::3], sl[2::3]], 0))
        out[c]["zp1"] = zp1
        out[c]["sc1"] = sc1
    return out


def _mk_nc():
    return bass.Bass("TRN2", target_bir_lowering=False)


def build_prep():
    nc = _mk_nc()
    dr = dict(
        qw0=nc.dram_tensor("qw0", [512, 4096], I32, kind="ExternalInput").ap(),
        am0=nc.dram_tensor("am0", [8, 4096], F32, kind="ExternalInput").ap(),
        qw1=nc.dram_tensor("qw1", [3, 128, 4096], I32, kind="ExternalInput").ap(),
        zp1=nc.dram_tensor("zp1", [1, 4096], F32, kind="ExternalInput").ap(),
        sc1=nc.dram_tensor("sc1", [1, 4096], F32, kind="ExternalInput").ap(),
        w0t_loc=nc.dram_tensor("w0t_loc", [512, 4096], BF16, kind="ExternalOutput").ap(),
        w1t_loc=nc.dram_tensor("w1t_loc", [512, 4096], BF16, kind="ExternalOutput").ap(),
    )
    with contextlib.ExitStack() as es:
        ctx = Ctx(nc, es)
        with nc.Block() as block:
            emit_prep(ctx, block, nc, es, dr)
    return nc


def build_p1():
    nc = _mk_nc()
    xsrc = nc.dram_tensor("xsrc", [D, TPC], F32, kind="ExternalInput").ap()
    xn0 = nc.dram_tensor("xn0", [D, 512], BF16, kind="ExternalOutput").ap()
    xn1 = nc.dram_tensor("xn1", [D, 512], BF16, kind="ExternalOutput").ap()
    with contextlib.ExitStack() as es:
        ctx = Ctx(nc, es)
        bufs = alloc_init_bufs(nc, es, False)
        with nc.Block() as block:
            emit_init(ctx, block, bufs, False)
            emit_p1(ctx, block, nc, es, bufs, xsrc, (xn0, xn1))
    return nc


def build_p2(layer):
    nc = _mk_nc()
    OF = 192 if layer == 0 else 1536
    of_tiles = [(0, 128), (128, 64)] if layer == 0 else [(i * 128, 128) for i in range(12)]
    wqkv = nc.dram_tensor("wqkv", [128, 32 * OF], F32, kind="ExternalInput").ap()
    bqkv = nc.dram_tensor("bqkv", [OF, 1], F32, kind="ExternalInput").ap()
    xn_g0 = nc.dram_tensor("xn_g0", [NCORES * D, 512], BF16, kind="ExternalInput").ap()
    xn_g1 = nc.dram_tensor("xn_g1", [NCORES * D, 512], BF16, kind="ExternalInput").ap()
    qkv_loc = nc.dram_tensor("qkv_loc", [OF, TOK], BF16, kind="ExternalOutput").ap()
    with contextlib.ExitStack() as es:
        ctx = Ctx(nc, es)
        bufs = alloc_init_bufs(nc, es, False)
        with nc.Block() as block:
            emit_init(ctx, block, bufs, False)
            emit_p2(ctx, block, nc, es, bufs, of_tiles, wqkv, bqkv, (xn_g0, xn_g1), qkv_loc)
    return nc


def build_p3(layer):
    nc = _mk_nc()
    OF = 192 if layer == 0 else 1536
    AOF = 64 if layer == 0 else 512
    qkv_loc = nc.dram_tensor("qkv_loc", [OF, TOK], BF16, kind="ExternalInput").ap()
    ao_loc = nc.dram_tensor("ao_loc", [8 * AOF, TPC], BF16, kind="ExternalOutput").ap()
    with contextlib.ExitStack() as es:
        ctx = Ctx(nc, es)
        bufs = alloc_init_bufs(nc, es, layer == 1)
        with nc.Block() as block:
            emit_init(ctx, block, bufs, layer == 1)
            emit_p3(ctx, block, nc, es, bufs, layer, qkv_loc, ao_loc)
    return nc


def build_p4(layer):
    nc = _mk_nc()
    AOF = 512 if layer == 0 else 4096
    nbr = 1 if layer == 0 else 2
    na = AOF // 128
    ao = nc.dram_tensor("ao", [AOF, TPC], BF16, kind="ExternalInput").ap()
    xn0 = nc.dram_tensor("xn0", [D, 512], BF16, kind="ExternalInput").ap()
    xn1 = nc.dram_tensor("xn1", [D, 512], BF16, kind="ExternalInput").ap()
    woT = nc.dram_tensor("woT", [128, 32 * na * 128], F32, kind="ExternalInput").ap()
    wfT = nc.dram_tensor("wfT", [D, D], BF16, kind="ExternalInput").ap()
    brows = [nc.dram_tensor(f"br{i}", [1, D], F32, kind="ExternalInput").ap()
             for i in range(nbr)]
    xsrc = nc.dram_tensor("xsrc", [D, TPC], F32, kind="ExternalInput").ap()
    xdst = nc.dram_tensor("xdst", [D, TPC], F32, kind="ExternalOutput").ap()
    with contextlib.ExitStack() as es:
        ctx = Ctx(nc, es)
        bufs = alloc_init_bufs(nc, es, False)
        with nc.Block() as block:
            emit_init(ctx, block, bufs, False)
            emit_p4(ctx, block, nc, es, bufs, AOF, ao, (xn0, xn1), woT, wfT, brows,
                    xsrc, xdst, ao_cols_dyn=False)
    return nc


_NC_CACHE = {}


def _get(name, fn, *a):
    key = (name,) + a
    if key not in _NC_CACHE:
        _NC_CACHE[key] = fn(*a)
    return _NC_CACHE[key]


def _run(nc, in_maps):
    return run_bass_kernel_spmd(nc, in_maps, core_ids=list(range(NCORES))).results


def kernel_unfused(**inputs):
    pc = host_prep(inputs)

    nc = _get("prep", build_prep)
    res = _run(nc, [{k: pc[c][k] for k in ["qw0", "am0", "qw1", "zp1", "sc1"]}
                    for c in range(NCORES)])
    w0t = np.concatenate([res[c]["w0t_loc"] for c in range(NCORES)], 0)
    w1t = np.concatenate([res[c]["w1t_loc"] for c in range(NCORES)], 0)

    xcur = [pc[c]["x0"] for c in range(NCORES)]
    for layer in range(2):
        nc = _get("p1", build_p1)
        res = _run(nc, [{"xsrc": xcur[c]} for c in range(NCORES)])
        xn0 = [res[c]["xn0"] for c in range(NCORES)]
        xn1 = [res[c]["xn1"] for c in range(NCORES)]
        xn_g0 = np.concatenate(xn0, 0)
        xn_g1 = np.concatenate(xn1, 0)

        nc = _get("p2", build_p2, layer)
        res = _run(nc, [{"wqkv": pc[c][f"wqkv{layer}t"], "bqkv": pc[c][f"bqkv{layer}"],
                         "xn_g0": xn_g0, "xn_g1": xn_g1} for c in range(NCORES)])
        qkv = [res[c]["qkv_loc"] for c in range(NCORES)]

        nc = _get("p3", build_p3, layer)
        res = _run(nc, [{"qkv_loc": qkv[c]} for c in range(NCORES)])
        AOFloc = 64 if layer == 0 else 512
        ao_per_core = [np.concatenate([res[i]["ao_loc"][c * AOFloc:(c + 1) * AOFloc]
                                       for i in range(NCORES)], 0) for c in range(NCORES)]

        nc = _get("p4", build_p4, layer)
        wf = w0t if layer == 0 else w1t
        in_maps = []
        for c in range(NCORES):
            m = {"ao": ao_per_core[c],
                 "xn0": xn0[c], "xn1": xn1[c], "woT": pc[c][f"wo{layer}t"], "wfT": wf,
                 "xsrc": xcur[c]}
            if layer == 0:
                m["br0"] = pc[c]["bo0"]
            else:
                m["br0"] = pc[c]["bo1"]
                m["br1"] = pc[c]["bf1"]
            in_maps.append(m)
        res = _run(nc, in_maps)
        xcur = [res[c]["xdst"] for c in range(NCORES)]

    full = np.empty((TOK, D), np.float32)
    for c in range(NCORES):
        full[TPC * c:TPC * (c + 1)] = xcur[c].T
    return full.reshape(2, SEQ, D)


def build_fused():
    nc = _mk_nc()
    ei, eo = "ExternalInput", "ExternalOutput"
    dr = {}
    dr["x0"] = nc.dram_tensor("x0", [D, TPC], F32, kind=ei).ap()
    dr["wqkv0"] = nc.dram_tensor("wqkv0", [128, 32 * 192], F32, kind=ei).ap()
    dr["bqkv0"] = nc.dram_tensor("bqkv0", [192, 1], F32, kind=ei).ap()
    dr["wqkv1"] = nc.dram_tensor("wqkv1", [128, 32 * 1536], F32, kind=ei).ap()
    dr["bqkv1"] = nc.dram_tensor("bqkv1", [1536, 1], F32, kind=ei).ap()
    dr["wo0t"] = nc.dram_tensor("wo0t", [128, 32 * 4 * 128], F32, kind=ei).ap()
    dr["wo1t"] = nc.dram_tensor("wo1t", [128, 32 * 32 * 128], F32, kind=ei).ap()
    dr["bo0"] = nc.dram_tensor("bo0", [1, D], F32, kind=ei).ap()
    dr["bo1"] = nc.dram_tensor("bo1", [1, D], F32, kind=ei).ap()
    dr["bf1"] = nc.dram_tensor("bf1", [1, D], F32, kind=ei).ap()
    dr["qw0"] = nc.dram_tensor("qw0", [512, 4096], I32, kind=ei).ap()
    dr["am0"] = nc.dram_tensor("am0", [8, 4096], F32, kind=ei).ap()
    dr["qw1"] = nc.dram_tensor("qw1", [3, 128, 4096], I32, kind=ei).ap()
    dr["zp1"] = nc.dram_tensor("zp1", [1, 4096], F32, kind=ei).ap()
    dr["sc1"] = nc.dram_tensor("sc1", [1, 4096], F32, kind=ei).ap()
    dr["xout"] = nc.dram_tensor("xout", [D, TPC], F32, kind=eo).ap()
    dr["w0t_loc"] = nc.dram_tensor("w0t_loc", [512, 4096], BF16).ap()
    dr["w1t_loc"] = nc.dram_tensor("w1t_loc", [512, 4096], BF16).ap()
    dr["w0t"] = nc.dram_tensor("w0t", [D, D], BF16).ap()
    dr["w1t"] = nc.dram_tensor("w1t", [D, D], BF16).ap()
    dr["xn_loc0"] = nc.dram_tensor("xn_loc0", [D, 512], BF16).ap()
    dr["xn_loc1"] = nc.dram_tensor("xn_loc1", [D, 512], BF16).ap()
    dr["xn_g0"] = nc.dram_tensor("xn_g0", [NCORES * D, 512], BF16).ap()
    dr["xn_g1"] = nc.dram_tensor("xn_g1", [NCORES * D, 512], BF16).ap()
    dr["qkv0"] = nc.dram_tensor("qkv0", [192, TOK], BF16).ap()
    dr["qkv1"] = nc.dram_tensor("qkv1", [1536, TOK], BF16).ap()
    dr["ao0_loc"] = nc.dram_tensor("ao0_loc", [512, TPC], BF16).ap()
    dr["ao0_g"] = nc.dram_tensor("ao0_g", [512, TPC], BF16).ap()
    dr["ao1_loc"] = nc.dram_tensor("ao1_loc", [4096, TPC], BF16).ap()
    dr["ao1_g"] = nc.dram_tensor("ao1_g", [4096, TPC], BF16).ap()
    dr["xmid"] = nc.dram_tensor("xmid", [D, TPC], F32).ap()

    of0 = [(0, 128), (128, 64)]
    of1 = [(i * 128, 128) for i in range(12)]
    rg = [list(range(NCORES))]

    with contextlib.ExitStack() as es:
        ctx = Ctx(nc, es)
        bufs = alloc_init_bufs(nc, es, False)
        with nc.Block() as block:
            emit_init(ctx, block, bufs, False, dyn_src=dr["x0"])
            es2a = contextlib.ExitStack()
            prew0 = emit_p2_wload(ctx, block, nc, es2a, of0,
                                  dr["wqkv0"], dr["bqkv0"], tag="2a")
            esp = contextlib.ExitStack()
            prep = PrepV(ctx, nc, esp, dr, rg, "p1st1a")
            prep.emit_loads(block)
            PREPX = ("ccw", "ppl", "ppst", "ppv", "prep_done")

            for layer, tg in ((0, "a"), (1, "b")):
                xsrc = dr["x0"] if layer == 0 else dr["xmid"]
                xdst = dr["xmid"] if layer == 0 else dr["xout"]
                qkv = dr[f"qkv{layer}"]
                aoloc = dr[f"ao{layer}_loc"]
                aog = dr[f"ao{layer}_g"]
                with contextlib.ExitStack() as es1:
                    emit_p1(ctx, block, nc, es1, bufs, xsrc,
                            (dr["xn_loc0"], dr["xn_loc1"]), tag="1" + tg)
                    if layer == 0:
                        prep.emit_rest(block)
                    else:
                        def _gx(g):
                            for hh in range(2):
                                ctx.wait(g, f"p1st1{tg}", 64 * (hh + 1))
                                ctx.inc("ccx" + tg, g.collective_compute(
                                    "AllGather", ALU.bypass, replica_groups=rg,
                                    ins=[dr[f"xn_loc{hh}"][:, :]],
                                    outs=[dr[f"xn_g{hh}"][:, :]]), 1)
                        block.gpsimd(_gx)
                    emit_barrier(ctx, block, exclude=PREPX + ("ccx" + tg,))
                with contextlib.ExitStack() as es2:
                    emit_p2(ctx, block, nc, es2, bufs,
                            of0 if layer == 0 else of1,
                            dr[f"wqkv{layer}"], dr[f"bqkv{layer}"],
                            (dr["xn_g0"], dr["xn_g1"]), qkv, tag="2" + tg,
                            ccx_name="ccx" + tg,
                            prew=prew0 if layer == 0 else None)
                    if layer == 0:
                        emit_barrier(ctx, block, exclude=("ccw",))
                    else:
                        emit_barrier(ctx, block, exclude=PREPX)
                if layer == 0:
                    prep.emit_ccw(block)
                    esp.close()
                    es2a.close()
                with contextlib.ExitStack() as es3:
                    emit_p3(ctx, block, nc, es3, bufs, layer, qkv, aoloc, tag="3" + tg)
                    emit_barrier(ctx, block, exclude=PREPX)
                def _ga(g):
                    ctx.inc("cca" + tg, g.collective_compute(
                        "AllToAll", ALU.bypass, replica_groups=rg,
                        ins=[aoloc[:, :]], outs=[aog[:, :]]), 1)
                block.gpsimd(_ga)
                emit_barrier(ctx, block)
                with contextlib.ExitStack() as es4:
                    brows = [dr["bo0"]] if layer == 0 else [dr["bo1"], dr["bf1"]]
                    emit_p4(ctx, block, nc, es4, bufs, 512 if layer == 0 else 4096,
                            aog, (dr["xn_loc0"], dr["xn_loc1"]), dr[f"wo{layer}t"],
                            dr[f"w{layer}t"], brows, xsrc, xdst,
                            tag="4" + tg, ao_cols_dyn=False)
                    if layer == 0:
                        emit_barrier(ctx, block)
    return nc


def kernel(**inputs):
    pc = host_prep(inputs)
    nc = _get("fused", build_fused)
    keys = ["x0", "bqkv0", "bqkv1", "bo0", "bo1", "bf1", "qw0", "am0",
            "qw1", "zp1", "sc1"]
    in_maps = []
    for c in range(NCORES):
        m = {k: pc[c][k] for k in keys}
        m["wqkv0"] = pc[c]["wqkv0t"]
        m["wqkv1"] = pc[c]["wqkv1t"]
        m["wo0t"] = pc[c]["wo0t"]
        m["wo1t"] = pc[c]["wo1t"]
        in_maps.append(m)
    res = _run(nc, in_maps)
    full = np.empty((TOK, D), np.float32)
    for c in range(NCORES):
        full[TPC * c:TPC * (c + 1)] = res[c]["xout"].T
    return full.reshape(2, SEQ, D)


# version 22 (final)
# speedup vs baseline: 1.1290x; 1.0333x over previous
import contextlib
import math
import os
os.environ.setdefault("TRN_ALLOW_ICI_ALL_TO_ALL", "1")
import numpy as np
import ml_dtypes
import concourse.bass as bass
import concourse.mybir as mybir
from concourse.bass_utils import run_bass_kernel_spmd

F32 = mybir.dt.float32
BF16 = mybir.dt.bfloat16
I32 = mybir.dt.int32
AF = mybir.ActivationFunctionType
ALU = mybir.AluOpType
f32 = np.float32

NCORES = 8
D = 4096
TOK = 8192
TPC = TOK // NCORES
SEQ = 4096

NF4 = [-1.0, -0.6961928009986877, -0.5250730514526367, -0.39491748809814453,
       -0.28444138169288635, -0.18477343022823334, -0.09105003625154495, 0.0,
       0.07958029955625534, 0.16093020141124725, 0.24611230194568634,
       0.33791524171829224, 0.44070982933044434, 0.5626170039176941,
       0.7229568362236023, 1.0]

M0 = 75.0
M1 = 12.0
LN_EPS = 1e-5


class Ctx:

    def __init__(self, nc, es):
        self.nc = nc
        self.es = es
        self.hw = []
        self.hwcnt = []
        self.free = []
        self.bind = {}
        self.active = set()
        self.persistent = set(["init", "bar"])
        self.nbar = 0

    def _hw_new(self):
        h = self.es.enter_context(self.nc.semaphore(f"s{len(self.hw)}"))
        self.hw.append(h)
        self.hwcnt.append(0)
        return len(self.hw) - 1

    def _bind(self, name):
        if name not in self.bind:
            idx = self.free.pop() if self.free else self._hw_new()
            self.bind[name] = (idx, self.hwcnt[idx])
            self.active.add(name)
        return self.bind[name]

    def sem(self, name):
        return self.hw[self._bind(name)[0]]

    def inc(self, name, inst, by=1):
        idx, base = self._bind(name)
        inst.then_inc(self.hw[idx], by)
        self.hwcnt[idx] += by
        return inst

    def wait(self, eng, name, value):
        if value > 0:
            idx, base = self._bind(name)
            eng.wait_ge(self.hw[idx], base + value)

    def retire_phase(self):
        for name in list(self.active):
            if name in self.persistent or name.startswith("cc"):
                continue
            self.active.discard(name)
            idx, _ = self.bind.pop(name)
            self.free.append(idx)


def emit_barrier(ctx, block, exclude=()):
    ctx.nbar += 1
    bidx, bbase = ctx._bind("bar")
    k = ctx.hwcnt[bidx] + 1
    excl_idx = set()
    for name in exclude:
        if name in ctx.bind:
            excl_idx.add(ctx.bind[name][0])
    excl_idx.add(bidx)

    def _s(s):
        for idx in range(len(ctx.hw)):
            if idx in excl_idx:
                continue
            if ctx.hwcnt[idx] > 0:
                s.wait_ge(ctx.hw[idx], ctx.hwcnt[idx])
        s.sem_inc(ctx.hw[bidx], 1)
    block.sync(_s)
    ctx.hwcnt[bidx] = k
    for emit in (block.scalar, block.tensor, block.vector, block.gpsimd):
        def _w(e):
            e.wait_ge(ctx.hw[bidx], k)
        emit(_w)
    ctx.retire_phase()


def alloc_init_bufs(nc, es, with_masks):
    bufs = {}
    bufs["ones_col_f32"] = es.enter_context(nc.sbuf_tensor("ones_col_f32", [128, 1], F32))
    bufs["ones_row_f32"] = es.enter_context(nc.sbuf_tensor("ones_row_f32", [1, 128], F32))
    bufs["ones_row_bf"] = es.enter_context(nc.sbuf_tensor("ones_row_bf", [1, 512], BF16))
    bufs["ones_k_bf"] = es.enter_context(nc.sbuf_tensor("ones_k_bf", [128, 128], BF16))
    bufs["ones_k_f32"] = es.enter_context(nc.sbuf_tensor("ones_k_f32", [128, 128], F32))
    bufs["ident"] = es.enter_context(nc.sbuf_tensor("ident", [128, 128], BF16))
    bufs["negident"] = es.enter_context(nc.sbuf_tensor("negident", [128, 128], BF16))
    if with_masks:
        bufs["masks"] = [es.enter_context(nc.sbuf_tensor(f"mask{p}", [128, 512], BF16))
                         for p in range(4)]
    return bufs


def emit_init(ctx, block, bufs, with_masks, dyn_src=None):
    if dyn_src is not None:
        def _s(s):
            ctx.pid = s.partition_id()
            ctx.pid_off = ctx.pid * TPC
            ctx.inc("init0", s.dma_start(out=bufs["ones_row_f32"][0:1, 0:16],
                                         in_=dyn_src[0:1, bass.ts(ctx.pid, 16)]), 16)
        block.sync(_s)

    def _(g):
        if dyn_src is not None:
            ctx.wait(g, "init0", 16)
        g.memset(bufs["ones_col_f32"][:, :], 1.0)
        g.memset(bufs["ones_row_f32"][:, :], 1.0)
        g.memset(bufs["ones_row_bf"][:, :], 1.0)
        g.memset(bufs["ones_k_bf"][:, :], 1.0)
        g.memset(bufs["ones_k_f32"][:, :], 1.0)
        g.memset(bufs["ident"][:, :], 0.0)
        g.memset(bufs["negident"][:, :], 0.0)
        g.drain()
        g.affine_select(out=bufs["ident"][:, :], in_=bufs["ident"][:, :],
                        compare_op=ALU.not_equal, fill=1.0, base=0,
                        pattern=[[-1, 128]], channel_multiplier=1)
        g.affine_select(out=bufs["negident"][:, :], in_=bufs["negident"][:, :],
                        compare_op=ALU.not_equal, fill=-1.0, base=0,
                        pattern=[[-1, 128]], channel_multiplier=1)
        if with_masks:
            for p in range(4):
                g.memset(bufs["masks"][p][:, :], 1.0)
            g.drain()
            for p in range(4):
                g.affine_select(out=bufs["masks"][p][:, :], in_=bufs["masks"][p][:, :],
                                compare_op=ALU.is_ge, fill=0.0, base=-128 * p,
                                pattern=[[1, 512]], channel_multiplier=-1)
        g.drain()
        ctx.inc("init", g.nop(), 1)
    block.gpsimd(_)

    def _sc(sc):
        ctx.wait(sc, "init", 1)
        sc.activation(bufs["ones_col_f32"][0:1, 0:1], bufs["ones_col_f32"][0:1, 0:1],
                      AF.Square)
        sc.drain()
    block.scalar(_sc)


class PrepV:
    def __init__(self, ctx, nc, es, dr, rg, stsem):
        self.ctx, self.nc, self.dr, self.rg, self.stsem = ctx, nc, dr, rg, stsem
        self.qwb = [es.enter_context(nc.sbuf_tensor(f"pv_qw{i}", [128, 4096], I32))
                    for i in range(2)]
        self.qf = es.enter_context(nc.sbuf_tensor("pv_qf", [128, 4096], F32))
        self.acc = es.enter_context(nc.sbuf_tensor("pv_acc", [128, 4096], F32))
        self.tmp = es.enter_context(nc.sbuf_tensor("pv_tmp", [128, 4096], F32))
        self.amb = es.enter_context(nc.sbuf_tensor("pv_amb", [128, 4096], F32))
        self.wout = es.enter_context(nc.sbuf_tensor("pv_w", [128, 4096], BF16))
        ctx.persistent.update({"ppl", "ppst", "ppv", "prep_done", stsem})
        self.nl = 0
        self.qtgt, self.atgt, self.ptgt = {}, {}, {}
        self.ztgt = self.stgt = None

    def _ldq(self, eng, t):
        self.ctx.inc("ppl", eng.dma_start(out=self.qwb[t % 2][:, :],
                     in_=self.dr["qw0"][128 * t:128 * (t + 1), :]), 16)
        self.nl += 16
        self.qtgt[t] = self.nl

    def _lda(self, eng, t):
        for h in range(2):
            self.ctx.inc("ppl", eng.dma_start(
                out=self.amb[64 * h:64 * (h + 1), :],
                in_=self.dr["am0"][2 * t + h:2 * t + h + 1, :].broadcast_to([64, 4096])), 16)
        self.nl += 32
        self.atgt[t] = self.nl

    def emit_loads(self, block):
        def _s(s):
            self._ldq(s, 0)
            self._ldq(s, 1)
            self._lda(s, 0)
        block.sync(_s)

    def emit_rest(self, block):
        ctx, dr = self.ctx, self.dr
        specs = [(0, False), (0, True), (1, False), (2, False)]

        def _sc(sc):
            def ldp(jj):
                ctx.inc("ppl", sc.dma_start(out=self.qwb[jj % 2][:, :],
                         in_=dr["qw1"][specs[jj][0], :, :]), 16)
                self.nl += 16
                self.ptgt[jj] = self.nl
            for k in range(1, 9):
                ctx.wait(sc, "ppv", k)
                if k <= 2:
                    self._ldq(sc, k + 1)
                if k <= 3:
                    self._lda(sc, k)
                if k == 3:
                    ldp(0)
                if k == 4:
                    ldp(1)
                    ctx.inc("ppl", sc.dma_start(out=self.amb[:, :],
                             in_=dr["zp1"].broadcast_to([128, 4096])), 16)
                    self.nl += 16
                    self.ztgt = self.nl
                    ctx.inc("ppl", sc.dma_start(out=self.tmp[:, :],
                             in_=dr["sc1"].broadcast_to([128, 4096])), 16)
                    self.nl += 16
                    self.stgt = self.nl
                if k == 5:
                    ldp(2)
                if k == 6:
                    ldp(3)
                if k <= 4:
                    t = k - 1
                    ctx.inc("ppst", sc.dma_start(
                        out=dr["w0t_loc"][128 * t:128 * (t + 1), :],
                        in_=self.wout[:, :]), 16)
                else:
                    jj = k - 5
                    dst = dr["w1t_loc"].rearrange("(g j) n -> j g n", j=4)[jj]
                    ctx.inc("ppst", sc.dma_start(out=dst, in_=self.wout[:, :]), 16)
        block.scalar(_sc)

        nf4 = [(float(j), float(c)) for j, c in enumerate(NF4) if c != 0.0]

        def _v(v):
            for t in range(4):
                ctx.wait(v, "ppl", self.qtgt[t])
                b = t % 2
                v.tensor_copy(self.qf[:, :], self.qwb[b][:, :])
                v.drain()
                first = True
                for j, c in nf4:
                    dstp = self.acc if first else self.tmp
                    v.tensor_scalar(dstp[:, :], self.qf[:, :], j, c,
                                    ALU.is_equal, ALU.mult)
                    v.drain()
                    if not first:
                        v.tensor_tensor(self.acc[:, :], self.acc[:, :],
                                        self.tmp[:, :], ALU.add)
                        v.drain()
                    first = False
                ctx.wait(v, "ppl", self.atgt[t])
                if t >= 1:
                    ctx.wait(v, "ppst", 16 * t)
                i = v.tensor_tensor(self.wout[:, :], self.acc[:, :],
                                    self.amb[:, :], ALU.mult)
                v.drain()
                ctx.inc("ppv", i, 1)
            for jj, (d, shift) in enumerate(specs):
                ctx.wait(v, "ppl", self.ptgt[jj])
                if jj == 0:
                    ctx.wait(v, "ppl", self.stgt)
                b = jj % 2
                if shift:
                    v.tensor_scalar(self.qwb[b][:, :], self.qwb[b][:, :], 6, 63,
                                    ALU.logical_shift_right, ALU.bitwise_and)
                else:
                    v.tensor_single_scalar(self.qwb[b][:, :], self.qwb[b][:, :],
                                           63, ALU.bitwise_and)
                v.drain()
                v.tensor_copy(self.qf[:, :], self.qwb[b][:, :])
                v.drain()
                v.tensor_tensor(self.qf[:, :], self.qf[:, :], self.amb[:, :],
                                ALU.subtract)
                v.drain()
                ctx.wait(v, "ppst", 16 * (4 + jj))
                i = v.tensor_tensor(self.wout[:, :], self.qf[:, :],
                                    self.tmp[:, :], ALU.mult)
                v.drain()
                ctx.inc("ppv", i, 1)
        block.vector(_v)

        def _g(g):
            ctx.wait(g, self.stsem, 64)
            ctx.inc("ccxa", g.collective_compute(
                "AllGather", ALU.bypass, replica_groups=self.rg,
                ins=[dr["xn_loc0"][:, :]], outs=[dr["xn_g0"][:, :]]), 1)
            ctx.wait(g, self.stsem, 128)
            ctx.inc("ccxa", g.collective_compute(
                "AllGather", ALU.bypass, replica_groups=self.rg,
                ins=[dr["xn_loc1"][:, :]], outs=[dr["xn_g1"][:, :]]), 1)
            ctx.wait(g, "ppst", 16 * 8)
            ctx.inc("prep_done", g.nop(), 1)
        block.gpsimd(_g)

    def emit_ccw(self, block):
        ctx, dr = self.ctx, self.dr

        def _g(g):
            ctx.wait(g, "ppst", 16 * 8)
            ctx.inc("ccw", g.collective_compute(
                "AllGather", ALU.bypass, replica_groups=self.rg,
                ins=[dr["w0t_loc"][:, :]], outs=[dr["w0t"][:, :]]), 1)
            ctx.inc("ccw", g.collective_compute(
                "AllGather", ALU.bypass, replica_groups=self.rg,
                ins=[dr["w1t_loc"][:, :]], outs=[dr["w1t"][:, :]]), 1)
        block.gpsimd(_g)


def emit_prep(ctx, block, nc, es, dr, tag=""):
    qw = es.enter_context(nc.sbuf_tensor(f"pp_qw{tag}", [128, 4096], I32))
    qf = es.enter_context(nc.sbuf_tensor(f"pp_qf{tag}", [128, 4096], F32))
    acc = es.enter_context(nc.sbuf_tensor(f"pp_acc{tag}", [128, 4096], F32))
    tmp = es.enter_context(nc.sbuf_tensor(f"pp_tmp{tag}", [128, 4096], F32))
    amb = es.enter_context(nc.sbuf_tensor(f"pp_amb{tag}", [128, 4096], F32))
    wout = es.enter_context(nc.sbuf_tensor(f"pp_w{tag}", [128, 4096], BF16))
    q3 = [es.enter_context(nc.sbuf_tensor(f"pp_q3_{d}{tag}", [128, 4096], I32)) for d in range(3)]
    zpb = es.enter_context(nc.sbuf_tensor(f"pp_zpb{tag}", [128, 4096], F32))
    scb = es.enter_context(nc.sbuf_tensor(f"pp_scb{tag}", [128, 4096], F32))
    u = es.enter_context(nc.sbuf_tensor(f"pp_u{tag}", [128, 4096], I32))
    uf = es.enter_context(nc.sbuf_tensor(f"pp_uf{tag}", [128, 4096], F32))
    ppl, ppv, ppst = f"ppl{tag}", f"ppv{tag}", f"ppst{tag}"

    def _sync(s):
        for t in range(4):
            if t > 0:
                ctx.wait(s, ppv, t)
            ctx.inc(ppl, s.dma_start(out=qw[:, :],
                     in_=dr["qw0"][128 * t:128 * t + 128, :]), 16)
            for half in range(2):
                ctx.inc(ppl, s.dma_start(
                    out=amb[64 * half:64 * half + 64, :],
                    in_=dr["am0"][2 * t + half:2 * t + half + 1, :].broadcast_to([64, 4096])), 16)
            ctx.wait(s, ppv, t + 1)
            ctx.inc(ppst, s.dma_start(out=dr["w0t_loc"][128 * t:128 * t + 128, :],
                                      in_=wout[:, :]), 16)
        for d in range(3):
            ctx.inc(ppl, s.dma_start(out=q3[d][:, :], in_=dr["qw1"][d, :, :]), 16)
        ctx.inc(ppl, s.dma_start(out=zpb[:, :], in_=dr["zp1"].broadcast_to([128, 4096])), 16)
        ctx.inc(ppl, s.dma_start(out=scb[:, :], in_=dr["sc1"].broadcast_to([128, 4096])), 16)
        for j in range(4):
            ctx.wait(s, ppv, 5 + j)
            dst = dr["w1t_loc"].rearrange("(g j) n -> j g n", j=4)[j]
            ctx.inc(ppst, s.dma_start(out=dst, in_=wout[:, :]), 16)
        ctx.wait(s, ppst, 16 * 8)
        ctx.inc(f"prep_done{tag}", s.nop(), 1)
    block.sync(_sync)

    def _vec(v):
        for t in range(4):
            ctx.wait(v, ppl, 48 * (t + 1))
            if t > 0:
                ctx.wait(v, ppst, 16 * t)
            v.tensor_copy(qf[:, :], qw[:, :])
            v.drain()
            first = True
            for j, c in enumerate(NF4):
                if c == 0.0:
                    continue
                dstp = acc if first else tmp
                v.tensor_scalar(dstp[:, :], qf[:, :], float(j), float(c),
                                ALU.is_equal, ALU.mult)
                v.drain()
                if not first:
                    v.tensor_tensor(acc[:, :], acc[:, :], tmp[:, :], ALU.add)
                    v.drain()
                first = False
            i = v.tensor_tensor(wout[:, :], acc[:, :], amb[:, :], ALU.mult)
            ctx.inc(ppv, i, 1)
            v.drain()
        ctx.wait(v, ppl, 192 + 80)
        specs = [(0, False), (0, True), (1, False), (2, False)]
        for j, (d, shift) in enumerate(specs):
            ctx.wait(v, ppst, 16 * (4 + j))
            if shift:
                v.tensor_scalar(u[:, :], q3[d][:, :], 6, 63,
                                ALU.logical_shift_right, ALU.bitwise_and)
            else:
                v.tensor_single_scalar(u[:, :], q3[d][:, :], 63, ALU.bitwise_and)
            v.drain()
            v.tensor_copy(uf[:, :], u[:, :])
            v.drain()
            v.tensor_tensor(uf[:, :], uf[:, :], zpb[:, :], ALU.subtract)
            v.drain()
            i = v.tensor_tensor(wout[:, :], uf[:, :], scb[:, :], ALU.mult)
            ctx.inc(ppv, i, 1)
            v.drain()
    block.vector(_vec)


def emit_p1(ctx, block, nc, es, bufs, xsrc, xn_locs, tag=""):
    NCH, CT = 8, 128
    xc = [es.enter_context(nc.sbuf_tensor(f"p1_x{i}{tag}", [128, 32 * CT], F32))
          for i in range(2)]
    sq = es.enter_context(nc.sbuf_tensor(f"p1_sq{tag}", [128, 32 * CT], F32))
    xn = es.enter_context(nc.sbuf_tensor(f"p1_xn{tag}", [128, 32 * CT], BF16))
    srow = es.enter_context(nc.sbuf_tensor(f"p1_srow{tag}", [1, CT], F32))
    vrow = es.enter_context(nc.sbuf_tensor(f"p1_vrow{tag}", [1, CT], F32))
    lnrow = es.enter_context(nc.sbuf_tensor(f"p1_lnrow{tag}", [1, CT], F32))
    rrow = es.enter_context(nc.sbuf_tensor(f"p1_rrow{tag}", [1, CT], F32))
    mrow = es.enter_context(nc.sbuf_tensor(f"p1_mrow{tag}", [1, CT], F32))
    rbs = es.enter_context(nc.sbuf_tensor(f"p1_rbs{tag}", [128, CT], F32))
    mbs = es.enter_context(nc.sbuf_tensor(f"p1_mbs{tag}", [128, CT], F32))
    ps_s = es.enter_context(nc.psum_tensor(f"p1_pss{tag}", [1, CT], F32))
    ps_q = es.enter_context(nc.psum_tensor(f"p1_psq{tag}", [1, CT], F32))
    ps_rb = es.enter_context(nc.psum_tensor(f"p1_psrb{tag}", [128, CT], F32))
    ps_mb = es.enter_context(nc.psum_tensor(f"p1_psmb{tag}", [128, CT], F32))
    ld, sqs, mms, var, rst, mr, bc, rbc, ap, st = [f"p1{n}{tag}" for n in
        ["ld", "sq", "mm", "var", "rst", "mr", "bc", "rbc", "ap", "st"]]
    stname = "p1st" + tag
    xcv = [b.rearrange("p (t c) -> p t c", t=32) for b in xc]
    sqv = sq.rearrange("p (t c) -> p t c", t=32)
    xnv = xn.rearrange("p (t c) -> p t c", t=32)

    def bcast(b):
        return b.rearrange("p (o c) -> p o c", o=1).broadcast_to([128, 32, CT])

    def _sync(s):
        for c in range(NCH):
            if c >= 2:
                ctx.wait(s, ap, c - 1)
            src = xsrc[:, CT * c:CT * (c + 1)].rearrange("(t p) c -> p t c", p=128)
            ctx.inc(ld, s.dma_start(out=xcv[c % 2], in_=src), 16)
            if c >= 1:
                ctx.wait(s, ap, c)
                h, sub = divmod(c - 1, NCH // 2)
                dst = xn_locs[h][:, CT * sub:CT * (sub + 1)] \
                    .rearrange("(t p) c -> p t c", p=128)
                ctx.inc(stname, s.dma_start(out=dst, in_=xnv), 16)
        ctx.wait(s, ap, NCH)
        h, sub = divmod(NCH - 1, NCH // 2)
        dst = xn_locs[h][:, CT * sub:CT * (sub + 1)].rearrange("(t p) c -> p t c", p=128)
        ctx.inc(stname, s.dma_start(out=dst, in_=xnv), 16)
        ctx.wait(s, stname, 16 * NCH)
        ctx.inc(f"p1_done{tag}", s.nop(), 1)
    block.sync(_sync)

    def _scalar(sc):
        for c in range(NCH):
            ctx.wait(sc, ld, 16 * (c + 1))
            if c >= 1:
                ctx.wait(sc, mms, c)
            i = sc.activation(sq[:, :], xc[c % 2][:, :], AF.Square)
            ctx.inc(sqs, i, 1)
            ctx.wait(sc, var, c + 1)
            if c >= 1:
                ctx.wait(sc, bc, c)
            sc.activation(lnrow[:, :], vrow[:, :], AF.Ln, bias=f32(LN_EPS))
            sc.drain()
            i = sc.activation(rrow[:, :], lnrow[:, :], AF.Exp, scale=f32(-0.5))
            sc.drain()
            ctx.inc(rst, i, 1)
    block.scalar(_scalar)

    def _tensor(t):
        ctx.wait(t, "init", 1)
        for c in range(NCH):
            ctx.wait(t, ld, 16 * (c + 1))
            ctx.wait(t, sqs, c + 1)
            if c >= 1:
                ctx.wait(t, var, c)
            for f in range(32):
                t.matmul(ps_s[:, :], bufs["ones_col_f32"][:, :], xcv[c % 2][:, f, :],
                         start=(f == 0), stop=(f == 31))
            for f in range(32):
                i = t.matmul(ps_q[:, :], bufs["ones_col_f32"][:, :], sqv[:, f, :],
                             start=(f == 0), stop=(f == 31))
            ctx.inc(mms, i, 1)
            ctx.wait(t, mr, c + 1)
            if c >= 1:
                ctx.wait(t, rbc, c)
            t.matmul(ps_rb[:, :], bufs["ones_row_f32"][:, :], rrow[:, :],
                     start=True, stop=True)
            i = t.matmul(ps_mb[:, :], bufs["ones_row_f32"][:, :], mrow[:, :],
                         start=True, stop=True)
            ctx.inc(bc, i, 1)
    block.tensor(_tensor)

    def _vec(v):
        for c in range(NCH):
            ctx.wait(v, mms, c + 1)
            v.tensor_single_scalar(srow[:, :], ps_s[:, :], 1.0 / D, ALU.mult)
            v.drain()
            v.tensor_single_scalar(vrow[:, :], ps_q[:, :], 1.0 / D, ALU.mult)
            v.drain()
            v.tensor_tensor(mrow[:, :], srow[:, :], srow[:, :], ALU.mult)
            v.drain()
            i = v.tensor_tensor(vrow[:, :], vrow[:, :], mrow[:, :], ALU.subtract)
            v.drain()
            ctx.inc(var, i, 1)
            ctx.wait(v, rst, c + 1)
            i = v.tensor_tensor(mrow[:, :], srow[:, :], rrow[:, :], ALU.mult)
            v.drain()
            ctx.inc(mr, i, 1)
            ctx.wait(v, bc, c + 1)
            v.tensor_copy(rbs[:, :], ps_rb[:, :])
            v.drain()
            i = v.tensor_copy(mbs[:, :], ps_mb[:, :])
            v.drain()
            ctx.inc(rbc, i, 1)
            if c >= 1:
                ctx.wait(v, stname, 16 * c)
            v.tensor_tensor(xcv[c % 2], xcv[c % 2], bcast(rbs), ALU.mult)
            v.drain()
            i = v.tensor_tensor(xnv, xcv[c % 2], bcast(mbs), ALU.subtract)
            v.drain()
            ctx.inc(ap, i, 1)
    block.vector(_vec)


def emit_p2_wload(ctx, block, nc, es, of_tiles, wqkv, bqkv, tag=""):
    nof = len(of_tiles)
    wsb = [es.enter_context(nc.sbuf_tensor(f"p2_w{i}{tag}", [128, 32 * of_tiles[i][1]], BF16))
           for i in range(nof)]
    bsb = es.enter_context(nc.sbuf_tensor(f"p2_b{tag}", [128, nof], F32))
    wl, bl = f"p2wl{tag}", f"p2bl{tag}"
    ctx.persistent.update({wl, bl})

    def _g1(g):
        for i, (r0, w) in enumerate(of_tiles):
            ctx.inc(wl, g.dma_start(out=wsb[i][:, :], in_=wqkv[:, 32 * r0:32 * (r0 + w)]), 16)
        for i, (r0, w) in enumerate(of_tiles):
            ctx.inc(bl, g.dma_start(out=bsb[0:w, i:i + 1], in_=bqkv[r0:r0 + w, :]), 16)
    block.gpsimd(_g1)
    return wsb, bsb


def emit_p2(ctx, block, nc, es, bufs, of_tiles, wqkv, bqkv, xn_gs, qkv_loc, tag="",
            ccx_name=None, prew=None):
    nof = len(of_tiles)
    if prew is None:
        wsb, bsb = emit_p2_wload(ctx, block, nc, es, of_tiles, wqkv, bqkv, tag)
    else:
        wsb, bsb = prew
    xnc = [es.enter_context(nc.sbuf_tensor(f"p2_xn{i}{tag}", [128, 32 * 512], BF16))
           for i in range(2)]
    stg = [es.enter_context(nc.sbuf_tensor(f"p2_stg{i}{tag}", [128, 512], BF16))
           for i in range(4)]
    pss = [es.enter_context(nc.psum_tensor(f"p2_ps{i}{tag}", [128, 512], F32))
           for i in range(8)]
    wl, bl, mm, ev = [f"p2{n}{tag}" for n in ["wl", "bl", "mm", "ev"]]
    xl = [f"p2xl{i}{tag}" for i in range(2)]
    st = [f"p2st{i}{tag}" for i in range(4)]

    chunks = [(r, 0) for r in range(8)] + [(r, 1) for r in range(8)]

    def _sync(s):
        for tp in range(8):
            if tp >= 1:
                ctx.wait(s, mm, tp * nof)
            for h in range(2):
                r, half = chunks[2 * tp + h]
                if ccx_name is not None and (2 * tp + h) in (0, 8):
                    ctx.wait(s, ccx_name, half + 1)
                src = xn_gs[half][r * D:(r + 1) * D, :] \
                    .rearrange("(t p) c -> p t c", p=128)
                ctx.inc(xl[h], s.dma_start(
                    out=xnc[h].rearrange("p (t c) -> p t c", t=32), in_=src), 16)
    block.sync(_sync)

    def _tensor(t):
        for tp in range(8):
            for h in range(2):
                ctx.wait(t, xl[h], 16 * (tp + 1))
            for i, (r0, w) in enumerate(of_tiles):
                k = tp * nof + i
                if tp == 0:
                    ctx.wait(t, wl, 16 * (i + 1))
                if k >= 4:
                    ctx.wait(t, ev, 2 * (k - 3))
                wv = wsb[i].rearrange("p (t o) -> p t o", t=32)
                for f in range(32):
                    for h in range(2):
                        ii = t.matmul(pss[(k % 4) * 2 + h][0:w, :], wv[:, f, :],
                                      xnc[h].rearrange("p (t c) -> p t c", t=32)[:, f, :],
                                      start=(f == 0), stop=(f == 31))
                ctx.inc(mm, ii, 1)
    block.tensor(_tensor)

    def _vec(v):
        ctx.wait(v, bl, 16 * nof)
        for k in range(8 * nof):
            i_of = k % nof
            w = of_tiles[i_of][1]
            ctx.wait(v, mm, k + 1)
            for h in range(2):
                e = 2 * k + h
                if e >= 4:
                    ctx.wait(v, st[e % 4], 16 * (e // 4))
                ii = v.tensor_scalar(stg[e % 4][0:w, :], pss[(k % 4) * 2 + h][0:w, :],
                                     bsb[0:w, i_of:i_of + 1], None, ALU.add)
                v.drain()
                ctx.inc(ev, ii, 1)
    block.vector(_vec)

    def _sc(sc):
        for k in range(8 * nof):
            tp, i_of = divmod(k, nof)
            r0, w = of_tiles[i_of]
            for h in range(2):
                e = 2 * k + h
                r, half = chunks[2 * tp + h]
                col = (2 * r + half) * 512
                ctx.wait(sc, ev, e + 1)
                ctx.inc(st[e % 4], sc.dma_start(
                    out=qkv_loc[r0:r0 + w, col:col + 512],
                    in_=stg[e % 4][0:w, :]), 16)
        for i4 in range(4):
            nst = (16 * nof + 3 - i4) // 4
            ctx.wait(sc, st[i4], 16 * nst)
    block.scalar(_sc)


def emit_p3(ctx, block, nc, es, bufs, layer, qkv_loc, ao_loc, tag=""):
    if layer == 0:
        dh, units = 64, [(0, 0), (1, 0)]
        qrow, krow, vrow = 0, 64, 128
        scale = 1.0
        ebias = -M0
        rec_bias = -math.log(math.sqrt(512.0))
        causal = False
    else:
        dh, units = 128, [(s, hl) for hl in range(4) for s in range(2)]
        qrow, krow, vrow = 0, 512, 1024
        scale = 1.0 / math.sqrt(128.0)
        ebias = -M1
        rec_bias = 0.0
        causal = True

    qsb = [es.enter_context(nc.sbuf_tensor(f"p3_q{i}{tag}", [128, 4096], BF16)) for i in range(2)]
    ksb = [es.enter_context(nc.sbuf_tensor(f"p3_k{i}{tag}", [128, 4096], BF16)) for i in range(2)]
    vsb = [es.enter_context(nc.sbuf_tensor(f"p3_v{i}{tag}", [128, 4096], BF16)) for i in range(2)]
    vt = es.enter_context(nc.sbuf_tensor(f"p3_vt{tag}", [128, 32 * dh], BF16))
    expS = es.enter_context(nc.sbuf_tensor(f"p3_e{tag}", [128, 32 * 512], BF16))
    lnb = [es.enter_context(nc.sbuf_tensor(f"p3_ln{i}{tag}", [128, 512], F32)) for i in range(2)]
    rec = [es.enter_context(nc.sbuf_tensor(f"p3_rec{i}{tag}", [128, 512], F32)) for i in range(2)]
    aos = [es.enter_context(nc.sbuf_tensor(f"p3_ao{i}{tag}", [128, 512], BF16)) for i in range(2)]
    esum = [es.enter_context(nc.sbuf_tensor(f"p3_es{i}{tag}", [128, 512], F32)) for i in range(2)]
    ps_s = [es.enter_context(nc.psum_tensor(f"p3_pss{i}{tag}", [128, 512], F32)) for i in range(4)]
    ps_o = es.enter_context(nc.psum_tensor(f"p3_pso{tag}", [128, 512], F32))
    ps_d = es.enter_context(nc.psum_tensor(f"p3_psd{tag}", [128, 512], F32))
    ps_t = [es.enter_context(nc.psum_tensor(f"p3_pst{i}{tag}", [128, 128], BF16)) for i in range(2)]
    vtm, vte, smm, sev, mk, av, rc, no, ds = [f"p3{n}{tag}" for n in
        ["vtm", "vte", "smm", "sev", "mk", "av", "rc", "no", "ds"]]
    qkvl = [f"p3qkv{i}{tag}" for i in range(2)]
    ast = [f"p3ast{i}{tag}" for i in range(2)]
    vtv = vt.rearrange("p (t d) -> p t d", t=32)
    ev_ = expS.rearrange("p (t c) -> p t c", t=32)
    evt_ = expS.rearrange("p (t c) -> p c t", t=32)
    NU = len(units)

    def kcm(qc):
        return min(32, 4 * (qc + 1)) if causal else 32

    KSUM = sum(kcm(q) for q in range(8))

    def SC(u, qc):
        return u * KSUM + sum(kcm(q) for q in range(qc))

    def QI(u, qc):
        return u * 8 + qc

    if causal:
        def _g(g):
            for u in range(NU):
                for qc in range(8):
                    for p in range(4):
                        kc = 4 * qc + p
                        ctx.wait(g, sev, SC(u, qc) + kc + 1)
                        i = g.affine_select(out=ev_[:, kc, :], in_=ev_[:, kc, :],
                                            compare_op=ALU.is_ge, fill=0.0,
                                            base=-128 * p,
                                            pattern=[[1, 512]], channel_multiplier=-1)
                        ctx.inc(mk, i, 1)
                    g.drain()
        block.gpsimd(_g)

    def _sync(s):
        def load(u):
            sq_, hl = units[u]
            c0 = sq_ * 4096
            ub = u % 2
            for row0, sb in ((qrow, qsb[ub]), (krow, ksb[ub]), (vrow, vsb[ub])):
                ctx.inc(qkvl[ub], s.dma_start(out=sb[0:dh, :],
                         in_=qkv_loc[row0 + hl * dh:row0 + (hl + 1) * dh, c0:c0 + 4096]), 16)
        load(0)
        if NU > 1:
            load(1)
        AOFloc = 64 if layer == 0 else 512
        for u, (sq_, hl) in enumerate(units):
            c0 = sq_ * 4096
            for qc in range(8):
                q = QI(u, qc)
                ctx.wait(s, no, q + 1)
                g0 = c0 + qc * 512
                blk = g0 // 1024
                r0 = blk * AOFloc + hl * dh
                ctx.inc(ast[q % 2], s.dma_start(
                    out=ao_loc[r0:r0 + dh, (g0 % 1024):(g0 % 1024) + 512],
                    in_=aos[q % 2][0:dh, :]), 16)
            if u + 2 < NU:
                ctx.wait(s, av, 8 * (u + 1))
                load(u + 2)
        ctx.wait(s, ast[0], 16 * 4 * NU)
        ctx.wait(s, ast[1], 16 * 4 * NU)
        ctx.inc(f"p3_done{tag}", s.nop(), 1)
    block.sync(_sync)

    def _tensor(t):
        ctx.wait(t, "init", 1)

        def emit_sprime(u, qc, ub):
            km = kcm(qc)
            for kc in range(km):
                n = SC(u, qc) + kc
                if n >= 4:
                    ctx.wait(t, sev, n - 3)
                i = t.matmul(ps_s[n % 4][:, :],
                             ksb[ub][0:dh, kc * 128:(kc + 1) * 128],
                             qsb[ub][0:dh, qc * 512:(qc + 1) * 512],
                             start=True, stop=True)
                ctx.inc(smm, i, 1)

        def emit_av(u, qc, ub):
            q = QI(u, qc)
            km = kcm(qc)
            ctx.wait(t, sev, SC(u, qc) + km)
            if causal:
                ctx.wait(t, mk, 4 * (q + 1))
            if qc == 0:
                ctx.wait(t, vte, 32 * (u + 1))
            ctx.wait(t, rc, q)
            ctx.wait(t, no, q)
            for kc in range(km):
                t.matmul(ps_o[0:dh, :], vtv[:, kc, :], ev_[:, kc, :],
                         start=(kc == 0), stop=(kc == km - 1))
            ctx.wait(t, ds, q + 1)
            i = t.matmul(ps_d[0:dh, :], bufs["ones_k_f32"][:, 0:dh], esum[q % 2][:, :],
                         start=True, stop=True)
            ctx.inc(av, i, 1)

        for u in range(NU):
            ub = u % 2
            ctx.wait(t, qkvl[ub], 48 * (u // 2 + 1))
            for kc in range(32):
                if u > 0 or kc >= 2:
                    ctx.wait(t, vte, 32 * u + kc - 1)
                i = t.transpose(ps_t[kc % 2][:, 0:dh], vsb[ub][0:dh, kc * 128:(kc + 1) * 128],
                                bufs["ident"][0:dh, 0:dh])
                ctx.inc(vtm, i, 1)
            emit_sprime(u, 0, ub)
            for qc in range(1, 8):
                emit_av(u, qc - 1, ub)
                emit_sprime(u, qc, ub)
            emit_av(u, 7, ub)
    block.tensor(_tensor)

    def _scalar(sc):
        for u in range(NU):
            for qc in range(8):
                q = QI(u, qc)
                km = kcm(qc)
                for kc in range(km):
                    n = SC(u, qc) + kc
                    ctx.wait(sc, smm, n + 1)
                    if q > 0 and kc == 0:
                        ctx.wait(sc, av, q)
                    i = sc.activation(ev_[:, kc, :], ps_s[n % 4][:, :], AF.Exp,
                                      scale=f32(scale), bias=f32(ebias))
                    ctx.inc(sev, i, 1)
                ctx.wait(sc, av, q + 1)
                if q >= 2:
                    ctx.wait(sc, no, q - 1)
                sc.activation(lnb[q % 2][0:dh, :], ps_d[0:dh, :], AF.Ln)
                sc.drain()
                i = sc.activation(rec[q % 2][0:dh, :], lnb[q % 2][0:dh, :], AF.Exp,
                                  scale=f32(-1.0), bias=f32(rec_bias))
                sc.drain()
                ctx.inc(rc, i, 1)
    block.scalar(_scalar)

    def _vec(v):
        ctx.wait(v, "init", 1)
        for u, (sq_, hl) in enumerate(units):
            ub = u % 2
            if u > 0:
                ctx.wait(v, av, 8 * u)
            for kc in range(32):
                ctx.wait(v, vtm, 32 * u + kc + 1)
                i = v.tensor_copy(vtv[:, kc, :], ps_t[kc % 2][:, 0:dh])
                v.drain()
                ctx.inc(vte, i, 1)
            for qc in range(8):
                q = QI(u, qc)
                km = kcm(qc)
                if q >= 2:
                    ctx.wait(v, av, q - 1)
                for kc in range(km):
                    if causal and kc == km - 4:
                        ctx.wait(v, mk, 4 * (q + 1))
                    ctx.wait(v, sev, SC(u, qc) + kc + 1)
                    if kc == 0:
                        v.tensor_copy(esum[q % 2][:, :], ev_[:, kc, :])
                    else:
                        v.tensor_tensor(esum[q % 2][:, :], esum[q % 2][:, :],
                                        ev_[:, kc, :], ALU.add)
                    i = v.drain()
                ctx.inc(ds, i, 1)
                ctx.wait(v, rc, q + 1)
                if q >= 2:
                    ctx.wait(v, ast[q % 2], 16 * (q // 2))
                i = v.tensor_tensor(aos[q % 2][0:dh, :], ps_o[0:dh, :],
                                    rec[q % 2][0:dh, :], ALU.mult)
                v.drain()
                ctx.inc(no, i, 1)
    block.vector(_vec)


def emit_p4(ctx, block, nc, es, bufs, AOF, ao_src, xn_locs, woT, wfT, brows,
            xsrc, xdst, tag="", ao_cols_dyn=False):
    na = AOF // 128
    nf = na + 32
    act = es.enter_context(nc.sbuf_tensor(f"p4_act{tag}", [128, nf * 1024], BF16))
    wsb = [es.enter_context(nc.sbuf_tensor(f"p4_w{i}{tag}", [128, nf * 128], BF16)) for i in range(2)]
    brsb = [es.enter_context(nc.sbuf_tensor(f"p4_br{i}{tag}", [1, 4096], BF16))
            for i in range(len(brows))]
    xt = [es.enter_context(nc.sbuf_tensor(f"p4_x{i}{tag}", [128, 1024], F32)) for i in range(2)]
    stg = [es.enter_context(nc.sbuf_tensor(f"p4_s{i}{tag}", [128, 1024], F32)) for i in range(2)]
    pss = [es.enter_context(nc.psum_tensor(f"p4_ps{i}{tag}", [128, 512], F32)) for i in range(4)]
    aol, xnl, brl, mm, ev = [f"p4{n}{tag}" for n in ["ao", "xn", "br", "mm", "ev"]]
    wl = [f"p4wl{i}{tag}" for i in range(2)]
    xld = [f"p4x{i}{tag}" for i in range(2)]
    st = [f"p4st{i}{tag}" for i in range(2)]
    nbr = len(brows)
    actv = act.rearrange("p (t c) -> p t c", t=nf)

    def _g(g):
        for i, br in enumerate(brows):
            ctx.inc(brl, g.dma_start(out=brsb[i][:, :], in_=br[:, :]), 16)
        for f in range(32):
            for hh in range(2):
                ctx.inc(xnl, g.dma_start(
                    out=actv[:, na + f, hh * 512:(hh + 1) * 512],
                    in_=xn_locs[hh][f * 128:(f + 1) * 128, :]), 16)
        for of in range(32):
            if of >= 2:
                ctx.wait(g, mm, 2 * (of - 1))
            wv = wsb[of % 2].rearrange("p (t o) -> p t o", t=nf)
            ctx.inc(wl[of % 2], g.dma_start(
                out=wv[:, 0:na, :],
                in_=woT[:, of * na * 128:(of + 1) * na * 128]
                    .rearrange("p (t o) -> p t o", t=na)), 16)
            for fbg in range(4):
                ctx.inc(wl[of % 2], g.dma_start(
                    out=wv[:, na + fbg * 8:na + (fbg + 1) * 8, :],
                    in_=wfT[fbg * 1024:(fbg + 1) * 1024, of * 128:(of + 1) * 128]
                        .rearrange("(t p) o -> p t o", p=128)), 16)
    block.gpsimd(_g)

    def _sync(s):
        if ao_cols_dyn:
            pid = ctx.pid if getattr(ctx, "pid", None) is not None else s.partition_id()
            srcap = ao_src[:, bass.ts(pid, 1024)].rearrange("(t p) c -> p t c", p=128)
        else:
            srcap = ao_src.rearrange("(t p) c -> p t c", p=128)
        ctx.inc(aol, s.dma_start(out=actv[:, 0:na, :], in_=srcap), 16)
        for of in range(34):
            if of >= 2:
                ctx.wait(s, ev, of - 1)
            if of < 32:
                ctx.inc(xld[of % 2], s.dma_start(out=xt[of % 2][:, :],
                                                 in_=xsrc[of * 128:(of + 1) * 128, :]), 16)
            if of >= 2:
                o2 = of - 2
                ctx.inc(st[o2 % 2], s.dma_start(out=xdst[o2 * 128:(o2 + 1) * 128, :],
                                                in_=stg[o2 % 2][:, :]), 16)
        ctx.wait(s, st[0], 16 * 16)
        ctx.wait(s, st[1], 16 * 16)
        ctx.inc(f"p4_done{tag}", s.nop(), 1)
    block.sync(_sync)

    def _tensor(t):
        ctx.wait(t, "init", 1)
        ctx.wait(t, aol, 16)
        ctx.wait(t, xnl, 16 * 64)
        ctx.wait(t, brl, 16 * nbr)
        for of in range(32):
            ctx.wait(t, wl[of % 2], 80 * (of // 2 + 1))
            if of >= 2:
                ctx.wait(t, ev, of - 1)
            wv = wsb[of % 2].rearrange("p (t o) -> p t o", t=nf)
            for f in range(nf):
                for tc in range(2):
                    t.matmul(pss[(of % 2) * 2 + tc][:, :], wv[:, f, :],
                             actv[:, f, tc * 512:(tc + 1) * 512],
                             start=(f == 0), stop=False)
            for tc in range(2):
                for i in range(nbr):
                    ii = t.matmul(pss[(of % 2) * 2 + tc][:, :],
                                  brsb[i][:, of * 128:(of + 1) * 128],
                                  bufs["ones_row_bf"][:, :],
                                  start=False, stop=(i == nbr - 1))
            ctx.inc(mm, ii, 2)
    block.tensor(_tensor)

    def _vec(v):
        for of in range(32):
            ctx.wait(v, mm, 2 * (of + 1))
            ctx.wait(v, xld[of % 2], 16 * (of // 2 + 1))
            if of >= 2:
                ctx.wait(v, st[of % 2], 16 * (of // 2))
            for tc in range(2):
                i = v.tensor_tensor(stg[of % 2][:, tc * 512:(tc + 1) * 512],
                                    pss[(of % 2) * 2 + tc][:, :],
                                    xt[of % 2][:, tc * 512:(tc + 1) * 512],
                                    ALU.add)
                v.drain()
            ctx.inc(ev, i, 1)
    block.vector(_vec)


def host_prep(inp):
    out = [dict() for _ in range(NCORES)]
    x = np.ascontiguousarray(np.asarray(inp["L0_attn_x"], np.float32).reshape(TOK, D))
    for c in range(NCORES):
        out[c]["x0"] = np.ascontiguousarray(x[TPC * c:TPC * (c + 1)].T)

    def _tile_w(wT, widths):
        parts = []
        r0 = 0
        for w in widths:
            parts.append(wT.reshape(32, 128, -1)[:, :, r0:r0 + w]
                         .transpose(1, 0, 2).reshape(128, 32 * w))
            r0 += w
        return np.ascontiguousarray(np.concatenate(parts, 1))

    def _tile_of(wT):
        na = wT.shape[0] // 128
        return np.ascontiguousarray(
            wT.reshape(na, 128, 32, 128).transpose(1, 2, 0, 3).reshape(128, 32 * na * 128))

    wqk = np.asarray(inp["L0_attn_Wqk_w"], np.float32)
    bqk = np.asarray(inp["L0_attn_Wqk_b"], np.float32)
    wv0 = np.asarray(inp["L0_attn_Wv_w"], np.float32)
    bv0 = np.asarray(inp["L0_attn_Wv_b"], np.float32)
    for c in range(NCORES):
        qr = [128 * c + 2 * e for e in range(64)]
        kr = [128 * c + 2 * e + 1 for e in range(64)]
        w = np.concatenate([wqk[qr], wqk[kr], wv0[64 * c:64 * c + 64]], 0)
        b = np.concatenate([bqk[qr], bqk[kr], bv0[64 * c:64 * c + 64]], 0)
        out[c]["wqkv0"] = np.ascontiguousarray(w.T)
        out[c]["wqkv0t"] = _tile_w(w.T, [128, 64])
        out[c]["bqkv0"] = np.ascontiguousarray(b[:, None])

    wq = np.asarray(inp["L1_attn_Wq"], np.float32)
    wk = np.asarray(inp["L1_attn_Wk"], np.float32)
    wv = np.asarray(inp["L1_attn_Wv"], np.float32)
    bq = np.asarray(inp["L1_attn_bq"], np.float32)
    bk = np.asarray(inp["L1_attn_bk"], np.float32)
    bv = np.asarray(inp["L1_attn_bv"], np.float32)
    for c in range(NCORES):
        sl = slice(512 * c, 512 * c + 512)
        w = np.concatenate([wq[sl], wk[sl], wv[sl]], 0)
        b = np.concatenate([bq[sl], bk[sl], bv[sl]], 0)
        out[c]["wqkv1"] = np.ascontiguousarray(w.T)
        out[c]["wqkv1t"] = _tile_w(w.T, [128] * 12)
        out[c]["bqkv1"] = np.ascontiguousarray(b[:, None])

    wo0T = np.ascontiguousarray(np.asarray(inp["L0_attn_Wo_w"], np.float32).T)
    bo0 = np.ascontiguousarray(np.asarray(inp["L0_attn_Wo_b"], np.float32)[None, :])
    wo1T = np.ascontiguousarray(np.asarray(inp["L1_attn_Wo"], np.float32).T)
    bo1 = np.ascontiguousarray(np.asarray(inp["L1_attn_bo"], np.float32)[None, :])
    bf1 = np.ascontiguousarray(np.asarray(inp["L1_ffn_bias"], np.float32)[None, :])
    wo0t = _tile_of(wo0T)
    wo1t = _tile_of(wo1T)
    for c in range(NCORES):
        out[c]["wo0T"] = wo0T
        out[c]["wo0t"] = wo0t
        out[c]["bo0"] = bo0
        out[c]["wo1T"] = wo1T
        out[c]["wo1t"] = wo1t
        out[c]["bo1"] = bo1
        out[c]["bf1"] = bf1

    qw0T = np.asarray(inp["L0_ffn_qweight"], np.int32).reshape(D, D).T
    am0T = np.asarray(inp["L0_ffn_absmax"], np.float32).reshape(D, 64).T
    qw1 = np.asarray(inp["L1_ffn_qweight"], np.int32)
    zp1 = np.ascontiguousarray(np.asarray(inp["L1_ffn_weight_zero_point"], np.float32).reshape(1, D))
    sc1 = np.ascontiguousarray(np.asarray(inp["L1_ffn_weight_scale"], np.float32).reshape(1, D))
    for c in range(NCORES):
        out[c]["qw0"] = np.ascontiguousarray(qw0T[512 * c:512 * c + 512])
        out[c]["am0"] = np.ascontiguousarray(am0T[8 * c:8 * c + 8])
        sl = np.ascontiguousarray(qw1[:, 384 * c:384 * c + 384].T)
        out[c]["qw1"] = np.ascontiguousarray(np.stack([sl[0::3], sl[1::3], sl[2::3]], 0))
        out[c]["zp1"] = zp1
        out[c]["sc1"] = sc1
    return out


def _mk_nc():
    return bass.Bass("TRN2", target_bir_lowering=False)


def build_prep():
    nc = _mk_nc()
    dr = dict(
        qw0=nc.dram_tensor("qw0", [512, 4096], I32, kind="ExternalInput").ap(),
        am0=nc.dram_tensor("am0", [8, 4096], F32, kind="ExternalInput").ap(),
        qw1=nc.dram_tensor("qw1", [3, 128, 4096], I32, kind="ExternalInput").ap(),
        zp1=nc.dram_tensor("zp1", [1, 4096], F32, kind="ExternalInput").ap(),
        sc1=nc.dram_tensor("sc1", [1, 4096], F32, kind="ExternalInput").ap(),
        w0t_loc=nc.dram_tensor("w0t_loc", [512, 4096], BF16, kind="ExternalOutput").ap(),
        w1t_loc=nc.dram_tensor("w1t_loc", [512, 4096], BF16, kind="ExternalOutput").ap(),
    )
    with contextlib.ExitStack() as es:
        ctx = Ctx(nc, es)
        with nc.Block() as block:
            emit_prep(ctx, block, nc, es, dr)
    return nc


def build_p1():
    nc = _mk_nc()
    xsrc = nc.dram_tensor("xsrc", [D, TPC], F32, kind="ExternalInput").ap()
    xn0 = nc.dram_tensor("xn0", [D, 512], BF16, kind="ExternalOutput").ap()
    xn1 = nc.dram_tensor("xn1", [D, 512], BF16, kind="ExternalOutput").ap()
    with contextlib.ExitStack() as es:
        ctx = Ctx(nc, es)
        bufs = alloc_init_bufs(nc, es, False)
        with nc.Block() as block:
            emit_init(ctx, block, bufs, False)
            emit_p1(ctx, block, nc, es, bufs, xsrc, (xn0, xn1))
    return nc


def build_p2(layer):
    nc = _mk_nc()
    OF = 192 if layer == 0 else 1536
    of_tiles = [(0, 128), (128, 64)] if layer == 0 else [(i * 128, 128) for i in range(12)]
    wqkv = nc.dram_tensor("wqkv", [128, 32 * OF], F32, kind="ExternalInput").ap()
    bqkv = nc.dram_tensor("bqkv", [OF, 1], F32, kind="ExternalInput").ap()
    xn_g0 = nc.dram_tensor("xn_g0", [NCORES * D, 512], BF16, kind="ExternalInput").ap()
    xn_g1 = nc.dram_tensor("xn_g1", [NCORES * D, 512], BF16, kind="ExternalInput").ap()
    qkv_loc = nc.dram_tensor("qkv_loc", [OF, TOK], BF16, kind="ExternalOutput").ap()
    with contextlib.ExitStack() as es:
        ctx = Ctx(nc, es)
        bufs = alloc_init_bufs(nc, es, False)
        with nc.Block() as block:
            emit_init(ctx, block, bufs, False)
            emit_p2(ctx, block, nc, es, bufs, of_tiles, wqkv, bqkv, (xn_g0, xn_g1), qkv_loc)
    return nc


def build_p3(layer):
    nc = _mk_nc()
    OF = 192 if layer == 0 else 1536
    AOF = 64 if layer == 0 else 512
    qkv_loc = nc.dram_tensor("qkv_loc", [OF, TOK], BF16, kind="ExternalInput").ap()
    ao_loc = nc.dram_tensor("ao_loc", [8 * AOF, TPC], BF16, kind="ExternalOutput").ap()
    with contextlib.ExitStack() as es:
        ctx = Ctx(nc, es)
        bufs = alloc_init_bufs(nc, es, layer == 1)
        with nc.Block() as block:
            emit_init(ctx, block, bufs, layer == 1)
            emit_p3(ctx, block, nc, es, bufs, layer, qkv_loc, ao_loc)
    return nc


def build_p4(layer):
    nc = _mk_nc()
    AOF = 512 if layer == 0 else 4096
    nbr = 1 if layer == 0 else 2
    na = AOF // 128
    ao = nc.dram_tensor("ao", [AOF, TPC], BF16, kind="ExternalInput").ap()
    xn0 = nc.dram_tensor("xn0", [D, 512], BF16, kind="ExternalInput").ap()
    xn1 = nc.dram_tensor("xn1", [D, 512], BF16, kind="ExternalInput").ap()
    woT = nc.dram_tensor("woT", [128, 32 * na * 128], F32, kind="ExternalInput").ap()
    wfT = nc.dram_tensor("wfT", [D, D], BF16, kind="ExternalInput").ap()
    brows = [nc.dram_tensor(f"br{i}", [1, D], F32, kind="ExternalInput").ap()
             for i in range(nbr)]
    xsrc = nc.dram_tensor("xsrc", [D, TPC], F32, kind="ExternalInput").ap()
    xdst = nc.dram_tensor("xdst", [D, TPC], F32, kind="ExternalOutput").ap()
    with contextlib.ExitStack() as es:
        ctx = Ctx(nc, es)
        bufs = alloc_init_bufs(nc, es, False)
        with nc.Block() as block:
            emit_init(ctx, block, bufs, False)
            emit_p4(ctx, block, nc, es, bufs, AOF, ao, (xn0, xn1), woT, wfT, brows,
                    xsrc, xdst, ao_cols_dyn=False)
    return nc


_NC_CACHE = {}


def _get(name, fn, *a):
    key = (name,) + a
    if key not in _NC_CACHE:
        _NC_CACHE[key] = fn(*a)
    return _NC_CACHE[key]


def _run(nc, in_maps):
    return run_bass_kernel_spmd(nc, in_maps, core_ids=list(range(NCORES))).results


def kernel_unfused(**inputs):
    pc = host_prep(inputs)

    nc = _get("prep", build_prep)
    res = _run(nc, [{k: pc[c][k] for k in ["qw0", "am0", "qw1", "zp1", "sc1"]}
                    for c in range(NCORES)])
    w0t = np.concatenate([res[c]["w0t_loc"] for c in range(NCORES)], 0)
    w1t = np.concatenate([res[c]["w1t_loc"] for c in range(NCORES)], 0)

    xcur = [pc[c]["x0"] for c in range(NCORES)]
    for layer in range(2):
        nc = _get("p1", build_p1)
        res = _run(nc, [{"xsrc": xcur[c]} for c in range(NCORES)])
        xn0 = [res[c]["xn0"] for c in range(NCORES)]
        xn1 = [res[c]["xn1"] for c in range(NCORES)]
        xn_g0 = np.concatenate(xn0, 0)
        xn_g1 = np.concatenate(xn1, 0)

        nc = _get("p2", build_p2, layer)
        res = _run(nc, [{"wqkv": pc[c][f"wqkv{layer}t"], "bqkv": pc[c][f"bqkv{layer}"],
                         "xn_g0": xn_g0, "xn_g1": xn_g1} for c in range(NCORES)])
        qkv = [res[c]["qkv_loc"] for c in range(NCORES)]

        nc = _get("p3", build_p3, layer)
        res = _run(nc, [{"qkv_loc": qkv[c]} for c in range(NCORES)])
        AOFloc = 64 if layer == 0 else 512
        ao_per_core = [np.concatenate([res[i]["ao_loc"][c * AOFloc:(c + 1) * AOFloc]
                                       for i in range(NCORES)], 0) for c in range(NCORES)]

        nc = _get("p4", build_p4, layer)
        wf = w0t if layer == 0 else w1t
        in_maps = []
        for c in range(NCORES):
            m = {"ao": ao_per_core[c],
                 "xn0": xn0[c], "xn1": xn1[c], "woT": pc[c][f"wo{layer}t"], "wfT": wf,
                 "xsrc": xcur[c]}
            if layer == 0:
                m["br0"] = pc[c]["bo0"]
            else:
                m["br0"] = pc[c]["bo1"]
                m["br1"] = pc[c]["bf1"]
            in_maps.append(m)
        res = _run(nc, in_maps)
        xcur = [res[c]["xdst"] for c in range(NCORES)]

    full = np.empty((TOK, D), np.float32)
    for c in range(NCORES):
        full[TPC * c:TPC * (c + 1)] = xcur[c].T
    return full.reshape(2, SEQ, D)


def build_fused():
    nc = _mk_nc()
    ei, eo = "ExternalInput", "ExternalOutput"
    dr = {}
    dr["x0"] = nc.dram_tensor("x0", [D, TPC], F32, kind=ei).ap()
    dr["wqkv0"] = nc.dram_tensor("wqkv0", [128, 32 * 192], F32, kind=ei).ap()
    dr["bqkv0"] = nc.dram_tensor("bqkv0", [192, 1], F32, kind=ei).ap()
    dr["wqkv1"] = nc.dram_tensor("wqkv1", [128, 32 * 1536], F32, kind=ei).ap()
    dr["bqkv1"] = nc.dram_tensor("bqkv1", [1536, 1], F32, kind=ei).ap()
    dr["wo0t"] = nc.dram_tensor("wo0t", [128, 32 * 4 * 128], F32, kind=ei).ap()
    dr["wo1t"] = nc.dram_tensor("wo1t", [128, 32 * 32 * 128], F32, kind=ei).ap()
    dr["bo0"] = nc.dram_tensor("bo0", [1, D], F32, kind=ei).ap()
    dr["bo1"] = nc.dram_tensor("bo1", [1, D], F32, kind=ei).ap()
    dr["bf1"] = nc.dram_tensor("bf1", [1, D], F32, kind=ei).ap()
    dr["qw0"] = nc.dram_tensor("qw0", [512, 4096], I32, kind=ei).ap()
    dr["am0"] = nc.dram_tensor("am0", [8, 4096], F32, kind=ei).ap()
    dr["qw1"] = nc.dram_tensor("qw1", [3, 128, 4096], I32, kind=ei).ap()
    dr["zp1"] = nc.dram_tensor("zp1", [1, 4096], F32, kind=ei).ap()
    dr["sc1"] = nc.dram_tensor("sc1", [1, 4096], F32, kind=ei).ap()
    dr["xout"] = nc.dram_tensor("xout", [D, TPC], F32, kind=eo).ap()
    dr["w0t_loc"] = nc.dram_tensor("w0t_loc", [512, 4096], BF16).ap()
    dr["w1t_loc"] = nc.dram_tensor("w1t_loc", [512, 4096], BF16).ap()
    dr["w0t"] = nc.dram_tensor("w0t", [D, D], BF16).ap()
    dr["w1t"] = nc.dram_tensor("w1t", [D, D], BF16).ap()
    dr["xn_loc0"] = nc.dram_tensor("xn_loc0", [D, 512], BF16).ap()
    dr["xn_loc1"] = nc.dram_tensor("xn_loc1", [D, 512], BF16).ap()
    dr["xn_g0"] = nc.dram_tensor("xn_g0", [NCORES * D, 512], BF16).ap()
    dr["xn_g1"] = nc.dram_tensor("xn_g1", [NCORES * D, 512], BF16).ap()
    dr["qkv0"] = nc.dram_tensor("qkv0", [192, TOK], BF16).ap()
    dr["qkv1"] = nc.dram_tensor("qkv1", [1536, TOK], BF16).ap()
    dr["ao0_loc"] = nc.dram_tensor("ao0_loc", [512, TPC], BF16).ap()
    dr["ao0_g"] = nc.dram_tensor("ao0_g", [512, TPC], BF16).ap()
    dr["ao1_loc"] = nc.dram_tensor("ao1_loc", [4096, TPC], BF16).ap()
    dr["ao1_g"] = nc.dram_tensor("ao1_g", [4096, TPC], BF16).ap()
    dr["xmid"] = nc.dram_tensor("xmid", [D, TPC], F32).ap()

    of0 = [(0, 128), (128, 64)]
    of1 = [(i * 128, 128) for i in range(12)]
    rg = [list(range(NCORES))]

    with contextlib.ExitStack() as es:
        ctx = Ctx(nc, es)
        bufs = alloc_init_bufs(nc, es, False)
        with nc.Block() as block:
            emit_init(ctx, block, bufs, False, dyn_src=dr["x0"])
            es2a = contextlib.ExitStack()
            prew0 = emit_p2_wload(ctx, block, nc, es2a, of0,
                                  dr["wqkv0"], dr["bqkv0"], tag="2a")
            esp = contextlib.ExitStack()
            prep = PrepV(ctx, nc, esp, dr, rg, "p1st1a")
            prep.emit_loads(block)
            PREPX = ("ccw", "ppl", "ppst", "ppv", "prep_done")

            for layer, tg in ((0, "a"), (1, "b")):
                xsrc = dr["x0"] if layer == 0 else dr["xmid"]
                xdst = dr["xmid"] if layer == 0 else dr["xout"]
                qkv = dr[f"qkv{layer}"]
                aoloc = dr[f"ao{layer}_loc"]
                aog = dr[f"ao{layer}_g"]
                with contextlib.ExitStack() as es1:
                    emit_p1(ctx, block, nc, es1, bufs, xsrc,
                            (dr["xn_loc0"], dr["xn_loc1"]), tag="1" + tg)
                    if layer == 0:
                        prep.emit_rest(block)
                    else:
                        def _gx(g):
                            for hh in range(2):
                                ctx.wait(g, f"p1st1{tg}", 64 * (hh + 1))
                                ctx.inc("ccx" + tg, g.collective_compute(
                                    "AllGather", ALU.bypass, replica_groups=rg,
                                    ins=[dr[f"xn_loc{hh}"][:, :]],
                                    outs=[dr[f"xn_g{hh}"][:, :]]), 1)
                        block.gpsimd(_gx)
                    emit_barrier(ctx, block, exclude=PREPX + ("ccx" + tg,))
                with contextlib.ExitStack() as es2:
                    emit_p2(ctx, block, nc, es2, bufs,
                            of0 if layer == 0 else of1,
                            dr[f"wqkv{layer}"], dr[f"bqkv{layer}"],
                            (dr["xn_g0"], dr["xn_g1"]), qkv, tag="2" + tg,
                            ccx_name="ccx" + tg,
                            prew=prew0 if layer == 0 else None)
                    if layer == 0:
                        emit_barrier(ctx, block, exclude=("ccw",))
                    else:
                        emit_barrier(ctx, block, exclude=PREPX)
                if layer == 0:
                    prep.emit_ccw(block)
                    esp.close()
                    es2a.close()
                with contextlib.ExitStack() as es3:
                    emit_p3(ctx, block, nc, es3, bufs, layer, qkv, aoloc, tag="3" + tg)
                    emit_barrier(ctx, block, exclude=PREPX)
                def _ga(g):
                    ctx.inc("cca" + tg, g.collective_compute(
                        "AllToAll", ALU.bypass, replica_groups=rg,
                        ins=[aoloc[:, :]], outs=[aog[:, :]]), 1)
                block.gpsimd(_ga)
                emit_barrier(ctx, block)
                with contextlib.ExitStack() as es4:
                    brows = [dr["bo0"]] if layer == 0 else [dr["bo1"], dr["bf1"]]
                    emit_p4(ctx, block, nc, es4, bufs, 512 if layer == 0 else 4096,
                            aog, (dr["xn_loc0"], dr["xn_loc1"]), dr[f"wo{layer}t"],
                            dr[f"w{layer}t"], brows, xsrc, xdst,
                            tag="4" + tg, ao_cols_dyn=False)
                    if layer == 0:
                        emit_barrier(ctx, block)
    return nc


def kernel(**inputs):
    pc = host_prep(inputs)
    nc = _get("fused", build_fused)
    keys = ["x0", "bqkv0", "bqkv1", "bo0", "bo1", "bf1", "qw0", "am0",
            "qw1", "zp1", "sc1"]
    in_maps = []
    for c in range(NCORES):
        m = {k: pc[c][k] for k in keys}
        m["wqkv0"] = pc[c]["wqkv0t"]
        m["wqkv1"] = pc[c]["wqkv1t"]
        m["wo0t"] = pc[c]["wo0t"]
        m["wo1t"] = pc[c]["wo1t"]
        in_maps.append(m)
    res = _run(nc, in_maps)
    full = np.empty((TOK, D), np.float32)
    for c in range(NCORES):
        full[TPC * c:TPC * (c + 1)] = res[c]["xout"].T
    return full.reshape(2, SEQ, D)
